# version 20
# baseline (speedup 1.0000x reference)
import math
from contextlib import ExitStack

import numpy as np
import ml_dtypes

import concourse.bass as bass
import concourse.mybir as mybir
from concourse.bass_utils import run_bass_kernel_spmd

F32 = mybir.dt.float32
BF16 = mybir.dt.bfloat16
AF = mybir.ActivationFunctionType
ALU = mybir.AluOpType
AX = mybir.AxisListType
U32 = mybir.dt.uint32
I32 = mybir.dt.int32
IOA = bass.IndirectOffsetOnAxis

B, T, C = 2, 4096, 2048
NB = T // 128
NCH = T // 512
TOPK = 4
EH = 1024
NEXP = 20
SCALE = 1.0 / math.sqrt(128.0)
EPS = 1e-5
NEG = -1e30
KiB = 1024

ENGINES = ("sync", "gpsimd", "tensor", "vector", "scalar")
DBG = False


def own_block(r, c):
    return 4 * c + r if c < 4 else 4 * c + 3 - r


def core_perm(r):
    out = []
    for c in range(NCH):
        blk = own_block(r, c)
        out.extend([blk] + [b for b in range(4 * c, 4 * c + 4) if b != blk])
    return out


def build_program():
    nc = bass.Bass("TRN2", target_bir_lowering=False, debug=False,
                   detect_race_conditions=False)

    d_x = nc.dram_tensor("xb", [T, C], F32, kind="ExternalInput").ap()
    d_wq = nc.dram_tensor("wq", [C, C], F32, kind="ExternalInput").ap()
    d_wk = nc.dram_tensor("wk", [C, 512], F32, kind="ExternalInput").ap()
    d_wv = nc.dram_tensor("wv", [C, 512], F32, kind="ExternalInput").ap()
    d_wp = nc.dram_tensor("wp", [C, C], F32, kind="ExternalInput").ap()
    d_wr = nc.dram_tensor("wr", [C, 16], F32, kind="ExternalInput").ap()
    d_we1 = nc.dram_tensor("we1", [16, C, EH], F32, kind="ExternalInput").ap()
    d_we3 = nc.dram_tensor("we3", [16, C, EH], F32, kind="ExternalInput").ap()
    d_we2 = nc.dram_tensor("we2", [16, EH, C], F32, kind="ExternalInput").ap()
    d_ws1 = nc.dram_tensor("ws1", [C, 4096], F32, kind="ExternalInput").ap()
    d_ws3 = nc.dram_tensor("ws3", [C, 4096], F32, kind="ExternalInput").ap()
    d_ws2 = nc.dram_tensor("ws2", [4096, C], F32, kind="ExternalInput").ap()
    d_idb = nc.dram_tensor("idb", [128, 128], BF16, kind="ExternalInput").ap()
    d_ropeC = nc.dram_tensor("ropeC", [2, 128, T], F32, kind="ExternalInput").ap()
    d_ropeS = nc.dram_tensor("ropeS", [2, 128, T], F32, kind="ExternalInput").ap()
    d_maskA = nc.dram_tensor("maskA", [128, NCH, 512], BF16, kind="ExternalInput").ap()
    d_maskB = nc.dram_tensor("maskB", [128, NCH, 4, 128], BF16, kind="ExternalInput").ap()
    d_out = nc.dram_tensor("out_own", [1024, C], F32, kind="ExternalOutput").ap()
    d_ln2 = nc.dram_tensor("ln2scr", [1025, C], BF16, kind="Internal").ap()
    d_cwsc = nc.dram_tensor("cwscr", [1025, 64], F32, kind="Internal").ap()
    d_bidx = nc.dram_tensor("bidxscr", [8192, 1], U32, kind="Internal").ap()
    d_accd = nc.dram_tensor("accscr", [1025, C], F32, kind="Internal").ap()
    d_tid = nc.dram_tensor("tidc", [128, 8], U32, kind="ExternalInput").ap()
    d_erow = nc.dram_tensor("erowc", [128, 16], F32, kind="ExternalInput").ap()
    d_ltu = nc.dram_tensor("ltuc", [128, 128], BF16, kind="ExternalInput").ap()
    ddbg = {}
    if DBG:
        for nm, shp in (("xfm_own", [128, 16, 1024]), ("k_sb", [128, 4, T]),
                        ("v_sb", [128, NB, 4, 129]), ("q_sb", [128, 16, 1024]),
                        ("yfm", [128, NCH, 16, 128]),
                        ("xfm2", [128, 16, 1024])):
            dt = BF16
            ddbg[nm] = nc.dram_tensor("dbg_" + nm, shp, dt, kind="ExternalOutput").ap()
        for nm in ("x2f", "acc"):
            ddbg[nm] = nc.dram_tensor("dbg_" + nm, [128, NCH, C], F32, kind="ExternalOutput").ap()
        ddbg["cw"] = nc.dram_tensor("dbg_cw", [128, NCH, 16], F32, kind="ExternalOutput").ap()
    d_x2 = nc.dram_tensor("x2scr", [1024, C], F32, kind="Internal").ap()

    prog = {e: [] for e in ENGINES}
    S = {}
    last_wait = {e: {} for e in ENGINES}
    master = ExitStack()
    sems = {}
    for name in ("mm", "ve", "sc", "scr", "bar", "fl", "sid", "srope", "smk",
                 "sbx", "sgt", "sgc", "ssc", "sac",
                 "sx0", "sx1", "sxo0", "sxo1", "sxr0", "sxr1", "sout0", "sout1",
                 "sws", "swq0", "swq1", "swp0", "swp1", "sw0", "sw1", "sw2", "sw3"):
        sems[name] = master.enter_context(nc.semaphore(name))

    def emit(eng, fn):
        prog[eng].append(fn)

    def inc(sem, amt=1):
        S[sem] = S.get(sem, 0) + amt
        return S[sem]

    def cur(sem):
        return S.get(sem, 0)

    def wait(eng, sem, val):
        if not val or val <= 0:
            return
        assert val <= S.get(sem, 0), ("wait-ahead", eng, sem, val, S.get(sem, 0))
        if last_wait[eng].get(sem, 0) >= val:
            return
        last_wait[eng][sem] = val
        h = sems[sem]
        emit(eng, lambda e, h=h, v=val: e.wait_ge(h, v))

    def barrier():
        h = sems["bar"]
        for eng in ENGINES:
            inc("bar")
            emit(eng, lambda e, h=h: e.sem_inc(h, 1))
        t = cur("bar")
        for eng in ENGINES:
            wait(eng, "bar", t)

    assert nc.sbuf_top >= 229344
    k_sb = nc.alloc_sbuf_tensor_at("k_sb", [128, 4, T], BF16, offset=196576)
    v_sb = nc.alloc_sbuf_tensor_at("v_sb", [128, NB, 4, 129], BF16, offset=163552)
    q_sb = nc.alloc_sbuf_tensor_at("q_sb", [128, 16, 1024], BF16, offset=130784)
    yfm = nc.alloc_sbuf_tensor_at("yfm", [128, NCH, 16, 128], BF16, offset=98016)
    x2f = nc.alloc_sbuf_tensor_at("x2f", [128, NCH, C], F32, offset=130784)
    xfm2 = nc.alloc_sbuf_tensor_at("xfm2", [128, 16, 1024], BF16, offset=196576)
    acc = nc.alloc_sbuf_tensor_at("acc", [128, NCH, C], F32, offset=130784)

    identb = nc.alloc_sbuf_tensor("identb", [128, 128], BF16)
    epsb = nc.alloc_sbuf_tensor("epsb", [128, 1], F32)
    onesb = nc.alloc_sbuf_tensor("onesb", [65, 128], BF16)
    v_id = inc("sid", 16)
    emit("sync", lambda e: e.dma_start(out=identb[:], in_=d_idb).then_inc(sems["sid"], 16))
    emit("vector", lambda e: e.memset(onesb[:], 1.0))
    v_eps = inc("ve")
    emit("vector", lambda e: e.memset(epsb[:], float(EPS)).then_inc(sems["ve"], 1))
    wait("scalar", "ve", v_eps)
    emit("vector", lambda e: e.memset(v_sb[:, :, :, 128:129], 1.0))

    ctxAB = ExitStack()
    ropeC = ctxAB.enter_context(nc.sbuf_tensor("ropeCt", [128, T], BF16))
    ropeS = ctxAB.enter_context(nc.sbuf_tensor("ropeSt", [128, T], BF16))
    roptmp = ctxAB.enter_context(nc.sbuf_tensor("roptmp", [128, 512], F32))
    roptmp2 = ctxAB.enter_context(nc.sbuf_tensor("roptmp2", [128, 512], F32))
    xfm_own = ctxAB.enter_context(nc.sbuf_tensor("xfm_own", [128, 16, 1024], BF16))

    ctx1a = ExitStack()
    ropestg = ctx1a.enter_context(nc.sbuf_tensor("ropestg", [128, 2, T], F32))
    v_rope_add = 0
    for (tbl, dst) in ((d_ropeC, ropeC), (d_ropeS, ropeS)):
        wait("sync", "ve", v_rope_add)
        v_l = inc("srope", 16)
        emit("sync", lambda e, t=tbl: e.dma_start(
            out=ropestg[:], in_=t.rearrange("h p t -> p h t")).then_inc(sems["srope"], 16))
        wait("vector", "srope", v_l)
        v_rope_add = inc("ve")
        emit("vector", lambda e, d=dst: e.tensor_add(
            d[:], ropestg[:, 0, :], ropestg[:, 1, :]).then_inc(sems["ve"], 1))
    ctx1a.close()

    ctx1a = ExitStack()
    xstage = [ctx1a.enter_context(nc.sbuf_tensor(f"xstage{i}", [128, C], F32)) for i in range(2)]
    lnstage = [ctx1a.enter_context(nc.sbuf_tensor(f"lnstage{i}", [128, C], BF16)) for i in range(2)]
    xfm = [ctx1a.enter_context(nc.sbuf_tensor(f"xfm{i}", [128, 16, 512], BF16)) for i in range(2)]
    wk_sb = ctx1a.enter_context(nc.sbuf_tensor("wk_sb", [128, 16, 512], BF16))
    wv_sb = ctx1a.enter_context(nc.sbuf_tensor("wv_sb", [128, 16, 512], BF16))
    st1 = ctx1a.enter_context(nc.sbuf_tensor("st1", [128, 2, 8], F32))
    psT = [ctx1a.enter_context(nc.psum_tensor(f"psT{i}", [128, 4, 128], BF16)) for i in range(4)]
    psK = [ctx1a.enter_context(nc.psum_tensor(f"psK{i}", [128, 512], F32)) for i in range(2)]
    psV = [ctx1a.enter_context(nc.psum_tensor(f"psV{i}", [128, 512], F32)) for i in range(2)]

    w_kv = []
    for (d_w, sb) in ((d_wk, wk_sb), (d_wv, wv_sb)):
        wait("gpsimd", "sws", cur("sws"))
        v = inc("sws", 16)
        w_kv.append(v)
        emit("gpsimd", lambda e, d=d_w, s=sb: e.dma_start(
            out=s[:], in_=d.rearrange("(dc p) f -> p dc f", p=128)).then_inc(sems["sws"], 16))

    ln_done, tp_done, evac_done = {}, {}, {}
    kv_mm_done, rope_done, vevac_done = {}, {}, {}

    for c in range(NCH):
        cb = c % 2
        for t in range(4):
            gt = 4 * c + t
            wait("sync", "ve", v_rope_add if gt == 0 else 0)
            wait("sync", "ve", ln_done.get(gt - 2, 0))
            lane = "sx%d" % (gt % 2)
            v_x = inc(lane, 16)
            emit("sync", lambda e, gt=gt, lane=lane: e.dma_start(
                out=xstage[gt % 2][:],
                in_=d_x[gt * 128:(gt + 1) * 128, :]).then_inc(sems[lane], 16))
            wait("vector", lane, v_x)
            v_rs = inc("ve")
            emit("vector", lambda e, gt=gt: e.reduce_sum(
                st1[:, gt % 2, 0:1], xstage[gt % 2][:], axis=AX.X).then_inc(sems["ve"], 1))
            wait("scalar", lane, v_x)
            wait("scalar", "ve", ln_done.get(gt - 2, 0))
            wait("scalar", "mm", tp_done.get(gt - 2, 0))
            v_sq = inc("sc")
            emit("scalar", lambda e, gt=gt: e.activation(
                lnstage[gt % 2][:], xstage[gt % 2][:], AF.Square,
                accum_out=st1[:, gt % 2, 1:2]).then_inc(sems["sc"], 1))
            wait("vector", "sc", v_sq)
            wait("vector", "ve", v_rs)
            v_mean = inc("ve")
            emit("vector", lambda e, gt=gt: e.tensor_scalar(
                st1[:, gt % 2, 2:3], st1[:, gt % 2, 0:1], 1.0 / C, None,
                op0=ALU.mult).then_inc(sems["ve"], 1))
            wait("vector", "ve", v_mean)
            v_msq = inc("ve")
            emit("vector", lambda e, gt=gt: e.tensor_mul(
                st1[:, gt % 2, 3:4], st1[:, gt % 2, 2:3],
                st1[:, gt % 2, 2:3]).then_inc(sems["ve"], 1))
            wait("vector", "ve", v_msq)
            v_var = inc("ve")
            emit("vector", lambda e, gt=gt: e.scalar_tensor_tensor(
                st1[:, gt % 2, 4:5], st1[:, gt % 2, 1:2], 1.0 / C, st1[:, gt % 2, 3:4],
                op0=ALU.mult, op1=ALU.subtract).then_inc(sems["ve"], 1))
            wait("scalar", "ve", v_var)
            v_sd = inc("sc")
            emit("scalar", lambda e, gt=gt: e.activation(
                st1[:, gt % 2, 5:6], st1[:, gt % 2, 4:5], AF.Sqrt,
                bias=epsb[:, 0:1]).then_inc(sems["sc"], 1))
            wait("vector", "sc", v_sd)
            v_rc2 = inc("ve")
            emit("vector", lambda e, gt=gt: e.reciprocal(
                st1[:, gt % 2, 6:7], st1[:, gt % 2, 5:6]).then_inc(sems["ve"], 1))
            wait("vector", "mm", tp_done.get(gt - 2, 0))
            wait("vector", "ve", v_rc2)
            v_ln = inc("ve")
            ln_done[gt] = v_ln
            emit("vector", lambda e, gt=gt: e.tensor_scalar(
                lnstage[gt % 2][:], xstage[gt % 2][:], st1[:, gt % 2, 2:3], st1[:, gt % 2, 6:7],
                op0=ALU.subtract, op1=ALU.mult).then_inc(sems["ve"], 1))
            wait("tensor", "sid", v_id)
            wait("tensor", "ve", v_ln)
            for g4 in range(4):
                gi = gt * 4 + g4
                par = gi % 4
                wait("tensor", "ve", evac_done.get(gi - 4, 0))
                for j in range(4):
                    dc = g4 * 4 + j
                    def tp(e, par=par, j=j, dc=dc, gt=gt, last=(j == 3)):
                        ins = e.transpose(psT[par][:, j, :],
                                          lnstage[gt % 2][:, dc * 128:(dc + 1) * 128], identb[:])
                        if last:
                            ins.then_inc(sems["mm"], 1)
                    emit("tensor", tp)
                v_tp = inc("mm")
                wait("vector", "mm", v_tp)
                wait("vector", "mm", kv_mm_done.get(c - 2, 0))
                v_ev = inc("ve")
                evac_done[gi] = v_ev
                emit("vector", lambda e, cb=cb, g4=g4, t=t, par=par: e.tensor_copy(
                    xfm[cb][:, g4 * 4:(g4 + 1) * 4, t * 128:(t + 1) * 128],
                    psT[par][:]).then_inc(sems["ve"], 1))
            tp_done[gt] = cur("mm")
        chunk_evac = evac_done[(4 * c + 3) * 4 + 3]
        wait("vector", "ve", evac_done[(4 * c) * 4 + 3])
        emit("vector", lambda e, cb=cb, c=c: e.tensor_copy(
            xfm_own[:, :, c * 128:(c + 1) * 128], xfm[cb][:, :, 0:128]))
        wait("tensor", "sws", w_kv[0])
        wait("tensor", "ve", chunk_evac)
        for fc in range(4):
            prev = (c, fc - 2) if fc >= 2 else (c - 1, fc + 2)
            lastro = (c, fc - 1) if fc >= 1 else (c - 1, 3)
            wait("tensor", "ve", rope_done.get(prev, 0))
            for dc in range(16):
                def kmm(e, cb=cb, fc=fc, dc=dc):
                    ins = e.matmul(psK[fc % 2][:],
                                   wk_sb[:, dc, fc * 128:(fc + 1) * 128],
                                   xfm[cb][:, dc, :],
                                   start=(dc == 0), stop=(dc == 15))
                    if dc == 15:
                        ins.then_inc(sems["mm"], 1)
                emit("tensor", kmm)
            v_kmm = inc("mm")
            wait("scalar", "mm", v_kmm)
            wait("scalar", "ve", rope_done.get(lastro, 0))
            v_rc = inc("sc")
            emit("scalar", lambda e, fc=fc: e.copy(roptmp[0:64, :], psK[fc % 2][64:128, :]))
            emit("scalar", lambda e, fc=fc: e.copy(
                roptmp[64:128, :], psK[fc % 2][0:64, :]).then_inc(sems["sc"], 1))
            wait("vector", "mm", v_kmm)
            wait("vector", "sc", v_rc)
            emit("vector", lambda e, c=c, fc=fc: e.tensor_mul(
                roptmp2[:], psK[fc % 2][:], ropeC[:, c * 512:(c + 1) * 512]))
            v_rm = inc("ve")
            emit("vector", lambda e, c=c: e.tensor_mul(
                roptmp[:], roptmp[:], ropeS[:, c * 512:(c + 1) * 512]).then_inc(sems["ve"], 1))
            wait("vector", "ve", v_rm)
            v_rp = inc("ve")
            rope_done[(c, fc)] = v_rp
            emit("vector", lambda e, c=c, fc=fc: e.tensor_add(
                k_sb[:, fc, c * 512:(c + 1) * 512],
                roptmp2[:], roptmp[:]).then_inc(sems["ve"], 1))
        wait("tensor", "sws", w_kv[1])
        for tt in range(4):
            prev = (c, tt - 2) if tt >= 2 else (c - 1, tt + 2)
            wait("tensor", "ve", vevac_done.get(prev, 0))
            for dc in range(16):
                def vmm(e, cb=cb, tt=tt, dc=dc):
                    ins = e.matmul(psV[tt % 2][:],
                                   xfm[cb][:, dc, tt * 128:(tt + 1) * 128],
                                   wv_sb[:, dc, :],
                                   start=(dc == 0), stop=(dc == 15))
                    if dc == 15:
                        ins.then_inc(sems["mm"], 1)
                emit("tensor", vmm)
            v_vmm = inc("mm")
            wait("vector", "mm", v_vmm)
            v_vev = inc("ve")
            vevac_done[(c, tt)] = v_vev
            emit("vector", lambda e, c=c, tt=tt: e.tensor_copy(
                v_sb[:, c * 4 + tt, :, 0:128],
                psV[tt % 2][:].rearrange("p (h d) -> p h d", h=4)).then_inc(sems["ve"], 1))
        kv_mm_done[c] = cur("mm")
    v_1a_ve = cur("ve")
    v_1a_sc = cur("sc")
    ctx1a.close()

    ctx1b = ExitStack()
    wqring = [ctx1b.enter_context(nc.sbuf_tensor(f"wqring{i}", [128, 16, 512], BF16)) for i in range(2)]
    psQ = [ctx1b.enter_context(nc.psum_tensor(f"psQ{i}", [128, 512], F32)) for i in range(2)]
    q_mm_done, qrope_done = {}, {}

    def emit_wq_load(fb):
        wait("gpsimd", "mm", q_mm_done.get(fb - 2, 0))
        lane = "swq%d" % (fb % 2)
        v = inc(lane, 16)
        emit("gpsimd", lambda e, fb=fb, lane=lane: e.dma_start(
            out=wqring[fb % 2][:],
            in_=d_wq[:, fb * 512:(fb + 1) * 512].rearrange("(dc p) f -> p dc f", p=128)
        ).then_inc(sems[lane], 16))
        return v

    wait("gpsimd", "ve", v_1a_ve)
    wait("gpsimd", "sc", v_1a_sc)
    wait("gpsimd", "mm", kv_mm_done[NCH - 1])
    wait("gpsimd", "sx0", cur("sx0"))
    wait("gpsimd", "sx1", cur("sx1"))
    wq_ld = {0: emit_wq_load(0), 1: emit_wq_load(1)}
    for fb in range(4):
        wait("tensor", "swq%d" % (fb % 2), wq_ld[fb])
        for fcc in range(4):
            fc = fb * 4 + fcc
            for tc in range(2):
                u = fc * 2 + tc
                wait("tensor", "ve", v_1a_ve)
                wait("tensor", "sc", v_1a_sc)
                wait("tensor", "ve", qrope_done.get(u - 2, 0))
                for dc in range(16):
                    def qmm(e, fb=fb, fcc=fcc, tc=tc, dc=dc, u=u):
                        ins = e.matmul(psQ[u % 2][:],
                                       wqring[fb % 2][:, dc, fcc * 128:(fcc + 1) * 128],
                                       xfm_own[:, dc, tc * 512:(tc + 1) * 512],
                                       start=(dc == 0), stop=(dc == 15))
                        if dc == 15:
                            ins.then_inc(sems["mm"], 1)
                    emit("tensor", qmm)
                v_qmm = inc("mm")
                wait("scalar", "mm", v_qmm)
                wait("scalar", "ve", qrope_done.get(u - 1, 0))
                v_rc = inc("sc")
                emit("scalar", lambda e, u=u: e.copy(roptmp[0:64, :], psQ[u % 2][64:128, :]))
                emit("scalar", lambda e, u=u: e.copy(
                    roptmp[64:128, :], psQ[u % 2][0:64, :]).then_inc(sems["sc"], 1))
                wait("vector", "mm", v_qmm)
                wait("vector", "sc", v_rc)

                def ropeap(tb, tc=tc):
                    return bass.AP(tb, tc * 2048, [[T, 128], [512, 4], [1, 128]])
                emit("vector", lambda e, u=u, ap=ropeap(ropeC): e.tensor_mul(
                    roptmp2[:], psQ[u % 2][:], ap))
                v_rm = inc("ve")
                emit("vector", lambda e, ap=ropeap(ropeS): e.tensor_mul(
                    roptmp[:], roptmp[:], ap).then_inc(sems["ve"], 1))
                wait("vector", "ve", v_rm)
                v_rp = inc("ve")
                qrope_done[u] = v_rp
                emit("vector", lambda e, fc=fc, tc=tc: e.tensor_add(
                    q_sb[:, fc, tc * 512:(tc + 1) * 512],
                    roptmp2[:], roptmp[:]).then_inc(sems["ve"], 1))
        q_mm_done[fb] = cur("mm")
        if fb + 2 < 4:
            wq_ld[fb + 2] = emit_wq_load(fb + 2)
    barrier()
    if DBG:
        inc("scr", 16)
        emit("sync", lambda e: e.dma_start(out=ddbg["xfm_own"], in_=xfm_own[:]).then_inc(sems["scr"], 16))
    if DBG:
        inc("scr", 16)
        emit("sync", lambda e: e.dma_start(out=ddbg["k_sb"], in_=k_sb[:]).then_inc(sems["scr"], 16))
    if DBG:
        inc("scr", 16)
        emit("sync", lambda e: e.dma_start(out=ddbg["v_sb"], in_=v_sb[:]).then_inc(sems["scr"], 16))
    if DBG:
        inc("scr", 16)
        emit("sync", lambda e: e.dma_start(out=ddbg["q_sb"], in_=q_sb[:]).then_inc(sems["scr"], 16))
    ctx1b.close()
    ctxAB.close()

    ctxW = ExitStack()
    wpring = [ctxW.enter_context(nc.sbuf_tensor(f"wpring{i}", [128, 16, 512], BF16)) for i in range(2)]
    ctxC = ExitStack()
    maskA = ctxC.enter_context(nc.sbuf_tensor("maskA_s", [128, NCH, 512], BF16))
    maskB = ctxC.enter_context(nc.sbuf_tensor("maskB_s", [128, NCH, 4, 128], BF16))
    pt_sb = [ctxC.enter_context(nc.sbuf_tensor(f"pt{i}", [128, NB, 128], BF16)) for i in range(2)]
    y_sb = [ctxC.enter_context(nc.sbuf_tensor(f"ysb{i}", [128, C], BF16)) for i in range(2)]
    rinvb = ctxC.enter_context(nc.sbuf_tensor("rinvb", [128, 1], F32))

    v_mA = inc("smk", 16)
    emit("sync", lambda e: e.dma_start(out=maskA[:], in_=d_maskA).then_inc(sems["smk"], 16))
    wait("sync", "smk", v_mA)
    v_mB = inc("smk", 16)
    emit("sync", lambda e: e.dma_start(out=maskB[:], in_=d_maskB).then_inc(sems["smk"], 16))


    ctxCB = ExitStack()
    psB = [ctxCB.enter_context(nc.psum_tensor(f"psB{i}", [128, 512], F32)) for i in range(4)]
    psY = [ctxCB.enter_context(nc.psum_tensor(f"psY{i}", [128, 129], F32)) for i in range(2)]
    psYT = [ctxCB.enter_context(nc.psum_tensor(f"psYT{i}", [128, 4, 128], BF16)) for i in range(2)]

    wp_mm_done, x2_done = {}, {}

    def emit_wp_load(fb):
        wait("gpsimd", "mm", wp_mm_done.get(fb - 2, 0))
        lane = "swp%d" % (fb % 2)
        v = inc(lane, 16)
        emit("gpsimd", lambda e, fb=fb, lane=lane: e.dma_start(
            out=wpring[fb % 2][:],
            in_=d_wp[:, fb * 512:(fb + 1) * 512].rearrange("(dc p) f -> p dc f", p=128)
        ).then_inc(sems[lane], 16))
        return v

    wp_ld = {0: emit_wp_load(0), 1: emit_wp_load(1)}
    wait("vector", "smk", v_mB)
    seqB = 0
    exp_done, yev_done, av_done, ytr_done = {}, {}, {}, {}
    ytr_last = {}
    for oc in range(NCH):
        for h in range(16):
            u = oc * 16 + h
            nk = oc + 1
            wait("scalar", "mm", av_done.get(u - 2, 0))
            for kc in range(nk):
                bslot = seqB % 4
                wait("tensor", "sc", exp_done.get(seqB - 4, 0))
                for j in range(4):
                    def bmm(e, bslot=bslot, h=h, oc=oc, kc=kc, j=j):
                        ins = e.matmul(psB[bslot][:, j * 128:(j + 1) * 128],
                                       k_sb[:, h // 4, (kc * 4 + j) * 128:(kc * 4 + j + 1) * 128],
                                       q_sb[:, h, oc * 128:(oc + 1) * 128],
                                       start=(j == 0), stop=(j == 3))
                        if j == 3:
                            ins.then_inc(sems["mm"], 1)
                    emit("tensor", bmm)
                v_bmm = inc("mm")
                if kc == nk - 1:
                    wait("vector", "mm", v_bmm)
                    v_msk = inc("ve")
                    emit("vector", lambda e, bslot=bslot, oc=oc: e.tensor_add(
                        psB[bslot][:], psB[bslot][:],
                        maskB[:, oc, :, :]).then_inc(sems["ve"], 1))
                    wait("scalar", "ve", v_msk)
                else:
                    wait("scalar", "mm", v_bmm)
                v_exp = inc("sc")
                exp_done[seqB] = v_exp
                emit("scalar", lambda e, bslot=bslot, kc=kc, ub=u % 2: e.activation(
                    pt_sb[ub][:, kc * 4:(kc + 1) * 4, :], psB[bslot][:], AF.Exp,
                    scale=float(SCALE)).then_inc(sems["sc"], 1))
                seqB += 1
            wait("tensor", "ve", yev_done.get(u - 2, 0))
            if nk > 1:
                wait("tensor", "sc", exp_done[seqB - 2])
            for j in range(4 * (nk - 1)):
                def ymm1(e, u=u, j=j, h=h):
                    e.matmul(psY[u % 2][:],
                             pt_sb[u % 2][:, j, :],
                             v_sb[:, j, h // 4, :],
                             start=(j == 0), stop=False)
                emit("tensor", ymm1)
            wait("tensor", "sc", exp_done[seqB - 1])
            for j in range(4 * (nk - 1), 4 * nk):
                def ymm(e, u=u, j=j, h=h, last=(j == 4 * nk - 1), first=(j == 0)):
                    ins = e.matmul(psY[u % 2][:],
                                   pt_sb[u % 2][:, j, :],
                                   v_sb[:, j, h // 4, :],
                                   start=first, stop=last)
                    if last:
                        ins.then_inc(sems["mm"], 1)
                emit("tensor", ymm)
            v_ymm = inc("mm")
            av_done[u] = v_ymm
            wait("vector", "mm", v_ymm)
            wait("vector", "mm", ytr_last.get(oc - 2, 0))
            v_ri = inc("ve")
            emit("vector", lambda e, u=u: e.reciprocal(
                rinvb[:], psY[u % 2][:, 128:129]).then_inc(sems["ve"], 1))
            wait("vector", "ve", v_ri)
            v_ye = inc("ve")
            yev_done[u] = v_ye
            emit("vector", lambda e, u=u, oc=oc, h=h: e.tensor_scalar(
                y_sb[oc % 2][:, h * 128:(h + 1) * 128], psY[u % 2][:, 0:128],
                rinvb[:, 0:1], None, op0=ALU.mult).then_inc(sems["ve"], 1))
        wait("tensor", "ve", yev_done[oc * 16 + 15])
        for g4 in range(4):
            gi = oc * 4 + g4
            par = gi % 2
            wait("tensor", "ve", ytr_done.get(gi - 2, 0))
            for j in range(4):
                cc = g4 * 4 + j
                def ytp(e, par=par, oc=oc, cc=cc, j=j, last=(j == 3)):
                    ins = e.transpose(psYT[par][:, j, :],
                                      y_sb[oc % 2][:, cc * 128:(cc + 1) * 128], identb[:])
                    if last:
                        ins.then_inc(sems["mm"], 1)
                emit("tensor", ytp)
            v_yt = inc("mm")
            wait("vector", "mm", v_yt)
            v_yte = inc("ve")
            ytr_done[gi] = v_yte
            emit("vector", lambda e, oc=oc, g4=g4, par=par: e.tensor_copy(
                yfm[:, oc, g4 * 4:(g4 + 1) * 4, :], psYT[par][:]).then_inc(sems["ve"], 1))
        ytr_last[oc] = cur("mm")
    barrier()
    if DBG:
        inc("scr", 16)
        emit("sync", lambda e: e.dma_start(out=ddbg["yfm"], in_=yfm[:]).then_inc(sems["scr"], 16))
    ctxCB.close()
    ctxC.close()

    ctxD = ExitStack()
    xorig = [ctxD.enter_context(nc.sbuf_tensor(f"xorig{i}", [128, 512], F32)) for i in range(2)]
    psW = [ctxD.enter_context(nc.psum_tensor(f"psW{i}", [128, 512], F32)) for i in range(2)]

    for fb in range(4):
        wait("tensor", "swp%d" % (fb % 2), wp_ld[fb])
        for oc in range(NCH):
            k = fb * NCH + oc
            wait("sync", "ve", x2_done.get(k - 2, 0))
            lane = "sxo%d" % (k % 2)
            v_xo = inc(lane, 16)
            emit("sync", lambda e, oc=oc, fb=fb, k=k, lane=lane: e.dma_start(
                out=xorig[k % 2][:],
                in_=d_x[oc * 512:oc * 512 + 128, fb * 512:(fb + 1) * 512]
            ).then_inc(sems[lane], 16))
            wait("tensor", "ve", x2_done.get(k - 2, 0))
            for cc in range(16):
                def pmm(e, fb=fb, oc=oc, cc=cc, k=k):
                    ins = e.matmul(psW[k % 2][:],
                                   yfm[:, oc, cc, :],
                                   wpring[fb % 2][:, cc, :],
                                   start=(cc == 0), stop=(cc == 15))
                    if cc == 15:
                        ins.then_inc(sems["mm"], 1)
                emit("tensor", pmm)
            v_pmm = inc("mm")
            wait("vector", "mm", v_pmm)
            wait("vector", lane, v_xo)
            v_x2 = inc("ve")
            x2_done[k] = v_x2
            emit("vector", lambda e, oc=oc, fb=fb, k=k: e.tensor_add(
                x2f[:, oc, fb * 512:(fb + 1) * 512], psW[k % 2][:],
                xorig[k % 2][:]).then_inc(sems["ve"], 1))
        wp_mm_done[fb] = cur("mm")
        if fb + 2 < 4:
            wp_ld[fb + 2] = emit_wp_load(fb + 2)
    barrier()
    if DBG:
        inc("scr", 16)
        emit("sync", lambda e: e.dma_start(out=ddbg["x2f"], in_=x2f[:]).then_inc(sems["scr"], 16))
    ctxD.close()
    ctxW.close()

    ctxE = ExitStack()
    ln2stage = ctxE.enter_context(nc.sbuf_tensor("ln2stage", [128, C], BF16))
    st2 = ctxE.enter_context(nc.sbuf_tensor("st2", [128, 2, 8], F32))
    psT2 = [ctxE.enter_context(nc.psum_tensor(f"psT2{i}", [128, 4, 128], BF16)) for i in range(4)]

    ln2_done, tp2_done, ev2_done = {}, {}, {}
    for oc in range(NCH):
        v_rs2 = inc("ve")
        emit("vector", lambda e, oc=oc: e.reduce_sum(
            st2[:, oc % 2, 0:1], x2f[:, oc, :], axis=AX.X).then_inc(sems["ve"], 1))
        wait("scalar", "ve", ln2_done.get(oc - 2, 0))
        wait("scalar", "mm", tp2_done.get(oc - 1, 0))
        wait("scalar", "sac", (oc) * 16)
        v_sq = inc("sc")
        emit("scalar", lambda e, oc=oc: e.activation(
            ln2stage[:], x2f[:, oc, :], AF.Square,
            accum_out=st2[:, oc % 2, 1:2]).then_inc(sems["sc"], 1))
        wait("vector", "sc", v_sq)
        wait("vector", "ve", v_rs2)
        v_mean = inc("ve")
        emit("vector", lambda e, oc=oc: e.tensor_scalar(
            st2[:, oc % 2, 2:3], st2[:, oc % 2, 0:1], 1.0 / C, None,
            op0=ALU.mult).then_inc(sems["ve"], 1))
        wait("vector", "ve", v_mean)
        v_msq = inc("ve")
        emit("vector", lambda e, oc=oc: e.tensor_mul(
            st2[:, oc % 2, 3:4], st2[:, oc % 2, 2:3],
            st2[:, oc % 2, 2:3]).then_inc(sems["ve"], 1))
        wait("vector", "ve", v_msq)
        v_var = inc("ve")
        emit("vector", lambda e, oc=oc: e.scalar_tensor_tensor(
            st2[:, oc % 2, 4:5], st2[:, oc % 2, 1:2], 1.0 / C, st2[:, oc % 2, 3:4],
            op0=ALU.mult, op1=ALU.subtract).then_inc(sems["ve"], 1))
        wait("scalar", "ve", v_var)
        v_sd = inc("sc")
        emit("scalar", lambda e, oc=oc: e.activation(
            st2[:, oc % 2, 5:6], st2[:, oc % 2, 4:5], AF.Sqrt,
            bias=epsb[:, 0:1]).then_inc(sems["sc"], 1))
        wait("vector", "sc", v_sd)
        v_rc2 = inc("ve")
        emit("vector", lambda e, oc=oc: e.reciprocal(
            st2[:, oc % 2, 6:7], st2[:, oc % 2, 5:6]).then_inc(sems["ve"], 1))
        wait("vector", "mm", tp2_done.get(oc - 1, 0))
        wait("vector", "ve", v_rc2)
        v_ln = inc("ve")
        ln2_done[oc] = v_ln
        emit("vector", lambda e, oc=oc: e.tensor_scalar(
            ln2stage[:], x2f[:, oc, :], st2[:, oc % 2, 2:3], st2[:, oc % 2, 6:7],
            op0=ALU.subtract, op1=ALU.mult).then_inc(sems["ve"], 1))
        wait("sync", "ve", v_ln)
        inc("scr", 16)
        emit("sync", lambda e, oc=oc: e.dma_start(
            out=d_x2[oc * 128:(oc + 1) * 128, :], in_=x2f[:, oc, :]).then_inc(sems["scr"], 16))
        wait("sync", "mm", tp2_done.get(oc, 0) if False else 0)
        inc("sac", 16)
        emit("sync", lambda e, oc=oc: e.dma_start(
            out=d_ln2[oc * 128:(oc + 1) * 128, :], in_=ln2stage[:]).then_inc(sems["sac"], 16))
        wait("tensor", "ve", v_ln)
        for g4 in range(4):
            gi = oc * 4 + g4
            par = gi % 4
            wait("tensor", "ve", ev2_done.get(gi - 4, 0))
            for j in range(4):
                dc = g4 * 4 + j
                def tp2(e, par=par, dc=dc, j=j, last=(j == 3)):
                    ins = e.transpose(psT2[par][:, j, :],
                                      ln2stage[:, dc * 128:(dc + 1) * 128], identb[:])
                    if last:
                        ins.then_inc(sems["mm"], 1)
                emit("tensor", tp2)
            v_tp = inc("mm")
            wait("vector", "mm", v_tp)
            v_ev = inc("ve")
            ev2_done[gi] = v_ev
            emit("vector", lambda e, oc=oc, g4=g4, par=par: e.tensor_copy(
                xfm2[:, g4 * 4:(g4 + 1) * 4, oc * 128:(oc + 1) * 128],
                psT2[par][:]).then_inc(sems["ve"], 1))
        tp2_done[oc] = cur("mm")
    v_scr_all = cur("scr")
    wait("sync", "scr", v_scr_all)
    barrier()
    if DBG:
        inc("scr", 16)
        emit("sync", lambda e: e.dma_start(out=ddbg["xfm2"], in_=xfm2[:]).then_inc(sems["scr"], 16))
    ctxE.close()

    ctxF = ExitStack()
    ring = [ctxF.enter_context(nc.sbuf_tensor(f"ring{i}", [128, 16, 512], BF16)) for i in range(4)]
    silu = [ctxF.enter_context(nc.sbuf_tensor(f"silu{i}", [128, 512], BF16)) for i in range(2)]
    rtr = ctxF.enter_context(nc.sbuf_tensor("rtr", [128, 32], F32))
    zap = ctxF.enter_context(nc.sbuf_tensor("zap", [128, 16], F32))
    sel = ctxF.enter_context(nc.sbuf_tensor("sel", [128, 16], F32))
    wr_sb = ctxF.enter_context(nc.sbuf_tensor("wr_sb", [128, 16, 16], BF16))
    cwpad = ctxF.enter_context(nc.sbuf_tensor("cwpad", [128, NCH, 64], F32))
    ekf = ctxF.enter_context(nc.sbuf_tensor("ekf", [128, NCH, 4], F32))
    idxi = ctxF.enter_context(nc.sbuf_tensor("idxi", [128, 8], U32))
    oh = ctxF.enter_context(nc.sbuf_tensor("oh", [128, NCH, 16], BF16))
    ohk = ctxF.enter_context(nc.sbuf_tensor("ohk", [128, NCH, 4, 16], BF16))
    rsK = ctxF.enter_context(nc.sbuf_tensor("rsK", [128, 16], F32))
    oht2 = ctxF.enter_context(nc.sbuf_tensor("oht2", [128, 16], BF16))
    rank_tok = ctxF.enter_context(nc.sbuf_tensor("rank_tok", [128, NCH, 16], F32))
    ranksel = ctxF.enter_context(nc.sbuf_tensor("ranksel", [128, NCH, 4], F32))
    slotf = ctxF.enter_context(nc.sbuf_tensor("slotf", [128, 32], F32))
    slotm = ctxF.enter_context(nc.sbuf_tensor("slotm", [128, 32], F32))
    slotg = ctxF.enter_context(nc.sbuf_tensor("slotg", [128, 32], F32))
    zero32 = ctxF.enter_context(nc.sbuf_tensor("zero32", [128, 32], F32))
    slotu = ctxF.enter_context(nc.sbuf_tensor("slotu", [128, 32], U32))
    erow_sb = ctxF.enter_context(nc.sbuf_tensor("erow_sb", [128, 16], F32))
    ltu_sb = ctxF.enter_context(nc.sbuf_tensor("ltu_sb", [128, 128], BF16))
    lton_sb = ctxF.enter_context(nc.sbuf_tensor("lton_sb", [128, 128], BF16))
    tid_sb = ctxF.enter_context(nc.sbuf_tensor("tid_sb", [128, 8], U32))
    identf = ctxF.enter_context(nc.sbuf_tensor("identf2", [128, 128], F32))
    pfil = ctxF.enter_context(nc.sbuf_tensor("pfil", [128, 64], U32))
    zrow = ctxF.enter_context(nc.sbuf_tensor("zrow", [1, C], BF16))
    zrow2 = ctxF.enter_context(nc.sbuf_tensor("zrow2", [1, 64], F32))
    bidx_sb = ctxF.enter_context(nc.sbuf_tensor("bidx_sb", [128, 64], U32))

    v_cst = inc("smk", 16)
    emit("sync", lambda e: e.dma_start(out=tid_sb[:], in_=d_tid).then_inc(sems["smk"], 16))
    wait("sync", "smk", v_cst)
    v_cst = inc("smk", 16)
    emit("sync", lambda e: e.dma_start(out=erow_sb[:], in_=d_erow).then_inc(sems["smk"], 16))
    wait("sync", "smk", v_cst)
    v_cst = inc("smk", 16)
    emit("sync", lambda e: e.dma_start(out=ltu_sb[:], in_=d_ltu).then_inc(sems["smk"], 16))
    v_cst_all = v_cst
    emit("vector", lambda e: e.memset(lton_sb[:], 1.0))
    emit("vector", lambda e: e.memset(zero32[:], 0.0))
    emit("vector", lambda e: e.memset(cwpad[:], 0.0))
    emit("vector", lambda e: e.memset(pfil[:], 1024))
    emit("vector", lambda e: e.memset(zrow[:], 0.0))
    emit("vector", lambda e: e.memset(zrow2[:], 0.0))
    v_ms = inc("ve")
    emit("vector", lambda e: e.tensor_copy(identf[:], identb[:]).then_inc(sems["ve"], 1))
    wait("sync", "ve", v_ms)
    inc("sac", 16)
    emit("sync", lambda e: e.dma_start(out=d_ln2[1024:1025, :], in_=zrow[:]).then_inc(sems["sac"], 16))
    wait("sync", "sac", cur("sac"))
    inc("sac", 16)
    emit("sync", lambda e: e.dma_start(out=d_cwsc[1024:1025, :], in_=zrow2[:]).then_inc(sems["sac"], 16))
    wait("gpsimd", "ve", v_ms)
    v_pf = inc("sbx", 16)
    emit("gpsimd", lambda e: e.dma_start(
        out=d_bidx.rearrange("(p g) o -> p (g o)", p=128), in_=pfil[:]).then_inc(sems["sbx"], 16))

    ctxF1 = ExitStack()
    gbuf = [ctxF1.enter_context(nc.sbuf_tensor(f"gbuf{i}", [128, 8, 512], BF16)) for i in range(2)]
    psFF = [ctxF1.enter_context(nc.psum_tensor(f"psFF{i}", [128, 512], F32)) for i in range(8)]

    bank_last = {i: ("ve", 0) for i in range(8)}

    def wait_bank(eng, bk):
        sem, val = bank_last[bk]
        wait(eng, sem, val)

    wait("gpsimd", "sws", cur("sws"))
    v_wr = inc("sws", 16)
    emit("gpsimd", lambda e: e.dma_start(
        out=wr_sb[:], in_=d_wr.rearrange("(dc p) f -> p dc f", p=128)).then_inc(sems["sws"], 16))
    wait("tensor", "sws", v_wr)
    for tt in range(NCH):
        for dc in range(16):
            def rmm(e, tt=tt, dc=dc, last=(tt == NCH - 1 and dc == 15)):
                ins = e.matmul(psFF[7][:, tt * 16:(tt + 1) * 16],
                               xfm2[:, dc, tt * 128:(tt + 1) * 128],
                               wr_sb[:, dc, :],
                               start=(tt == 0 and dc == 0), stop=last)
                if last:
                    ins.then_inc(sems["mm"], 1)
            emit("tensor", rmm)
    v_rmm = inc("mm")
    wait("vector", "mm", v_rmm)
    wait("vector", "smk", v_cst_all)
    for tt in range(NCH):
        v1 = inc("ve")
        emit("vector", lambda e, tt=tt: e.tensor_copy(
            rtr[:, 0:16], psFF[7][:, tt * 16:(tt + 1) * 16]).then_inc(sems["ve"], 1))
        wait("vector", "ve", v1)
        v2 = inc("ve")
        emit("vector", lambda e: e.max(out=rtr[:, 16:24], in_=rtr[:, 0:16]).then_inc(sems["ve"], 1))
        wait("vector", "ve", v2)
        v2i = inc("ve")
        emit("vector", lambda e: e.max_index(
            out=idxi[:], in_max=rtr[:, 16:24], in_values=rtr[:, 0:16]).then_inc(sems["ve"], 1))
        wait("vector", "ve", v2i)
        v2c = inc("ve")
        emit("vector", lambda e, tt=tt: e.tensor_copy(
            ekf[:, tt, :], idxi[:, 0:4]).then_inc(sems["ve"], 1))
        wait("vector", "ve", v2c)
        for k in range(4):
            v_t = inc("ve")
            emit("vector", lambda e, tt=tt, k=k: e.tensor_scalar(
                ohk[:, tt, k, :], erow_sb[:], ekf[:, tt, k:k + 1], None,
                op0=ALU.is_equal).then_inc(sems["ve"], 1))
        wait("vector", "ve", cur("ve"))
        v_oh = inc("ve")
        emit("vector", lambda e, tt=tt: e.tensor_add(
            oh[:, tt, :], ohk[:, tt, 0, :], ohk[:, tt, 1, :]).then_inc(sems["ve"], 1))
        wait("vector", "ve", v_oh)
        v_oh2 = inc("ve")
        emit("vector", lambda e, tt=tt: e.tensor_add(
            oht2[:], ohk[:, tt, 2, :], ohk[:, tt, 3, :]).then_inc(sems["ve"], 1))
        wait("vector", "ve", v_oh2)
        v_oh3 = inc("ve")
        emit("vector", lambda e, tt=tt: e.tensor_add(
            oh[:, tt, :], oh[:, tt, :], oht2[:]).then_inc(sems["ve"], 1))
        v3 = inc("ve")
        emit("vector", lambda e: e.memset(rtr[:, 20:24], 3e38).then_inc(sems["ve"], 1))
        wait("vector", "ve", v3)
        emit("vector", lambda e: e.match_replace(
            out=zap[:], in_to_replace=rtr[:, 16:24], in_values=rtr[:, 0:16], imm_value=NEG))
        v4 = inc("ve")
        emit("vector", lambda e: e.match_replace(
            out=zap[:], in_to_replace=rtr[:, 16:24], in_values=rtr[:, 0:16],
            imm_value=NEG).then_inc(sems["ve"], 1))
        wait("vector", "ve", v4)
        v5 = inc("ve")
        emit("vector", lambda e: e.scalar_tensor_tensor(
            zap[:], zap[:], -NEG, rtr[:, 0:16], op0=ALU.add,
            op1=ALU.subtract).then_inc(sems["ve"], 1))
        wait("vector", "ve", v5)
        v_z = inc("ve")
        emit("vector", lambda e: e.tensor_scalar(
            rtr[:, 24:25], rtr[:, 16:17], -1.0, None, op0=ALU.mult).then_inc(sems["ve"], 1))
        wait("scalar", "ve", v_z)
        v_e = inc("sc")
        emit("scalar", lambda e: e.activation(
            sel[:], zap[:], AF.Exp, bias=rtr[:, 24:25], scale=-1.0,
            accum_out=rtr[:, 25:26]).then_inc(sems["sc"], 1))
        wait("vector", "sc", v_e)
        v_ri2 = inc("ve")
        emit("vector", lambda e: e.reciprocal(rtr[:, 26:27], rtr[:, 25:26]).then_inc(sems["ve"], 1))
        wait("vector", "ve", v_ri2)
        v_cw = inc("ve")
        emit("vector", lambda e, tt=tt: e.tensor_scalar(
            cwpad[:, tt, 0:16], sel[:], rtr[:, 26:27], float(TOPK) / (TOPK + 1),
            op0=ALU.mult, op1=ALU.mult).then_inc(sems["ve"], 1))
    v_router_ve = cur("ve")
    wait("sync", "ve", v_router_ve)
    wait("sync", "sac", cur("sac"))
    inc("sac", 16)
    emit("sync", lambda e: e.dma_start(
        out=d_cwsc[0:1024, :].rearrange("(tt p) f -> p tt f", p=128),
        in_=cwpad[:]).then_inc(sems["sac"], 16))
    wait("tensor", "ve", v_router_ve)
    rk_ev = {}
    for tt in range(NCH):
        bk = 6 - (tt % 2)
        wait_bank("tensor", bk)
        for tp in range(tt + 1):
            def kmm2(e, tt=tt, tp=tp, bk=bk, last=(tp == tt)):
                ins = e.matmul(psFF[bk][:, 0:16],
                               ltu_sb[:] if tp == tt else lton_sb[:],
                               oh[:, tp, :],
                               start=(tp == 0), stop=last)
                if last:
                    ins.then_inc(sems["mm"], 1)
            emit("tensor", kmm2)
        v_k = inc("mm")
        wait("vector", "mm", v_k)
        v_ev = inc("ve")
        rk_ev[tt] = v_ev
        bank_last[bk] = ("ve", v_ev)
        emit("vector", lambda e, tt=tt, bk=bk: e.tensor_copy(
            rank_tok[:, tt, :], psFF[bk][:, 0:16]).then_inc(sems["ve"], 1))
    for tt in range(NCH):
        for k in range(4):
            v_mu = inc("ve")
            emit("vector", lambda e, tt=tt, k=k: e.tensor_mul(
                rsK[:], rank_tok[:, tt, :], ohk[:, tt, k, :]).then_inc(sems["ve"], 1))
            wait("vector", "ve", v_mu)
            v_rs = inc("ve")
            emit("vector", lambda e, tt=tt, k=k: e.reduce_sum(
                ranksel[:, tt, k:k + 1], rsK[:], axis=AX.X).then_inc(sems["ve"], 1))
            wait("vector", "ve", v_rs)
    v_ranks_done = cur("ve")
    wait("vector", "ve", v_ranks_done)
    v_sf = inc("ve")
    emit("vector", lambda e: e.scalar_tensor_tensor(
        slotf[:], ekf[:].rearrange("p a b -> p (a b)"), 512.0,
        ranksel[:].rearrange("p a b -> p (a b)"),
        op0=ALU.mult, op1=ALU.add).then_inc(sems["ve"], 1))
    wait("vector", "ve", v_sf)
    rsl = ranksel[:].rearrange("p a b -> p (a b)")
    ekl = ekf[:].rearrange("p a b -> p (a b)")
    v_c1 = inc("ve")
    emit("vector", lambda e: e.tensor_single_scalar(
        slotm[:], rsl, 128.0, op=ALU.is_ge).then_inc(sems["ve"], 1))
    v_c2 = inc("ve")
    emit("vector", lambda e: e.tensor_single_scalar(
        slotg[:], rsl, 256.0, op=ALU.is_ge).then_inc(sems["ve"], 1))
    v_c3 = inc("ve")
    emit("vector", lambda e: e.tensor_single_scalar(
        zero32[:], rsl, 384.0, op=ALU.is_ge).then_inc(sems["ve"], 1))
    wait("vector", "ve", v_c2)
    v_g1 = inc("ve")
    emit("vector", lambda e: e.tensor_add(slotm[:], slotm[:], slotg[:]).then_inc(sems["ve"], 1))
    wait("vector", "ve", v_g1)
    v_g2 = inc("ve")
    emit("vector", lambda e: e.tensor_add(slotm[:], slotm[:], zero32[:]).then_inc(sems["ve"], 1))
    wait("vector", "ve", v_g2)
    v_p = inc("ve")
    emit("vector", lambda e: e.scalar_tensor_tensor(
        slotg[:], slotm[:], -128.0, rsl, op0=ALU.mult, op1=ALU.add).then_inc(sems["ve"], 1))
    v_g = inc("ve")
    emit("vector", lambda e: e.scalar_tensor_tensor(
        slotm[:], ekl, 4.0, slotm[:], op0=ALU.mult, op1=ALU.add).then_inc(sems["ve"], 1))
    wait("vector", "ve", v_g)
    v_ff = inc("ve")
    emit("vector", lambda e: e.scalar_tensor_tensor(
        slotf[:], slotg[:], 64.0, slotm[:], op0=ALU.mult, op1=ALU.add).then_inc(sems["ve"], 1))
    wait("vector", "ve", v_ff)
    v_su = inc("ve")
    emit("vector", lambda e: e.tensor_copy(slotu[:], slotf[:]).then_inc(sems["ve"], 1))
    wait("gpsimd", "ve", v_su)
    wait("gpsimd", "sbx", v_pf)
    wait("gpsimd", "smk", 16)
    for tt in range(NCH):
        for k in range(4):
            def bsc(e, tt=tt, k=k):
                e.indirect_dma_start(
                    out=d_bidx, out_offset=IOA(slotu[:, tt * 4 + k:tt * 4 + k + 1], 0),
                    in_=tid_sb[:, tt:tt + 1], in_offset=None).then_inc(sems["sbx"], 16)
            emit("gpsimd", bsc)
            inc("sbx", 16)
    v_bidx_built = cur("sbx")
    wait("gpsimd", "sbx", v_bidx_built)
    v_bread = inc("sbx", 16)
    emit("gpsimd", lambda e: e.dma_start(
        out=bidx_sb[:], in_=d_bidx.rearrange("(p g) o -> p (g o)", p=128)).then_inc(sems["sbx"], 16))

    slot_free = {}
    ld_done = {}
    nload = [0]

    def exp_srcs(e_):
        if e_ < 16:
            w1 = d_we1[e_]; w3 = d_we3[e_]; w2 = d_we2[e_]
        else:
            s = e_ - 16
            w1 = d_ws1[:, s * EH:(s + 1) * EH]
            w3 = d_ws3[:, s * EH:(s + 1) * EH]
            w2 = d_ws2[s * EH:(s + 1) * EH, :]
        return (
            w1[:, 0:512].rearrange("(dc p) f -> p dc f", p=128),
            w3[:, 0:512].rearrange("(dc p) f -> p dc f", p=128),
            w1[:, 512:1024].rearrange("(dc p) f -> p dc f", p=128),
            w3[:, 512:1024].rearrange("(dc p) f -> p dc f", p=128),
            w2[0:512, :].rearrange("(hc p) f -> p hc f", p=128),
            w2[512:1024, :].rearrange("(hc p) f -> p hc f", p=128),
        )

    def emit_load(key, src):
        i = nload[0]
        nload[0] += 1
        s = i % 4
        if i >= 4:
            assert i - 4 in slot_free, ("ring order", i)
            wait("gpsimd", "mm", slot_free[i - 4])
        lane = "sw%d" % s
        v = inc(lane, 16)
        ld_done[key] = (v, s)
        emit("gpsimd", lambda e, src=src, s=s, lane=lane: e.dma_start(
            out=ring[s][:], in_=src).then_inc(sems[lane], 16))
        return i

    DSH = 4
    loadix = {}
    for k in range(4):
        loadix[(0, k)] = emit_load((0, k), exp_srcs(16)[k])
    gq = 0
    sil_done, gmul_done = {}, {}
    we2_mm_last = {}
    ffe_done = {}
    for e_ in range(DSH):
        ge = 4 * (e_ % 2)
        for hh in range(2):
            v1, s1 = ld_done[(e_, 2 * hh)]
            v3, s3 = ld_done[(e_, 2 * hh + 1)]
            wait("tensor", "sw%d" % s1, v1)
            wait("tensor", "sw%d" % s3, v3)
            for tc in range(2):
                for hc4 in range(4):
                    hc = hh * 4 + hc4
                    b1 = ge + (gq % 2)
                    b3 = ge + 2 + (gq % 2)
                    wait_bank("tensor", b1)
                    for dc in range(16):
                        def g1mm(e, s1=s1, hc4=hc4, dc=dc, tc=tc, b1=b1):
                            ins = e.matmul(psFF[b1][:],
                                           ring[s1][:, dc, hc4 * 128:(hc4 + 1) * 128],
                                           xfm2[:, dc, tc * 512:(tc + 1) * 512],
                                           start=(dc == 0), stop=(dc == 15))
                            if dc == 15:
                                ins.then_inc(sems["mm"], 1)
                        emit("tensor", g1mm)
                    v_m1 = inc("mm")
                    wait_bank("tensor", b3)
                    for dc in range(16):
                        def g3mm(e, s3=s3, hc4=hc4, dc=dc, tc=tc, b3=b3):
                            ins = e.matmul(psFF[b3][:],
                                           ring[s3][:, dc, hc4 * 128:(hc4 + 1) * 128],
                                           xfm2[:, dc, tc * 512:(tc + 1) * 512],
                                           start=(dc == 0), stop=(dc == 15))
                            if dc == 15:
                                ins.then_inc(sems["mm"], 1)
                        emit("tensor", g3mm)
                    v_m3 = inc("mm")
                    wait("scalar", "mm", v_m1)
                    wait("scalar", "ve", gmul_done.get(gq - 2, 0))
                    v_sl = inc("sc")
                    sil_done[gq] = v_sl
                    emit("scalar", lambda e, b1=b1, sb=gq % 2: e.activation(
                        silu[sb][:], psFF[b1][:], AF.Silu).then_inc(sems["sc"], 1))
                    bank_last[b1] = ("sc", v_sl)
                    wait("vector", "mm", v_m3)
                    wait("vector", "sc", v_sl)
                    wait("vector", "mm", we2_mm_last.get((e_ - 1, tc), 0))
                    v_gm = inc("ve")
                    gmul_done[gq] = v_gm
                    emit("vector", lambda e, b3=b3, tc=tc, hc=hc, sb=gq % 2: e.tensor_mul(
                        gbuf[tc][:, hc, :], silu[sb][:], psFF[b3][:]).then_inc(sems["ve"], 1))
                    bank_last[b3] = ("ve", v_gm)
                    gq += 1
            if hh == 0:
                slot_free[loadix[(e_, 0)]] = cur("mm")
                slot_free[loadix[(e_, 1)]] = cur("mm")
                loadix[(e_, 4)] = emit_load((e_, 4), exp_srcs(16 + e_)[4])
                loadix[(e_, 5)] = emit_load((e_, 5), exp_srcs(16 + e_)[5])
        slot_free[loadix[(e_, 2)]] = cur("mm")
        slot_free[loadix[(e_, 3)]] = cur("mm")
        if e_ + 1 < DSH:
            loadix[(e_ + 1, 0)] = emit_load((e_ + 1, 0), exp_srcs(16 + e_ + 1)[0])
            loadix[(e_ + 1, 1)] = emit_load((e_ + 1, 1), exp_srcs(16 + e_ + 1)[1])
        v2a, s2a = ld_done[(e_, 4)]
        v2b, s2b = ld_done[(e_, 5)]
        wait("tensor", "sw%d" % s2a, v2a)
        wait("tensor", "sw%d" % s2b, v2b)
        w = 0
        gq0 = gq - 16
        for tc in range(2):
            wait("tensor", "ve", gmul_done[gq0 + 8 + 4 * tc + 3])
            for tt4 in range(4):
                tt = tc * 4 + tt4
                wset = 4 * ((e_ + w + 1) % 2)
                for dc4 in range(4):
                    wait_bank("tensor", wset + dc4)
                for hc in range(8):
                    s2 = s2a if hc < 4 else s2b
                    for dc4 in range(4):
                        def w2mm(e, s2=s2, hc=hc, tt4=tt4, tc=tc, dc4=dc4, wset=wset,
                                 last=(hc == 7 and dc4 == 3)):
                            rhs = bass.AP(ring[s2], (hc % 4) * 2048 + dc4 * 512,
                                          [[8192, 128], [1, 512]])
                            ins = e.matmul(psFF[wset + dc4][:],
                                           gbuf[tc][:, hc, tt4 * 128:(tt4 + 1) * 128],
                                           rhs,
                                           start=(hc == 0), stop=(hc == 7))
                            if last:
                                ins.then_inc(sems["mm"], 1)
                        emit("tensor", w2mm)
                v_w2 = inc("mm")
                wait("vector", "mm", v_w2)
                for dc4 in range(4):
                    def fev(e, e_=e_, tt=tt, dc4=dc4, wset=wset, last=(dc4 == 3)):
                        accsl = acc[:, tt, dc4 * 512:(dc4 + 1) * 512]
                        if e_ == 0:
                            ins = e.tensor_scalar(accsl, psFF[wset + dc4][:],
                                                  1.0 / (TOPK + 1), None, op0=ALU.mult)
                        else:
                            ins = e.scalar_tensor_tensor(accsl, psFF[wset + dc4][:],
                                                         1.0 / (TOPK + 1), accsl,
                                                         op0=ALU.mult, op1=ALU.add)
                        if last:
                            ins.then_inc(sems["ve"], 1)
                    emit("vector", fev)
                v_fe = inc("ve")
                ffe_done[(e_, tt4 + 4 * tc)] = v_fe
                for dc4 in range(4):
                    bank_last[wset + dc4] = ("ve", v_fe)
                w += 1
            we2_mm_last[(e_, tc)] = cur("mm")
        slot_free[loadix[(e_, 4)]] = cur("mm")
        slot_free[loadix[(e_, 5)]] = cur("mm")
        if e_ + 1 < DSH:
            loadix[(e_ + 1, 2)] = emit_load((e_ + 1, 2), exp_srcs(16 + e_ + 1)[2])
            loadix[(e_ + 1, 3)] = emit_load((e_ + 1, 3), exp_srcs(16 + e_ + 1)[3])
    v_shared_ve = cur("ve")
    v_shared_mm = cur("mm")
    wait("sync", "ve", v_shared_ve)
    wait("sync", "sac", cur("sac"))
    acst = {}
    for tt in range(NCH):
        acst[tt] = inc("sac", 16)
        emit("sync", lambda e, tt=tt: e.dma_start(
            out=d_accd[tt * 128:(tt + 1) * 128, :], in_=acc[:, tt, :]).then_inc(sems["sac"], 16))
    v_acc_stored = cur("sac")
    ctxF1.close()

    ctxF2 = ExitStack()
    xg_tok = nc.alloc_sbuf_tensor_at("xg_tok", [128, 4, C], BF16, offset=130784)
    xgfm = nc.alloc_sbuf_tensor_at("xgfm", [128, 16, 512], BF16, offset=147168)
    msta = [nc.alloc_sbuf_tensor_at("msta0", [128, 4, C], F32, offset=163552),
            nc.alloc_sbuf_tensor_at("msta1", [128, 4, C], F32, offset=196576)]
    cwg = [ctxF2.enter_context(nc.sbuf_tensor(f"cwg{i}", [128, 4, 64], F32)) for i in range(2)]
    g_sp = [ctxF2.enter_context(nc.sbuf_tensor(f"g_sp{i}", [128, 8, 512], BF16)) for i in range(2)]
    assert nc.sbuf_base <= 130784, ("sparse left overflow", nc.sbuf_base)
    psT3 = [ctxF2.enter_context(nc.psum_tensor(f"psT3{i}", [128, 4, 128], BF16)) for i in range(2)]
    psG1 = [ctxF2.enter_context(nc.psum_tensor(f"psG1{i}", [128, 512], F32)) for i in range(2)]
    psG3 = [ctxF2.enter_context(nc.psum_tensor(f"psG3{i}", [128, 512], F32)) for i in range(2)]
    psW2 = [ctxF2.enter_context(nc.psum_tensor(f"psW2{i}", [128, 512], F32)) for i in range(2)]

    wait("gpsimd", "mm", v_shared_mm)
    wait("gpsimd", "ve", v_shared_ve)
    wait("tensor", "ve", v_shared_ve)
    wait("tensor", "sc", cur("sc"))

    tp_sp = {}
    ev_sp = {}
    gm_sp = {}
    si_sp = {}
    fe_sp = {}
    gthc = {}
    sc_done = {}
    w2_sp = {}
    gth = {}
    def emit_gathers(e_):
        wait("gpsimd", "sbx", v_bread)
        wait("gpsimd", "mm", tp_sp.get(e_ - 1, 0))
        for sub in range(4):
            def gx1(e, e_=e_, sub=sub):
                e.indirect_dma_start(
                    out=xg_tok[:, sub, :], out_offset=None,
                    in_=d_ln2, in_offset=IOA(bidx_sb[:, e_ * 4 + sub:e_ * 4 + sub + 1], 0)
                ).then_inc(sems["sgt"], 16)
            emit("gpsimd", gx1)
            inc("sgt", 16)
        wait("gpsimd", "ve", fe_sp.get((e_ - 2, 15), 0))
        wait("gpsimd", "sac", v_acc_stored)
        wait("gpsimd", "sgc", cur("sgc"))
        for sub in range(4):
            def gx2(e, e_=e_, sub=sub):
                e.indirect_dma_start(
                    out=cwg[e_ % 2][:, sub, :], out_offset=None,
                    in_=d_cwsc, in_offset=IOA(bidx_sb[:, e_ * 4 + sub:e_ * 4 + sub + 1], 0)
                ).then_inc(sems["sgc"], 16)
            emit("gpsimd", gx2)
            inc("sgc", 16)
        gth[e_] = cur("sgt")
        gthc[e_] = cur("sgc")

    gq2 = 0
    for e_ in range(16):
        if e_ == 0:
            for k in range(4):
                loadix[(10, e_, k)] = emit_load((10, e_, k), exp_srcs(0)[k])
            emit_gathers(0)
        wait("tensor", "sgt", gth[e_])
        wait("tensor", "mm", w2_sp.get(e_ - 1, 0))
        for sub in range(4):
            for dq in range(4):
                grp = sub * 4 + dq
                par = grp % 2
                wait("tensor", "ve", ev_sp.get((e_, grp - 2), ev_sp.get((e_ - 1, grp + 14), 0)))
                for j in range(4):
                    dc = dq * 4 + j
                    def tpx(e, par=par, sub=sub, dc=dc, j=j, last=(j == 3)):
                        ins = e.transpose(psT3[par][:, j, :],
                                          xg_tok[:, sub, dc * 128:(dc + 1) * 128], identb[:])
                        if last:
                            ins.then_inc(sems["mm"], 1)
                    emit("tensor", tpx)
                v_tp = inc("mm")
                wait("vector", "mm", v_tp)
                v_ev = inc("ve")
                ev_sp[(e_, grp)] = v_ev
                emit("vector", lambda e, par=par, sub=sub, dq=dq: e.tensor_copy(
                    xgfm[:, dq * 4:(dq + 1) * 4, sub * 128:(sub + 1) * 128],
                    psT3[par][:]).then_inc(sems["ve"], 1))
        tp_sp[e_] = cur("mm")
        v_xgfm = cur("ve")
        for hh in range(2):
            v1, s1 = ld_done[(10, e_, 2 * hh)]
            v3, s3 = ld_done[(10, e_, 2 * hh + 1)]
            wait("tensor", "sw%d" % s1, v1)
            wait("tensor", "sw%d" % s3, v3)
            wait("tensor", "ve", v_xgfm)
            for hc4 in range(4):
                hc = hh * 4 + hc4
                b1 = gq2 % 2
                b3 = gq2 % 2
                wait("tensor", "sc", si_sp.get(gq2 - 2, 0))
                for dc in range(16):
                    def sg1(e, s1=s1, hc4=hc4, dc=dc, b1=b1):
                        ins = e.matmul(psG1[b1][:],
                                       ring[s1][:, dc, hc4 * 128:(hc4 + 1) * 128],
                                       xgfm[:, dc, :],
                                       start=(dc == 0), stop=(dc == 15))
                        if dc == 15:
                            ins.then_inc(sems["mm"], 1)
                    emit("tensor", sg1)
                v_m1 = inc("mm")
                wait("tensor", "ve", gm_sp.get((e_, hc - 2), gm_sp.get((e_ - 1, hc + 6), 0)))
                for dc in range(16):
                    def sg3(e, s3=s3, hc4=hc4, dc=dc, b3=b3):
                        ins = e.matmul(psG3[b3][:],
                                       ring[s3][:, dc, hc4 * 128:(hc4 + 1) * 128],
                                       xgfm[:, dc, :],
                                       start=(dc == 0), stop=(dc == 15))
                        if dc == 15:
                            ins.then_inc(sems["mm"], 1)
                    emit("tensor", sg3)
                v_m3 = inc("mm")
                wait("scalar", "mm", v_m1)
                wait("scalar", "ve", gm_sp.get((e_, hc - 2), gm_sp.get((e_ - 1, hc + 6), 0)))
                v_sl = inc("sc")
                si_sp[gq2] = v_sl
                emit("scalar", lambda e, b1=b1, sb=gq2 % 2: e.activation(
                    silu[sb][:], psG1[b1][:], AF.Silu).then_inc(sems["sc"], 1))
                wait("vector", "mm", v_m3)
                wait("vector", "sc", v_sl)
                wait("vector", "mm", w2_sp.get(e_ - 1, 0))
                v_gm = inc("ve")
                gm_sp[(e_, hc)] = v_gm
                emit("vector", lambda e, b3=b3, e_=e_, hc=hc, sb=gq2 % 2: e.tensor_mul(
                    g_sp[e_ % 2][:, hc, :], silu[sb][:], psG3[b3][:]).then_inc(sems["ve"], 1))
                gq2 += 1
            if hh == 0 and e_ == 0:
                slot_free[loadix[(10, 0, 0)]] = cur("mm")
                slot_free[loadix[(10, 0, 1)]] = cur("mm")
                loadix[(10, 0, 4)] = emit_load((10, 0, 4), exp_srcs(0)[4])
                loadix[(10, 0, 5)] = emit_load((10, 0, 5), exp_srcs(0)[5])
            elif hh == 0:
                slot_free[loadix[(10, e_, 0)]] = cur("mm")
                slot_free[loadix[(10, e_, 1)]] = cur("mm")
                loadix[(10, e_, 4)] = emit_load((10, e_, 4), exp_srcs(e_)[4])
                loadix[(10, e_, 5)] = emit_load((10, e_, 5), exp_srcs(e_)[5])
        slot_free[loadix[(10, e_, 2)]] = cur("mm")
        slot_free[loadix[(10, e_, 3)]] = cur("mm")
        if e_ + 1 < 16:
            loadix[(10, e_ + 1, 0)] = emit_load((10, e_ + 1, 0), exp_srcs(e_ + 1)[0])
            loadix[(10, e_ + 1, 1)] = emit_load((10, e_ + 1, 1), exp_srcs(e_ + 1)[1])
        v2a, s2a = ld_done[(10, e_, 4)]
        v2b, s2b = ld_done[(10, e_, 5)]
        wait("tensor", "sw%d" % s2a, v2a)
        wait("tensor", "sw%d" % s2b, v2b)
        wait("tensor", "ve", gm_sp[(e_, 7)])
        wt = 0
        for sub in range(4):
            for dc4 in range(4):
                bw = wt % 2
                wait("tensor", "ve", fe_sp.get((e_, wt - 2), fe_sp.get((e_ - 1, wt + 14), 0)))
                for hc in range(8):
                    s2 = s2a if hc < 4 else s2b
                    def sw2(e, s2=s2, hc=hc, sub=sub, dc4=dc4, bw=bw, e_=e_,
                            last=(hc == 7)):
                        rhs = bass.AP(ring[s2], (hc % 4) * 2048 + dc4 * 512,
                                      [[8192, 128], [1, 512]])
                        ins = e.matmul(psW2[bw][:],
                                       g_sp[e_ % 2][:, hc, sub * 128:(sub + 1) * 128],
                                       rhs,
                                       start=(hc == 0), stop=(hc == 7))
                        if last:
                            ins.then_inc(sems["mm"], 1)
                    emit("tensor", sw2)
                v_w2 = inc("mm")
                wait("vector", "mm", v_w2)
                wait("vector", "sgc", gthc[e_])
                wait("vector", "ssc", sc_done.get(e_ - 2, 0))
                v_fe = inc("ve")
                fe_sp[(e_, wt)] = v_fe
                emit("vector", lambda e, e_=e_, sub=sub, dc4=dc4, bw=bw: e.tensor_scalar(
                    msta[e_ % 2][:, sub, dc4 * 512:(dc4 + 1) * 512], psW2[bw][:],
                    cwg[e_ % 2][:, sub, e_:e_ + 1], None,
                    op0=ALU.mult).then_inc(sems["ve"], 1))
                wt += 1
        w2_sp[e_] = cur("mm")
        slot_free[loadix[(10, e_, 4)]] = cur("mm")
        slot_free[loadix[(10, e_, 5)]] = cur("mm")
        if e_ + 1 < 16:
            emit_gathers(e_ + 1)
            loadix[(10, e_ + 1, 2)] = emit_load((10, e_ + 1, 2), exp_srcs(e_ + 1)[2])
            loadix[(10, e_ + 1, 3)] = emit_load((10, e_ + 1, 3), exp_srcs(e_ + 1)[3])
        wait("gpsimd", "ve", fe_sp[(e_, 15)])
        wait("gpsimd", "ssc", sc_done.get(e_ - 1, 0))
        wait("gpsimd", "sac", v_acc_stored)
        for sub in range(4):
            def ssc1(e, e_=e_, sub=sub):
                e.indirect_dma_start(
                    out=d_accd, out_offset=IOA(bidx_sb[:, e_ * 4 + sub:e_ * 4 + sub + 1], 0),
                    in_=msta[e_ % 2][:, sub, :], in_offset=None,
                    compute_op=ALU.add).then_inc(sems["ssc"], 16)
            emit("gpsimd", ssc1)
            inc("ssc", 16)
        sc_done[e_] = cur("ssc")
    v_ssc_all = cur("ssc")

    ctxF3 = ExitStack()
    x2r = [ctxF3.enter_context(nc.sbuf_tensor(f"x2r{i}", [128, C], F32)) for i in range(2)]
    accr = [ctxF3.enter_context(nc.sbuf_tensor(f"accr{i}", [128, C], F32)) for i in range(2)]
    out_done = {}
    for tt in range(NCH):
        wait("sync", "scr", v_scr_all)
        wait("sync", "ssc", v_ssc_all)
        olane = "sout%d" % (tt % 2)
        rlane = "sxr%d" % (tt % 2)
        wait("sync", olane, out_done.get(tt - 2, 0))
        v_xr = inc(rlane, 16)
        emit("sync", lambda e, tt=tt, rlane=rlane: e.dma_start(
            out=x2r[tt % 2][:], in_=d_x2[tt * 128:(tt + 1) * 128, :]).then_inc(sems[rlane], 16))
        wait("sync", rlane, v_xr)
        v_ar = inc(rlane, 16)
        emit("sync", lambda e, tt=tt, rlane=rlane: e.dma_start(
            out=accr[tt % 2][:], in_=d_accd[tt * 128:(tt + 1) * 128, :]).then_inc(sems[rlane], 16))
        wait("vector", rlane, v_ar)
        wait("vector", olane, out_done.get(tt - 2, 0))
        v_fin = inc("ve")
        emit("vector", lambda e, tt=tt: e.tensor_add(
            x2r[tt % 2][:], x2r[tt % 2][:], accr[tt % 2][:]).then_inc(sems["ve"], 1))
        wait("sync", "ve", v_fin)
        out_done[tt] = inc(olane, 16)
        emit("sync", lambda e, tt=tt, olane=olane: e.dma_start(
            out=d_out[tt * 128:(tt + 1) * 128, :], in_=x2r[tt % 2][:]).then_inc(sems[olane], 16))
    wait("sync", "sout0", out_done[6])
    wait("sync", "sout1", out_done[7])
    ctxF3.close()
    ctxF2.close()
    ctxF.close()

    with nc.Block() as block:
        @block.sync
        def _(e):
            for fn in prog["sync"]:
                fn(e)

        @block.gpsimd
        def _(e):
            for fn in prog["gpsimd"]:
                fn(e)

        @block.tensor
        def _(e):
            for fn in prog["tensor"]:
                fn(e)

        @block.vector
        def _(e):
            for fn in prog["vector"]:
                fn(e)

        @block.scalar
        def _(e):
            for fn in prog["scalar"]:
                fn(e)

    master.close()
    return nc


def _rope_tables_f64():
    j = np.arange(64, dtype=np.float64)
    inv = 10000.0 ** (-2.0 * j / 128.0)
    t = np.arange(T, dtype=np.float64)
    ang = t[None, :] * inv[:, None]
    cos = np.cos(ang)
    sin = np.sin(ang)
    Ct = np.concatenate([cos, cos], axis=0)
    St = np.concatenate([-sin, sin], axis=0)
    return Ct, St


def _hi_lo(a64):
    hi = a64.astype(np.float32)
    lo = (a64 - hi.astype(np.float64)).astype(np.float32)
    return np.stack([hi, lo], axis=0)


def kernel(L0_attn_x, L0_attn_Wq, L0_attn_Wk, L0_attn_Wv, L0_attn_Wproj,
           L0_ffn_w_router, L0_ffn_we1, L0_ffn_we3, L0_ffn_we2,
           L0_ffn_ws1, L0_ffn_ws3, L0_ffn_ws2):
    x = np.ascontiguousarray(np.asarray(L0_attn_x, dtype=np.float32))
    weights = {
        "wq": np.ascontiguousarray(np.asarray(L0_attn_Wq, dtype=np.float32)),
        "wk": np.ascontiguousarray(np.asarray(L0_attn_Wk, dtype=np.float32)),
        "wv": np.ascontiguousarray(np.asarray(L0_attn_Wv, dtype=np.float32)),
        "wp": np.ascontiguousarray(np.asarray(L0_attn_Wproj, dtype=np.float32)),
        "wr": np.ascontiguousarray(np.asarray(L0_ffn_w_router, dtype=np.float32)),
        "we1": np.ascontiguousarray(np.asarray(L0_ffn_we1, dtype=np.float32)),
        "we3": np.ascontiguousarray(np.asarray(L0_ffn_we3, dtype=np.float32)),
        "we2": np.ascontiguousarray(np.asarray(L0_ffn_we2, dtype=np.float32)),
        "ws1": np.ascontiguousarray(np.asarray(L0_ffn_ws1, dtype=np.float32)),
        "ws3": np.ascontiguousarray(np.asarray(L0_ffn_ws3, dtype=np.float32)),
        "ws2": np.ascontiguousarray(np.asarray(L0_ffn_ws2, dtype=np.float32)),
    }
    ident = np.eye(128, dtype=ml_dtypes.bfloat16)
    Ct64, St64 = _rope_tables_f64()
    tidc = (np.arange(8)[None, :] * 128 + np.arange(128)[:, None]).astype(np.uint32)
    erowc = np.tile(np.arange(16, dtype=np.float32)[None, :], (128, 1))
    ltuc = np.triu(np.ones((128, 128), np.float32), 1).astype(ml_dtypes.bfloat16)

    in_maps = []
    for core in range(8):
        b, r = core // 4, core % 4
        perm = core_perm(r)
        rowp = np.concatenate([np.arange(p * 128, (p + 1) * 128) for p in perm])
        maskA = np.zeros((128, NCH, 512), np.float32)
        maskB = np.zeros((128, NCH, 4, 128), np.float32)
        for oc in range(NCH):
            blk = own_block(r, oc)
            qpos = blk * 128 + np.arange(128)
            kpos = rowp[oc * 512:(oc + 1) * 512]
            bad = kpos[None, :] > qpos[:, None]
            maskA[:, oc, :] = np.where(bad, NEG, 0.0)
            maskB[:, oc, :, :] = np.where(bad, NEG, 0.0).T.reshape(4, 128, 128).transpose(1, 0, 2)
        m = {
            "xb": np.ascontiguousarray(x[b][rowp]),
            "idb": ident,
            "tidc": tidc,
            "erowc": erowc,
            "ltuc": ltuc,
            "ropeC": np.ascontiguousarray(_hi_lo(Ct64[:, rowp])),
            "ropeS": np.ascontiguousarray(_hi_lo(St64[:, rowp])),
            "maskA": maskA.astype(ml_dtypes.bfloat16),
            "maskB": maskB.astype(ml_dtypes.bfloat16),
        }
        m.update(weights)
        in_maps.append(m)

    nc = build_program()
    res = run_bass_kernel_spmd(nc, in_maps, core_ids=list(range(8)))

    out = np.empty((B, T, C), dtype=np.float32)
    for core in range(8):
        b, r = core // 4, core % 4
        oo = np.asarray(res.results[core]["out_own"])
        for oc in range(NCH):
            blk = own_block(r, oc)
            out[b, blk * 128:(blk + 1) * 128, :] = oo[oc * 128:(oc + 1) * 128, :]
    return out


# version 21
# speedup vs baseline: 1.0064x; 1.0064x over previous
import math
from contextlib import ExitStack

import numpy as np
import ml_dtypes

import concourse.bass as bass
import concourse.mybir as mybir
from concourse.bass_utils import run_bass_kernel_spmd

F32 = mybir.dt.float32
BF16 = mybir.dt.bfloat16
AF = mybir.ActivationFunctionType
ALU = mybir.AluOpType
AX = mybir.AxisListType
U32 = mybir.dt.uint32
I32 = mybir.dt.int32
IOA = bass.IndirectOffsetOnAxis

B, T, C = 2, 4096, 2048
NB = T // 128
NCH = T // 512
TOPK = 4
EH = 1024
NEXP = 20
SCALE = 1.0 / math.sqrt(128.0)
EPS = 1e-5
NEG = -1e30
KiB = 1024

ENGINES = ("sync", "gpsimd", "tensor", "vector", "scalar")
DBG = False


def own_block(r, c):
    return 4 * c + r if c < 4 else 4 * c + 3 - r


def core_perm(r):
    out = []
    for c in range(NCH):
        blk = own_block(r, c)
        out.extend([blk] + [b for b in range(4 * c, 4 * c + 4) if b != blk])
    return out


def build_program():
    nc = bass.Bass("TRN2", target_bir_lowering=False, debug=False,
                   detect_race_conditions=False)

    d_x = nc.dram_tensor("xb", [T, C], F32, kind="ExternalInput").ap()
    d_wq = nc.dram_tensor("wq", [C, C], F32, kind="ExternalInput").ap()
    d_wk = nc.dram_tensor("wk", [C, 512], F32, kind="ExternalInput").ap()
    d_wv = nc.dram_tensor("wv", [C, 512], F32, kind="ExternalInput").ap()
    d_wp = nc.dram_tensor("wp", [C, C], F32, kind="ExternalInput").ap()
    d_wr = nc.dram_tensor("wr", [C, 16], F32, kind="ExternalInput").ap()
    d_we1 = nc.dram_tensor("we1", [16, C, EH], F32, kind="ExternalInput").ap()
    d_we3 = nc.dram_tensor("we3", [16, C, EH], F32, kind="ExternalInput").ap()
    d_we2 = nc.dram_tensor("we2", [16, EH, C], F32, kind="ExternalInput").ap()
    d_ws1 = nc.dram_tensor("ws1", [C, 4096], F32, kind="ExternalInput").ap()
    d_ws3 = nc.dram_tensor("ws3", [C, 4096], F32, kind="ExternalInput").ap()
    d_ws2 = nc.dram_tensor("ws2", [4096, C], F32, kind="ExternalInput").ap()
    d_idb = nc.dram_tensor("idb", [128, 128], BF16, kind="ExternalInput").ap()
    d_ropeC = nc.dram_tensor("ropeC", [2, 128, T], F32, kind="ExternalInput").ap()
    d_ropeS = nc.dram_tensor("ropeS", [2, 128, T], F32, kind="ExternalInput").ap()
    d_maskA = nc.dram_tensor("maskA", [128, NCH, 512], BF16, kind="ExternalInput").ap()
    d_maskB = nc.dram_tensor("maskB", [128, NCH, 4, 128], BF16, kind="ExternalInput").ap()
    d_out = nc.dram_tensor("out_own", [1024, C], F32, kind="ExternalOutput").ap()
    d_ln2 = nc.dram_tensor("ln2scr", [1025, C], BF16, kind="Internal").ap()
    d_cwsc = nc.dram_tensor("cwscr", [1025, 64], F32, kind="Internal").ap()
    d_bidx = nc.dram_tensor("bidxscr", [8192, 1], U32, kind="Internal").ap()
    d_accd = nc.dram_tensor("accscr", [1025, C], F32, kind="Internal").ap()
    d_tid = nc.dram_tensor("tidc", [128, 8], U32, kind="ExternalInput").ap()
    d_erow = nc.dram_tensor("erowc", [128, 16], F32, kind="ExternalInput").ap()
    d_ltu = nc.dram_tensor("ltuc", [128, 128], BF16, kind="ExternalInput").ap()
    ddbg = {}
    if DBG:
        for nm, shp in (("xfm_own", [128, 16, 1024]), ("k_sb", [128, 4, T]),
                        ("v_sb", [128, NB, 4, 129]), ("q_sb", [128, 16, 1024]),
                        ("yfm", [128, NCH, 16, 128]),
                        ("xfm2", [128, 16, 1024])):
            dt = BF16
            ddbg[nm] = nc.dram_tensor("dbg_" + nm, shp, dt, kind="ExternalOutput").ap()
        for nm in ("x2f", "acc"):
            ddbg[nm] = nc.dram_tensor("dbg_" + nm, [128, NCH, C], F32, kind="ExternalOutput").ap()
        ddbg["cw"] = nc.dram_tensor("dbg_cw", [128, NCH, 16], F32, kind="ExternalOutput").ap()
    d_x2 = nc.dram_tensor("x2scr", [1024, C], F32, kind="Internal").ap()

    prog = {e: [] for e in ENGINES}
    S = {}
    last_wait = {e: {} for e in ENGINES}
    master = ExitStack()
    sems = {}
    for name in ("mm", "ve", "sc", "scr", "bar", "fl", "sid", "srope", "smk",
                 "sbx", "sgt", "sgc", "ssc", "sac",
                 "sx0", "sx1", "sxo0", "sxo1", "sxr0", "sxr1", "sout0", "sout1",
                 "sws", "swq0", "swq1", "swp0", "swp1", "sw0", "sw1", "sw2", "sw3"):
        sems[name] = master.enter_context(nc.semaphore(name))

    def emit(eng, fn):
        prog[eng].append(fn)

    def inc(sem, amt=1):
        S[sem] = S.get(sem, 0) + amt
        return S[sem]

    def cur(sem):
        return S.get(sem, 0)

    def wait(eng, sem, val):
        if not val or val <= 0:
            return
        assert val <= S.get(sem, 0), ("wait-ahead", eng, sem, val, S.get(sem, 0))
        if last_wait[eng].get(sem, 0) >= val:
            return
        last_wait[eng][sem] = val
        h = sems[sem]
        emit(eng, lambda e, h=h, v=val: e.wait_ge(h, v))

    def barrier():
        h = sems["bar"]
        for eng in ENGINES:
            inc("bar")
            emit(eng, lambda e, h=h: e.sem_inc(h, 1))
        t = cur("bar")
        for eng in ENGINES:
            wait(eng, "bar", t)

    assert nc.sbuf_top >= 229344
    k_sb = nc.alloc_sbuf_tensor_at("k_sb", [128, 4, T], BF16, offset=196576)
    v_sb = nc.alloc_sbuf_tensor_at("v_sb", [128, NB, 4, 129], BF16, offset=163552)
    q_sb = nc.alloc_sbuf_tensor_at("q_sb", [128, 16, 1024], BF16, offset=130784)
    yfm = nc.alloc_sbuf_tensor_at("yfm", [128, NCH, 16, 128], BF16, offset=98016)
    x2f = nc.alloc_sbuf_tensor_at("x2f", [128, NCH, C], F32, offset=130784)
    xfm2 = nc.alloc_sbuf_tensor_at("xfm2", [128, 16, 1024], BF16, offset=196576)
    acc = nc.alloc_sbuf_tensor_at("acc", [128, NCH, C], F32, offset=130784)

    identb = nc.alloc_sbuf_tensor("identb", [128, 128], BF16)
    epsb = nc.alloc_sbuf_tensor("epsb", [128, 1], F32)
    onesb = nc.alloc_sbuf_tensor("onesb", [65, 128], BF16)
    v_id = inc("sid", 16)
    emit("sync", lambda e: e.dma_start(out=identb[:], in_=d_idb).then_inc(sems["sid"], 16))
    emit("vector", lambda e: e.memset(onesb[:], 1.0))
    v_eps = inc("ve")
    emit("vector", lambda e: e.memset(epsb[:], float(EPS)).then_inc(sems["ve"], 1))
    wait("scalar", "ve", v_eps)
    emit("vector", lambda e: e.memset(v_sb[:, :, :, 128:129], 1.0))

    ctxAB = ExitStack()
    ropeC = ctxAB.enter_context(nc.sbuf_tensor("ropeCt", [128, T], BF16))
    ropeS = ctxAB.enter_context(nc.sbuf_tensor("ropeSt", [128, T], BF16))
    roptmp = ctxAB.enter_context(nc.sbuf_tensor("roptmp", [128, 512], F32))
    roptmp2 = ctxAB.enter_context(nc.sbuf_tensor("roptmp2", [128, 512], F32))
    xfm_own = ctxAB.enter_context(nc.sbuf_tensor("xfm_own", [128, 16, 1024], BF16))

    ctx1a = ExitStack()
    ropestg = ctx1a.enter_context(nc.sbuf_tensor("ropestg", [128, 2, T], F32))
    v_rope_add = 0
    for (tbl, dst) in ((d_ropeC, ropeC), (d_ropeS, ropeS)):
        wait("sync", "ve", v_rope_add)
        v_l = inc("srope", 16)
        emit("sync", lambda e, t=tbl: e.dma_start(
            out=ropestg[:], in_=t.rearrange("h p t -> p h t")).then_inc(sems["srope"], 16))
        wait("vector", "srope", v_l)
        v_rope_add = inc("ve")
        emit("vector", lambda e, d=dst: e.tensor_add(
            d[:], ropestg[:, 0, :], ropestg[:, 1, :]).then_inc(sems["ve"], 1))
    ctx1a.close()

    ctx1a = ExitStack()
    xstage = [ctx1a.enter_context(nc.sbuf_tensor(f"xstage{i}", [128, C], F32)) for i in range(2)]
    lnstage = [ctx1a.enter_context(nc.sbuf_tensor(f"lnstage{i}", [128, C], BF16)) for i in range(2)]
    xfm = [ctx1a.enter_context(nc.sbuf_tensor(f"xfm{i}", [128, 16, 512], BF16)) for i in range(2)]
    wk_sb = ctx1a.enter_context(nc.sbuf_tensor("wk_sb", [128, 16, 512], BF16))
    wv_sb = ctx1a.enter_context(nc.sbuf_tensor("wv_sb", [128, 16, 512], BF16))
    st1 = ctx1a.enter_context(nc.sbuf_tensor("st1", [128, 2, 8], F32))
    psT = [ctx1a.enter_context(nc.psum_tensor(f"psT{i}", [128, 4, 128], BF16)) for i in range(4)]
    psK = [ctx1a.enter_context(nc.psum_tensor(f"psK{i}", [128, 512], F32)) for i in range(2)]
    psV = [ctx1a.enter_context(nc.psum_tensor(f"psV{i}", [128, 512], F32)) for i in range(2)]

    w_kv = []
    for (d_w, sb) in ((d_wk, wk_sb), (d_wv, wv_sb)):
        wait("gpsimd", "sws", cur("sws"))
        v = inc("sws", 16)
        w_kv.append(v)
        emit("gpsimd", lambda e, d=d_w, s=sb: e.dma_start(
            out=s[:], in_=d.rearrange("(dc p) f -> p dc f", p=128)).then_inc(sems["sws"], 16))

    ln_done, tp_done, evac_done = {}, {}, {}
    kv_mm_done, rope_done, vevac_done = {}, {}, {}

    for c in range(NCH):
        cb = c % 2
        for t in range(4):
            gt = 4 * c + t
            wait("sync", "ve", v_rope_add if gt == 0 else 0)
            wait("sync", "ve", ln_done.get(gt - 2, 0))
            lane = "sx%d" % (gt % 2)
            v_x = inc(lane, 16)
            emit("sync", lambda e, gt=gt, lane=lane: e.dma_start(
                out=xstage[gt % 2][:],
                in_=d_x[gt * 128:(gt + 1) * 128, :]).then_inc(sems[lane], 16))
            wait("vector", lane, v_x)
            v_rs = inc("ve")
            emit("vector", lambda e, gt=gt: e.reduce_sum(
                st1[:, gt % 2, 0:1], xstage[gt % 2][:], axis=AX.X).then_inc(sems["ve"], 1))
            wait("scalar", lane, v_x)
            wait("scalar", "ve", ln_done.get(gt - 2, 0))
            wait("scalar", "mm", tp_done.get(gt - 2, 0))
            v_sq = inc("sc")
            emit("scalar", lambda e, gt=gt: e.activation(
                lnstage[gt % 2][:], xstage[gt % 2][:], AF.Square,
                accum_out=st1[:, gt % 2, 1:2]).then_inc(sems["sc"], 1))
            wait("vector", "sc", v_sq)
            wait("vector", "ve", v_rs)
            v_mean = inc("ve")
            emit("vector", lambda e, gt=gt: e.tensor_scalar(
                st1[:, gt % 2, 2:3], st1[:, gt % 2, 0:1], 1.0 / C, None,
                op0=ALU.mult).then_inc(sems["ve"], 1))
            wait("vector", "ve", v_mean)
            v_msq = inc("ve")
            emit("vector", lambda e, gt=gt: e.tensor_mul(
                st1[:, gt % 2, 3:4], st1[:, gt % 2, 2:3],
                st1[:, gt % 2, 2:3]).then_inc(sems["ve"], 1))
            wait("vector", "ve", v_msq)
            v_var = inc("ve")
            emit("vector", lambda e, gt=gt: e.scalar_tensor_tensor(
                st1[:, gt % 2, 4:5], st1[:, gt % 2, 1:2], 1.0 / C, st1[:, gt % 2, 3:4],
                op0=ALU.mult, op1=ALU.subtract).then_inc(sems["ve"], 1))
            wait("scalar", "ve", v_var)
            v_sd = inc("sc")
            emit("scalar", lambda e, gt=gt: e.activation(
                st1[:, gt % 2, 5:6], st1[:, gt % 2, 4:5], AF.Sqrt,
                bias=epsb[:, 0:1]).then_inc(sems["sc"], 1))
            wait("vector", "sc", v_sd)
            v_rc2 = inc("ve")
            emit("vector", lambda e, gt=gt: e.reciprocal(
                st1[:, gt % 2, 6:7], st1[:, gt % 2, 5:6]).then_inc(sems["ve"], 1))
            wait("vector", "mm", tp_done.get(gt - 2, 0))
            wait("vector", "ve", v_rc2)
            v_ln = inc("ve")
            ln_done[gt] = v_ln
            emit("vector", lambda e, gt=gt: e.tensor_scalar(
                lnstage[gt % 2][:], xstage[gt % 2][:], st1[:, gt % 2, 2:3], st1[:, gt % 2, 6:7],
                op0=ALU.subtract, op1=ALU.mult).then_inc(sems["ve"], 1))
            wait("tensor", "sid", v_id)
            wait("tensor", "ve", v_ln)
            for g4 in range(4):
                gi = gt * 4 + g4
                par = gi % 4
                wait("tensor", "ve", evac_done.get(gi - 4, 0))
                for j in range(4):
                    dc = g4 * 4 + j
                    def tp(e, par=par, j=j, dc=dc, gt=gt, last=(j == 3)):
                        ins = e.transpose(psT[par][:, j, :],
                                          lnstage[gt % 2][:, dc * 128:(dc + 1) * 128], identb[:])
                        if last:
                            ins.then_inc(sems["mm"], 1)
                    emit("tensor", tp)
                v_tp = inc("mm")
                wait("vector", "mm", v_tp)
                wait("vector", "mm", kv_mm_done.get(c - 2, 0))
                v_ev = inc("ve")
                evac_done[gi] = v_ev
                emit("vector", lambda e, cb=cb, g4=g4, t=t, par=par: e.tensor_copy(
                    xfm[cb][:, g4 * 4:(g4 + 1) * 4, t * 128:(t + 1) * 128],
                    psT[par][:]).then_inc(sems["ve"], 1))
            tp_done[gt] = cur("mm")
        chunk_evac = evac_done[(4 * c + 3) * 4 + 3]
        wait("vector", "ve", evac_done[(4 * c) * 4 + 3])
        emit("vector", lambda e, cb=cb, c=c: e.tensor_copy(
            xfm_own[:, :, c * 128:(c + 1) * 128], xfm[cb][:, :, 0:128]))
        wait("tensor", "sws", w_kv[0])
        wait("tensor", "ve", chunk_evac)
        for fc in range(4):
            prev = (c, fc - 2) if fc >= 2 else (c - 1, fc + 2)
            lastro = (c, fc - 1) if fc >= 1 else (c - 1, 3)
            wait("tensor", "ve", rope_done.get(prev, 0))
            for dc in range(16):
                def kmm(e, cb=cb, fc=fc, dc=dc):
                    ins = e.matmul(psK[fc % 2][:],
                                   wk_sb[:, dc, fc * 128:(fc + 1) * 128],
                                   xfm[cb][:, dc, :],
                                   start=(dc == 0), stop=(dc == 15))
                    if dc == 15:
                        ins.then_inc(sems["mm"], 1)
                emit("tensor", kmm)
            v_kmm = inc("mm")
            wait("scalar", "mm", v_kmm)
            wait("scalar", "ve", rope_done.get(lastro, 0))
            v_rc = inc("sc")
            emit("scalar", lambda e, fc=fc: e.copy(roptmp[0:64, :], psK[fc % 2][64:128, :]))
            emit("scalar", lambda e, fc=fc: e.copy(
                roptmp[64:128, :], psK[fc % 2][0:64, :]).then_inc(sems["sc"], 1))
            wait("vector", "mm", v_kmm)
            wait("vector", "sc", v_rc)
            emit("vector", lambda e, c=c, fc=fc: e.tensor_mul(
                roptmp2[:], psK[fc % 2][:], ropeC[:, c * 512:(c + 1) * 512]))
            v_rm = inc("ve")
            emit("vector", lambda e, c=c: e.tensor_mul(
                roptmp[:], roptmp[:], ropeS[:, c * 512:(c + 1) * 512]).then_inc(sems["ve"], 1))
            wait("vector", "ve", v_rm)
            v_rp = inc("ve")
            rope_done[(c, fc)] = v_rp
            emit("vector", lambda e, c=c, fc=fc: e.tensor_add(
                k_sb[:, fc, c * 512:(c + 1) * 512],
                roptmp2[:], roptmp[:]).then_inc(sems["ve"], 1))
        wait("tensor", "sws", w_kv[1])
        for tt in range(4):
            prev = (c, tt - 2) if tt >= 2 else (c - 1, tt + 2)
            wait("tensor", "ve", vevac_done.get(prev, 0))
            for dc in range(16):
                def vmm(e, cb=cb, tt=tt, dc=dc):
                    ins = e.matmul(psV[tt % 2][:],
                                   xfm[cb][:, dc, tt * 128:(tt + 1) * 128],
                                   wv_sb[:, dc, :],
                                   start=(dc == 0), stop=(dc == 15))
                    if dc == 15:
                        ins.then_inc(sems["mm"], 1)
                emit("tensor", vmm)
            v_vmm = inc("mm")
            wait("vector", "mm", v_vmm)
            v_vev = inc("ve")
            vevac_done[(c, tt)] = v_vev
            emit("vector", lambda e, c=c, tt=tt: e.tensor_copy(
                v_sb[:, c * 4 + tt, :, 0:128],
                psV[tt % 2][:].rearrange("p (h d) -> p h d", h=4)).then_inc(sems["ve"], 1))
        kv_mm_done[c] = cur("mm")
    v_1a_ve = cur("ve")
    v_1a_sc = cur("sc")
    ctx1a.close()

    ctx1b = ExitStack()
    wqring = [ctx1b.enter_context(nc.sbuf_tensor(f"wqring{i}", [128, 16, 512], BF16)) for i in range(2)]
    psQ = [ctx1b.enter_context(nc.psum_tensor(f"psQ{i}", [128, 512], F32)) for i in range(4)]
    q_mm_done, qrope_done = {}, {}

    def emit_wq_load(fb):
        wait("gpsimd", "mm", q_mm_done.get(fb - 2, 0))
        lane = "swq%d" % (fb % 2)
        v = inc(lane, 16)
        emit("gpsimd", lambda e, fb=fb, lane=lane: e.dma_start(
            out=wqring[fb % 2][:],
            in_=d_wq[:, fb * 512:(fb + 1) * 512].rearrange("(dc p) f -> p dc f", p=128)
        ).then_inc(sems[lane], 16))
        return v

    wait("gpsimd", "ve", v_1a_ve)
    wait("gpsimd", "sc", v_1a_sc)
    wait("gpsimd", "mm", kv_mm_done[NCH - 1])
    wait("gpsimd", "sx0", cur("sx0"))
    wait("gpsimd", "sx1", cur("sx1"))
    wq_ld = {0: emit_wq_load(0), 1: emit_wq_load(1)}
    for fb in range(4):
        wait("tensor", "swq%d" % (fb % 2), wq_ld[fb])
        for fcc in range(4):
            fc = fb * 4 + fcc
            for tc in range(2):
                u = fc * 2 + tc
                wait("tensor", "ve", v_1a_ve)
                wait("tensor", "sc", v_1a_sc)
                wait("tensor", "ve", qrope_done.get(u - 4, 0))
                for dc in range(16):
                    def qmm(e, fb=fb, fcc=fcc, tc=tc, dc=dc, u=u):
                        ins = e.matmul(psQ[u % 4][:],
                                       wqring[fb % 2][:, dc, fcc * 128:(fcc + 1) * 128],
                                       xfm_own[:, dc, tc * 512:(tc + 1) * 512],
                                       start=(dc == 0), stop=(dc == 15))
                        if dc == 15:
                            ins.then_inc(sems["mm"], 1)
                    emit("tensor", qmm)
                v_qmm = inc("mm")
                wait("scalar", "mm", v_qmm)
                wait("scalar", "ve", qrope_done.get(u - 1, 0))
                v_rc = inc("sc")
                emit("scalar", lambda e, u=u: e.copy(roptmp[0:64, :], psQ[u % 4][64:128, :]))
                emit("scalar", lambda e, u=u: e.copy(
                    roptmp[64:128, :], psQ[u % 4][0:64, :]).then_inc(sems["sc"], 1))
                wait("vector", "mm", v_qmm)
                wait("vector", "sc", v_rc)

                def ropeap(tb, tc=tc):
                    return bass.AP(tb, tc * 2048, [[T, 128], [512, 4], [1, 128]])
                emit("vector", lambda e, u=u, ap=ropeap(ropeC): e.tensor_mul(
                    roptmp2[:], psQ[u % 4][:], ap))
                v_rm = inc("ve")
                emit("vector", lambda e, ap=ropeap(ropeS): e.tensor_mul(
                    roptmp[:], roptmp[:], ap).then_inc(sems["ve"], 1))
                wait("vector", "ve", v_rm)
                v_rp = inc("ve")
                qrope_done[u] = v_rp
                emit("vector", lambda e, fc=fc, tc=tc: e.tensor_add(
                    q_sb[:, fc, tc * 512:(tc + 1) * 512],
                    roptmp2[:], roptmp[:]).then_inc(sems["ve"], 1))
        q_mm_done[fb] = cur("mm")
        if fb + 2 < 4:
            wq_ld[fb + 2] = emit_wq_load(fb + 2)
    barrier()
    if DBG:
        inc("scr", 16)
        emit("sync", lambda e: e.dma_start(out=ddbg["xfm_own"], in_=xfm_own[:]).then_inc(sems["scr"], 16))
    if DBG:
        inc("scr", 16)
        emit("sync", lambda e: e.dma_start(out=ddbg["k_sb"], in_=k_sb[:]).then_inc(sems["scr"], 16))
    if DBG:
        inc("scr", 16)
        emit("sync", lambda e: e.dma_start(out=ddbg["v_sb"], in_=v_sb[:]).then_inc(sems["scr"], 16))
    if DBG:
        inc("scr", 16)
        emit("sync", lambda e: e.dma_start(out=ddbg["q_sb"], in_=q_sb[:]).then_inc(sems["scr"], 16))
    ctx1b.close()
    ctxAB.close()

    ctxW = ExitStack()
    wpring = [ctxW.enter_context(nc.sbuf_tensor(f"wpring{i}", [128, 16, 512], BF16)) for i in range(2)]
    ctxC = ExitStack()
    maskA = ctxC.enter_context(nc.sbuf_tensor("maskA_s", [128, NCH, 512], BF16))
    maskB = ctxC.enter_context(nc.sbuf_tensor("maskB_s", [128, NCH, 4, 128], BF16))
    pt_sb = [ctxC.enter_context(nc.sbuf_tensor(f"pt{i}", [128, NB, 128], BF16)) for i in range(2)]
    y_sb = [ctxC.enter_context(nc.sbuf_tensor(f"ysb{i}", [128, C], BF16)) for i in range(2)]
    rinvb = ctxC.enter_context(nc.sbuf_tensor("rinvb", [128, 1], F32))

    v_mA = inc("smk", 16)
    emit("sync", lambda e: e.dma_start(out=maskA[:], in_=d_maskA).then_inc(sems["smk"], 16))
    wait("sync", "smk", v_mA)
    v_mB = inc("smk", 16)
    emit("sync", lambda e: e.dma_start(out=maskB[:], in_=d_maskB).then_inc(sems["smk"], 16))


    ctxCB = ExitStack()
    psB = [ctxCB.enter_context(nc.psum_tensor(f"psB{i}", [128, 512], F32)) for i in range(4)]
    psY = [ctxCB.enter_context(nc.psum_tensor(f"psY{i}", [128, 129], F32)) for i in range(2)]
    psYT = [ctxCB.enter_context(nc.psum_tensor(f"psYT{i}", [128, 4, 128], BF16)) for i in range(2)]

    wp_mm_done, x2_done = {}, {}

    def emit_wp_load(fb):
        wait("gpsimd", "mm", wp_mm_done.get(fb - 2, 0))
        lane = "swp%d" % (fb % 2)
        v = inc(lane, 16)
        emit("gpsimd", lambda e, fb=fb, lane=lane: e.dma_start(
            out=wpring[fb % 2][:],
            in_=d_wp[:, fb * 512:(fb + 1) * 512].rearrange("(dc p) f -> p dc f", p=128)
        ).then_inc(sems[lane], 16))
        return v

    wp_ld = {0: emit_wp_load(0), 1: emit_wp_load(1)}
    wait("vector", "smk", v_mB)
    seqB = 0
    exp_done, yev_done, av_done, ytr_done = {}, {}, {}, {}
    ytr_last = {}
    for oc in range(NCH):
        for h in range(16):
            u = oc * 16 + h
            nk = oc + 1
            wait("scalar", "mm", av_done.get(u - 2, 0))
            for kc in range(nk):
                bslot = seqB % 4
                wait("tensor", "sc", exp_done.get(seqB - 4, 0))
                for j in range(4):
                    def bmm(e, bslot=bslot, h=h, oc=oc, kc=kc, j=j):
                        ins = e.matmul(psB[bslot][:, j * 128:(j + 1) * 128],
                                       k_sb[:, h // 4, (kc * 4 + j) * 128:(kc * 4 + j + 1) * 128],
                                       q_sb[:, h, oc * 128:(oc + 1) * 128],
                                       start=(j == 0), stop=(j == 3))
                        if j == 3:
                            ins.then_inc(sems["mm"], 1)
                    emit("tensor", bmm)
                v_bmm = inc("mm")
                if kc == nk - 1:
                    wait("vector", "mm", v_bmm)
                    v_msk = inc("ve")
                    emit("vector", lambda e, bslot=bslot, oc=oc: e.tensor_add(
                        psB[bslot][:], psB[bslot][:],
                        maskB[:, oc, :, :]).then_inc(sems["ve"], 1))
                    wait("scalar", "ve", v_msk)
                else:
                    wait("scalar", "mm", v_bmm)
                v_exp = inc("sc")
                exp_done[seqB] = v_exp
                emit("scalar", lambda e, bslot=bslot, kc=kc, ub=u % 2: e.activation(
                    pt_sb[ub][:, kc * 4:(kc + 1) * 4, :], psB[bslot][:], AF.Exp,
                    scale=float(SCALE)).then_inc(sems["sc"], 1))
                seqB += 1
            wait("tensor", "ve", yev_done.get(u - 2, 0))
            if nk > 1:
                wait("tensor", "sc", exp_done[seqB - 2])
            for j in range(4 * (nk - 1)):
                def ymm1(e, u=u, j=j, h=h):
                    e.matmul(psY[u % 2][:],
                             pt_sb[u % 2][:, j, :],
                             v_sb[:, j, h // 4, :],
                             start=(j == 0), stop=False)
                emit("tensor", ymm1)
            wait("tensor", "sc", exp_done[seqB - 1])
            for j in range(4 * (nk - 1), 4 * nk):
                def ymm(e, u=u, j=j, h=h, last=(j == 4 * nk - 1), first=(j == 0)):
                    ins = e.matmul(psY[u % 2][:],
                                   pt_sb[u % 2][:, j, :],
                                   v_sb[:, j, h // 4, :],
                                   start=first, stop=last)
                    if last:
                        ins.then_inc(sems["mm"], 1)
                emit("tensor", ymm)
            v_ymm = inc("mm")
            av_done[u] = v_ymm
            wait("vector", "mm", v_ymm)
            wait("vector", "mm", ytr_last.get(oc - 2, 0))
            v_ri = inc("ve")
            emit("vector", lambda e, u=u: e.reciprocal(
                rinvb[:], psY[u % 2][:, 128:129]).then_inc(sems["ve"], 1))
            wait("vector", "ve", v_ri)
            v_ye = inc("ve")
            yev_done[u] = v_ye
            emit("vector", lambda e, u=u, oc=oc, h=h: e.tensor_scalar(
                y_sb[oc % 2][:, h * 128:(h + 1) * 128], psY[u % 2][:, 0:128],
                rinvb[:, 0:1], None, op0=ALU.mult).then_inc(sems["ve"], 1))
        wait("tensor", "ve", yev_done[oc * 16 + 15])
        for g4 in range(4):
            gi = oc * 4 + g4
            par = gi % 2
            wait("tensor", "ve", ytr_done.get(gi - 2, 0))
            for j in range(4):
                cc = g4 * 4 + j
                def ytp(e, par=par, oc=oc, cc=cc, j=j, last=(j == 3)):
                    ins = e.transpose(psYT[par][:, j, :],
                                      y_sb[oc % 2][:, cc * 128:(cc + 1) * 128], identb[:])
                    if last:
                        ins.then_inc(sems["mm"], 1)
                emit("tensor", ytp)
            v_yt = inc("mm")
            wait("vector", "mm", v_yt)
            v_yte = inc("ve")
            ytr_done[gi] = v_yte
            emit("vector", lambda e, oc=oc, g4=g4, par=par: e.tensor_copy(
                yfm[:, oc, g4 * 4:(g4 + 1) * 4, :], psYT[par][:]).then_inc(sems["ve"], 1))
        ytr_last[oc] = cur("mm")
    barrier()
    if DBG:
        inc("scr", 16)
        emit("sync", lambda e: e.dma_start(out=ddbg["yfm"], in_=yfm[:]).then_inc(sems["scr"], 16))
    ctxCB.close()
    ctxC.close()

    ctxD = ExitStack()
    xorig = [ctxD.enter_context(nc.sbuf_tensor(f"xorig{i}", [128, 512], F32)) for i in range(2)]
    psW = [ctxD.enter_context(nc.psum_tensor(f"psW{i}", [128, 512], F32)) for i in range(4)]

    for fb in range(4):
        wait("tensor", "swp%d" % (fb % 2), wp_ld[fb])
        for oc in range(NCH):
            k = fb * NCH + oc
            wait("sync", "ve", x2_done.get(k - 2, 0))
            lane = "sxo%d" % (k % 2)
            v_xo = inc(lane, 16)
            emit("sync", lambda e, oc=oc, fb=fb, k=k, lane=lane: e.dma_start(
                out=xorig[k % 2][:],
                in_=d_x[oc * 512:oc * 512 + 128, fb * 512:(fb + 1) * 512]
            ).then_inc(sems[lane], 16))
            wait("tensor", "ve", x2_done.get(k - 4, 0))
            for cc in range(16):
                def pmm(e, fb=fb, oc=oc, cc=cc, k=k):
                    ins = e.matmul(psW[k % 4][:],
                                   yfm[:, oc, cc, :],
                                   wpring[fb % 2][:, cc, :],
                                   start=(cc == 0), stop=(cc == 15))
                    if cc == 15:
                        ins.then_inc(sems["mm"], 1)
                emit("tensor", pmm)
            v_pmm = inc("mm")
            wait("vector", "mm", v_pmm)
            wait("vector", lane, v_xo)
            v_x2 = inc("ve")
            x2_done[k] = v_x2
            emit("vector", lambda e, oc=oc, fb=fb, k=k: e.tensor_add(
                x2f[:, oc, fb * 512:(fb + 1) * 512], psW[k % 4][:],
                xorig[k % 2][:]).then_inc(sems["ve"], 1))
        wp_mm_done[fb] = cur("mm")
        if fb + 2 < 4:
            wp_ld[fb + 2] = emit_wp_load(fb + 2)
    barrier()
    if DBG:
        inc("scr", 16)
        emit("sync", lambda e: e.dma_start(out=ddbg["x2f"], in_=x2f[:]).then_inc(sems["scr"], 16))
    ctxD.close()
    ctxW.close()

    ctxE = ExitStack()
    ln2stage = ctxE.enter_context(nc.sbuf_tensor("ln2stage", [128, C], BF16))
    st2 = ctxE.enter_context(nc.sbuf_tensor("st2", [128, 2, 8], F32))
    psT2 = [ctxE.enter_context(nc.psum_tensor(f"psT2{i}", [128, 4, 128], BF16)) for i in range(4)]

    ln2_done, tp2_done, ev2_done = {}, {}, {}
    for oc in range(NCH):
        v_rs2 = inc("ve")
        emit("vector", lambda e, oc=oc: e.reduce_sum(
            st2[:, oc % 2, 0:1], x2f[:, oc, :], axis=AX.X).then_inc(sems["ve"], 1))
        wait("scalar", "ve", ln2_done.get(oc - 2, 0))
        wait("scalar", "mm", tp2_done.get(oc - 1, 0))
        wait("scalar", "sac", (oc) * 16)
        v_sq = inc("sc")
        emit("scalar", lambda e, oc=oc: e.activation(
            ln2stage[:], x2f[:, oc, :], AF.Square,
            accum_out=st2[:, oc % 2, 1:2]).then_inc(sems["sc"], 1))
        wait("vector", "sc", v_sq)
        wait("vector", "ve", v_rs2)
        v_mean = inc("ve")
        emit("vector", lambda e, oc=oc: e.tensor_scalar(
            st2[:, oc % 2, 2:3], st2[:, oc % 2, 0:1], 1.0 / C, None,
            op0=ALU.mult).then_inc(sems["ve"], 1))
        wait("vector", "ve", v_mean)
        v_msq = inc("ve")
        emit("vector", lambda e, oc=oc: e.tensor_mul(
            st2[:, oc % 2, 3:4], st2[:, oc % 2, 2:3],
            st2[:, oc % 2, 2:3]).then_inc(sems["ve"], 1))
        wait("vector", "ve", v_msq)
        v_var = inc("ve")
        emit("vector", lambda e, oc=oc: e.scalar_tensor_tensor(
            st2[:, oc % 2, 4:5], st2[:, oc % 2, 1:2], 1.0 / C, st2[:, oc % 2, 3:4],
            op0=ALU.mult, op1=ALU.subtract).then_inc(sems["ve"], 1))
        wait("scalar", "ve", v_var)
        v_sd = inc("sc")
        emit("scalar", lambda e, oc=oc: e.activation(
            st2[:, oc % 2, 5:6], st2[:, oc % 2, 4:5], AF.Sqrt,
            bias=epsb[:, 0:1]).then_inc(sems["sc"], 1))
        wait("vector", "sc", v_sd)
        v_rc2 = inc("ve")
        emit("vector", lambda e, oc=oc: e.reciprocal(
            st2[:, oc % 2, 6:7], st2[:, oc % 2, 5:6]).then_inc(sems["ve"], 1))
        wait("vector", "mm", tp2_done.get(oc - 1, 0))
        wait("vector", "ve", v_rc2)
        v_ln = inc("ve")
        ln2_done[oc] = v_ln
        emit("vector", lambda e, oc=oc: e.tensor_scalar(
            ln2stage[:], x2f[:, oc, :], st2[:, oc % 2, 2:3], st2[:, oc % 2, 6:7],
            op0=ALU.subtract, op1=ALU.mult).then_inc(sems["ve"], 1))
        wait("sync", "ve", v_ln)
        inc("scr", 16)
        emit("sync", lambda e, oc=oc: e.dma_start(
            out=d_x2[oc * 128:(oc + 1) * 128, :], in_=x2f[:, oc, :]).then_inc(sems["scr"], 16))
        wait("sync", "mm", tp2_done.get(oc, 0) if False else 0)
        inc("sac", 16)
        emit("sync", lambda e, oc=oc: e.dma_start(
            out=d_ln2[oc * 128:(oc + 1) * 128, :], in_=ln2stage[:]).then_inc(sems["sac"], 16))
        wait("tensor", "ve", v_ln)
        for g4 in range(4):
            gi = oc * 4 + g4
            par = gi % 4
            wait("tensor", "ve", ev2_done.get(gi - 4, 0))
            for j in range(4):
                dc = g4 * 4 + j
                def tp2(e, par=par, dc=dc, j=j, last=(j == 3)):
                    ins = e.transpose(psT2[par][:, j, :],
                                      ln2stage[:, dc * 128:(dc + 1) * 128], identb[:])
                    if last:
                        ins.then_inc(sems["mm"], 1)
                emit("tensor", tp2)
            v_tp = inc("mm")
            wait("vector", "mm", v_tp)
            v_ev = inc("ve")
            ev2_done[gi] = v_ev
            emit("vector", lambda e, oc=oc, g4=g4, par=par: e.tensor_copy(
                xfm2[:, g4 * 4:(g4 + 1) * 4, oc * 128:(oc + 1) * 128],
                psT2[par][:]).then_inc(sems["ve"], 1))
        tp2_done[oc] = cur("mm")
    v_scr_all = cur("scr")
    wait("sync", "scr", v_scr_all)
    barrier()
    if DBG:
        inc("scr", 16)
        emit("sync", lambda e: e.dma_start(out=ddbg["xfm2"], in_=xfm2[:]).then_inc(sems["scr"], 16))
    ctxE.close()

    ctxF = ExitStack()
    ring = [ctxF.enter_context(nc.sbuf_tensor(f"ring{i}", [128, 16, 512], BF16)) for i in range(4)]
    silu = [ctxF.enter_context(nc.sbuf_tensor(f"silu{i}", [128, 512], BF16)) for i in range(2)]
    rtr = ctxF.enter_context(nc.sbuf_tensor("rtr", [128, 32], F32))
    zap = ctxF.enter_context(nc.sbuf_tensor("zap", [128, 16], F32))
    sel = ctxF.enter_context(nc.sbuf_tensor("sel", [128, 16], F32))
    wr_sb = ctxF.enter_context(nc.sbuf_tensor("wr_sb", [128, 16, 16], BF16))
    cwpad = ctxF.enter_context(nc.sbuf_tensor("cwpad", [128, NCH, 64], F32))
    ekf = ctxF.enter_context(nc.sbuf_tensor("ekf", [128, NCH, 4], F32))
    idxi = ctxF.enter_context(nc.sbuf_tensor("idxi", [128, 8], U32))
    oh = ctxF.enter_context(nc.sbuf_tensor("oh", [128, NCH, 16], BF16))
    ohk = ctxF.enter_context(nc.sbuf_tensor("ohk", [128, NCH, 4, 16], BF16))
    rsK = ctxF.enter_context(nc.sbuf_tensor("rsK", [128, 16], F32))
    oht2 = ctxF.enter_context(nc.sbuf_tensor("oht2", [128, 16], BF16))
    rank_tok = ctxF.enter_context(nc.sbuf_tensor("rank_tok", [128, NCH, 16], F32))
    ranksel = ctxF.enter_context(nc.sbuf_tensor("ranksel", [128, NCH, 4], F32))
    slotf = ctxF.enter_context(nc.sbuf_tensor("slotf", [128, 32], F32))
    slotm = ctxF.enter_context(nc.sbuf_tensor("slotm", [128, 32], F32))
    slotg = ctxF.enter_context(nc.sbuf_tensor("slotg", [128, 32], F32))
    zero32 = ctxF.enter_context(nc.sbuf_tensor("zero32", [128, 32], F32))
    slotu = ctxF.enter_context(nc.sbuf_tensor("slotu", [128, 32], U32))
    erow_sb = ctxF.enter_context(nc.sbuf_tensor("erow_sb", [128, 16], F32))
    ltu_sb = ctxF.enter_context(nc.sbuf_tensor("ltu_sb", [128, 128], BF16))
    lton_sb = ctxF.enter_context(nc.sbuf_tensor("lton_sb", [128, 128], BF16))
    tid_sb = ctxF.enter_context(nc.sbuf_tensor("tid_sb", [128, 8], U32))
    identf = ctxF.enter_context(nc.sbuf_tensor("identf2", [128, 128], F32))
    pfil = ctxF.enter_context(nc.sbuf_tensor("pfil", [128, 64], U32))
    zrow = ctxF.enter_context(nc.sbuf_tensor("zrow", [1, C], BF16))
    zrow2 = ctxF.enter_context(nc.sbuf_tensor("zrow2", [1, 64], F32))
    bidx_sb = ctxF.enter_context(nc.sbuf_tensor("bidx_sb", [128, 64], U32))

    v_cst = inc("smk", 16)
    emit("sync", lambda e: e.dma_start(out=tid_sb[:], in_=d_tid).then_inc(sems["smk"], 16))
    wait("sync", "smk", v_cst)
    v_cst = inc("smk", 16)
    emit("sync", lambda e: e.dma_start(out=erow_sb[:], in_=d_erow).then_inc(sems["smk"], 16))
    wait("sync", "smk", v_cst)
    v_cst = inc("smk", 16)
    emit("sync", lambda e: e.dma_start(out=ltu_sb[:], in_=d_ltu).then_inc(sems["smk"], 16))
    v_cst_all = v_cst
    emit("vector", lambda e: e.memset(lton_sb[:], 1.0))
    emit("vector", lambda e: e.memset(zero32[:], 0.0))
    emit("vector", lambda e: e.memset(cwpad[:], 0.0))
    emit("vector", lambda e: e.memset(pfil[:], 1024))
    emit("vector", lambda e: e.memset(zrow[:], 0.0))
    emit("vector", lambda e: e.memset(zrow2[:], 0.0))
    v_ms = inc("ve")
    emit("vector", lambda e: e.tensor_copy(identf[:], identb[:]).then_inc(sems["ve"], 1))
    wait("sync", "ve", v_ms)
    inc("sac", 16)
    emit("sync", lambda e: e.dma_start(out=d_ln2[1024:1025, :], in_=zrow[:]).then_inc(sems["sac"], 16))
    wait("sync", "sac", cur("sac"))
    inc("sac", 16)
    emit("sync", lambda e: e.dma_start(out=d_cwsc[1024:1025, :], in_=zrow2[:]).then_inc(sems["sac"], 16))
    wait("gpsimd", "ve", v_ms)
    v_pf = inc("sbx", 16)
    emit("gpsimd", lambda e: e.dma_start(
        out=d_bidx.rearrange("(p g) o -> p (g o)", p=128), in_=pfil[:]).then_inc(sems["sbx"], 16))

    ctxF1 = ExitStack()
    gbuf = [ctxF1.enter_context(nc.sbuf_tensor(f"gbuf{i}", [128, 8, 512], BF16)) for i in range(2)]
    psFF = [ctxF1.enter_context(nc.psum_tensor(f"psFF{i}", [128, 512], F32)) for i in range(8)]

    bank_last = {i: ("ve", 0) for i in range(8)}

    def wait_bank(eng, bk):
        sem, val = bank_last[bk]
        wait(eng, sem, val)

    wait("gpsimd", "sws", cur("sws"))
    v_wr = inc("sws", 16)
    emit("gpsimd", lambda e: e.dma_start(
        out=wr_sb[:], in_=d_wr.rearrange("(dc p) f -> p dc f", p=128)).then_inc(sems["sws"], 16))
    wait("tensor", "sws", v_wr)
    for tt in range(NCH):
        for dc in range(16):
            def rmm(e, tt=tt, dc=dc, last=(tt == NCH - 1 and dc == 15)):
                ins = e.matmul(psFF[7][:, tt * 16:(tt + 1) * 16],
                               xfm2[:, dc, tt * 128:(tt + 1) * 128],
                               wr_sb[:, dc, :],
                               start=(tt == 0 and dc == 0), stop=last)
                if last:
                    ins.then_inc(sems["mm"], 1)
            emit("tensor", rmm)
    v_rmm = inc("mm")
    wait("vector", "mm", v_rmm)
    wait("vector", "smk", v_cst_all)
    for tt in range(NCH):
        v1 = inc("ve")
        emit("vector", lambda e, tt=tt: e.tensor_copy(
            rtr[:, 0:16], psFF[7][:, tt * 16:(tt + 1) * 16]).then_inc(sems["ve"], 1))
        wait("vector", "ve", v1)
        v2 = inc("ve")
        emit("vector", lambda e: e.max(out=rtr[:, 16:24], in_=rtr[:, 0:16]).then_inc(sems["ve"], 1))
        wait("vector", "ve", v2)
        v2i = inc("ve")
        emit("vector", lambda e: e.max_index(
            out=idxi[:], in_max=rtr[:, 16:24], in_values=rtr[:, 0:16]).then_inc(sems["ve"], 1))
        wait("vector", "ve", v2i)
        v2c = inc("ve")
        emit("vector", lambda e, tt=tt: e.tensor_copy(
            ekf[:, tt, :], idxi[:, 0:4]).then_inc(sems["ve"], 1))
        wait("vector", "ve", v2c)
        for k in range(4):
            v_t = inc("ve")
            emit("vector", lambda e, tt=tt, k=k: e.tensor_scalar(
                ohk[:, tt, k, :], erow_sb[:], ekf[:, tt, k:k + 1], None,
                op0=ALU.is_equal).then_inc(sems["ve"], 1))
        wait("vector", "ve", cur("ve"))
        v_oh = inc("ve")
        emit("vector", lambda e, tt=tt: e.tensor_add(
            oh[:, tt, :], ohk[:, tt, 0, :], ohk[:, tt, 1, :]).then_inc(sems["ve"], 1))
        wait("vector", "ve", v_oh)
        v_oh2 = inc("ve")
        emit("vector", lambda e, tt=tt: e.tensor_add(
            oht2[:], ohk[:, tt, 2, :], ohk[:, tt, 3, :]).then_inc(sems["ve"], 1))
        wait("vector", "ve", v_oh2)
        v_oh3 = inc("ve")
        emit("vector", lambda e, tt=tt: e.tensor_add(
            oh[:, tt, :], oh[:, tt, :], oht2[:]).then_inc(sems["ve"], 1))
        v3 = inc("ve")
        emit("vector", lambda e: e.memset(rtr[:, 20:24], 3e38).then_inc(sems["ve"], 1))
        wait("vector", "ve", v3)
        emit("vector", lambda e: e.match_replace(
            out=zap[:], in_to_replace=rtr[:, 16:24], in_values=rtr[:, 0:16], imm_value=NEG))
        v4 = inc("ve")
        emit("vector", lambda e: e.match_replace(
            out=zap[:], in_to_replace=rtr[:, 16:24], in_values=rtr[:, 0:16],
            imm_value=NEG).then_inc(sems["ve"], 1))
        wait("vector", "ve", v4)
        v5 = inc("ve")
        emit("vector", lambda e: e.scalar_tensor_tensor(
            zap[:], zap[:], -NEG, rtr[:, 0:16], op0=ALU.add,
            op1=ALU.subtract).then_inc(sems["ve"], 1))
        wait("vector", "ve", v5)
        v_z = inc("ve")
        emit("vector", lambda e: e.tensor_scalar(
            rtr[:, 24:25], rtr[:, 16:17], -1.0, None, op0=ALU.mult).then_inc(sems["ve"], 1))
        wait("scalar", "ve", v_z)
        v_e = inc("sc")
        emit("scalar", lambda e: e.activation(
            sel[:], zap[:], AF.Exp, bias=rtr[:, 24:25], scale=-1.0,
            accum_out=rtr[:, 25:26]).then_inc(sems["sc"], 1))
        wait("vector", "sc", v_e)
        v_ri2 = inc("ve")
        emit("vector", lambda e: e.reciprocal(rtr[:, 26:27], rtr[:, 25:26]).then_inc(sems["ve"], 1))
        wait("vector", "ve", v_ri2)
        v_cw = inc("ve")
        emit("vector", lambda e, tt=tt: e.tensor_scalar(
            cwpad[:, tt, 0:16], sel[:], rtr[:, 26:27], float(TOPK) / (TOPK + 1),
            op0=ALU.mult, op1=ALU.mult).then_inc(sems["ve"], 1))
    v_router_ve = cur("ve")
    wait("sync", "ve", v_router_ve)
    wait("sync", "sac", cur("sac"))
    inc("sac", 16)
    emit("sync", lambda e: e.dma_start(
        out=d_cwsc[0:1024, :].rearrange("(tt p) f -> p tt f", p=128),
        in_=cwpad[:]).then_inc(sems["sac"], 16))
    wait("tensor", "ve", v_router_ve)
    rk_ev = {}
    for tt in range(NCH):
        bk = 6 - (tt % 2)
        wait_bank("tensor", bk)
        for tp in range(tt + 1):
            def kmm2(e, tt=tt, tp=tp, bk=bk, last=(tp == tt)):
                ins = e.matmul(psFF[bk][:, 0:16],
                               ltu_sb[:] if tp == tt else lton_sb[:],
                               oh[:, tp, :],
                               start=(tp == 0), stop=last)
                if last:
                    ins.then_inc(sems["mm"], 1)
            emit("tensor", kmm2)
        v_k = inc("mm")
        wait("vector", "mm", v_k)
        v_ev = inc("ve")
        rk_ev[tt] = v_ev
        bank_last[bk] = ("ve", v_ev)
        emit("vector", lambda e, tt=tt, bk=bk: e.tensor_copy(
            rank_tok[:, tt, :], psFF[bk][:, 0:16]).then_inc(sems["ve"], 1))
    for tt in range(NCH):
        for k in range(4):
            v_mu = inc("ve")
            emit("vector", lambda e, tt=tt, k=k: e.tensor_mul(
                rsK[:], rank_tok[:, tt, :], ohk[:, tt, k, :]).then_inc(sems["ve"], 1))
            wait("vector", "ve", v_mu)
            v_rs = inc("ve")
            emit("vector", lambda e, tt=tt, k=k: e.reduce_sum(
                ranksel[:, tt, k:k + 1], rsK[:], axis=AX.X).then_inc(sems["ve"], 1))
            wait("vector", "ve", v_rs)
    v_ranks_done = cur("ve")
    wait("vector", "ve", v_ranks_done)
    v_sf = inc("ve")
    emit("vector", lambda e: e.scalar_tensor_tensor(
        slotf[:], ekf[:].rearrange("p a b -> p (a b)"), 512.0,
        ranksel[:].rearrange("p a b -> p (a b)"),
        op0=ALU.mult, op1=ALU.add).then_inc(sems["ve"], 1))
    wait("vector", "ve", v_sf)
    rsl = ranksel[:].rearrange("p a b -> p (a b)")
    ekl = ekf[:].rearrange("p a b -> p (a b)")
    v_c1 = inc("ve")
    emit("vector", lambda e: e.tensor_single_scalar(
        slotm[:], rsl, 128.0, op=ALU.is_ge).then_inc(sems["ve"], 1))
    v_c2 = inc("ve")
    emit("vector", lambda e: e.tensor_single_scalar(
        slotg[:], rsl, 256.0, op=ALU.is_ge).then_inc(sems["ve"], 1))
    v_c3 = inc("ve")
    emit("vector", lambda e: e.tensor_single_scalar(
        zero32[:], rsl, 384.0, op=ALU.is_ge).then_inc(sems["ve"], 1))
    wait("vector", "ve", v_c2)
    v_g1 = inc("ve")
    emit("vector", lambda e: e.tensor_add(slotm[:], slotm[:], slotg[:]).then_inc(sems["ve"], 1))
    wait("vector", "ve", v_g1)
    v_g2 = inc("ve")
    emit("vector", lambda e: e.tensor_add(slotm[:], slotm[:], zero32[:]).then_inc(sems["ve"], 1))
    wait("vector", "ve", v_g2)
    v_p = inc("ve")
    emit("vector", lambda e: e.scalar_tensor_tensor(
        slotg[:], slotm[:], -128.0, rsl, op0=ALU.mult, op1=ALU.add).then_inc(sems["ve"], 1))
    v_g = inc("ve")
    emit("vector", lambda e: e.scalar_tensor_tensor(
        slotm[:], ekl, 4.0, slotm[:], op0=ALU.mult, op1=ALU.add).then_inc(sems["ve"], 1))
    wait("vector", "ve", v_g)
    v_ff = inc("ve")
    emit("vector", lambda e: e.scalar_tensor_tensor(
        slotf[:], slotg[:], 64.0, slotm[:], op0=ALU.mult, op1=ALU.add).then_inc(sems["ve"], 1))
    wait("vector", "ve", v_ff)
    v_su = inc("ve")
    emit("vector", lambda e: e.tensor_copy(slotu[:], slotf[:]).then_inc(sems["ve"], 1))
    wait("gpsimd", "ve", v_su)
    wait("gpsimd", "sbx", v_pf)
    wait("gpsimd", "smk", 16)
    for tt in range(NCH):
        for k in range(4):
            def bsc(e, tt=tt, k=k):
                e.indirect_dma_start(
                    out=d_bidx, out_offset=IOA(slotu[:, tt * 4 + k:tt * 4 + k + 1], 0),
                    in_=tid_sb[:, tt:tt + 1], in_offset=None).then_inc(sems["sbx"], 16)
            emit("gpsimd", bsc)
            inc("sbx", 16)
    v_bidx_built = cur("sbx")
    wait("gpsimd", "sbx", v_bidx_built)
    v_bread = inc("sbx", 16)
    emit("gpsimd", lambda e: e.dma_start(
        out=bidx_sb[:], in_=d_bidx.rearrange("(p g) o -> p (g o)", p=128)).then_inc(sems["sbx"], 16))

    slot_free = {}
    ld_done = {}
    nload = [0]

    def exp_srcs(e_):
        if e_ < 16:
            w1 = d_we1[e_]; w3 = d_we3[e_]; w2 = d_we2[e_]
        else:
            s = e_ - 16
            w1 = d_ws1[:, s * EH:(s + 1) * EH]
            w3 = d_ws3[:, s * EH:(s + 1) * EH]
            w2 = d_ws2[s * EH:(s + 1) * EH, :]
        return (
            w1[:, 0:512].rearrange("(dc p) f -> p dc f", p=128),
            w3[:, 0:512].rearrange("(dc p) f -> p dc f", p=128),
            w1[:, 512:1024].rearrange("(dc p) f -> p dc f", p=128),
            w3[:, 512:1024].rearrange("(dc p) f -> p dc f", p=128),
            w2[0:512, :].rearrange("(hc p) f -> p hc f", p=128),
            w2[512:1024, :].rearrange("(hc p) f -> p hc f", p=128),
        )

    def emit_load(key, src):
        i = nload[0]
        nload[0] += 1
        s = i % 4
        if i >= 4:
            assert i - 4 in slot_free, ("ring order", i)
            wait("gpsimd", "mm", slot_free[i - 4])
        lane = "sw%d" % s
        v = inc(lane, 16)
        ld_done[key] = (v, s)
        emit("gpsimd", lambda e, src=src, s=s, lane=lane: e.dma_start(
            out=ring[s][:], in_=src).then_inc(sems[lane], 16))
        return i

    DSH = 4
    loadix = {}
    for k in range(4):
        loadix[(0, k)] = emit_load((0, k), exp_srcs(16)[k])
    gq = 0
    sil_done, gmul_done = {}, {}
    we2_mm_last = {}
    ffe_done = {}
    for e_ in range(DSH):
        ge = 4 * (e_ % 2)
        for hh in range(2):
            v1, s1 = ld_done[(e_, 2 * hh)]
            v3, s3 = ld_done[(e_, 2 * hh + 1)]
            wait("tensor", "sw%d" % s1, v1)
            wait("tensor", "sw%d" % s3, v3)
            for tc in range(2):
                for hc4 in range(4):
                    hc = hh * 4 + hc4
                    b1 = ge + (gq % 2)
                    b3 = ge + 2 + (gq % 2)
                    wait_bank("tensor", b1)
                    for dc in range(16):
                        def g1mm(e, s1=s1, hc4=hc4, dc=dc, tc=tc, b1=b1):
                            ins = e.matmul(psFF[b1][:],
                                           ring[s1][:, dc, hc4 * 128:(hc4 + 1) * 128],
                                           xfm2[:, dc, tc * 512:(tc + 1) * 512],
                                           start=(dc == 0), stop=(dc == 15))
                            if dc == 15:
                                ins.then_inc(sems["mm"], 1)
                        emit("tensor", g1mm)
                    v_m1 = inc("mm")
                    wait_bank("tensor", b3)
                    for dc in range(16):
                        def g3mm(e, s3=s3, hc4=hc4, dc=dc, tc=tc, b3=b3):
                            ins = e.matmul(psFF[b3][:],
                                           ring[s3][:, dc, hc4 * 128:(hc4 + 1) * 128],
                                           xfm2[:, dc, tc * 512:(tc + 1) * 512],
                                           start=(dc == 0), stop=(dc == 15))
                            if dc == 15:
                                ins.then_inc(sems["mm"], 1)
                        emit("tensor", g3mm)
                    v_m3 = inc("mm")
                    wait("scalar", "mm", v_m1)
                    wait("scalar", "ve", gmul_done.get(gq - 2, 0))
                    v_sl = inc("sc")
                    sil_done[gq] = v_sl
                    emit("scalar", lambda e, b1=b1, sb=gq % 2: e.activation(
                        silu[sb][:], psFF[b1][:], AF.Silu).then_inc(sems["sc"], 1))
                    bank_last[b1] = ("sc", v_sl)
                    wait("vector", "mm", v_m3)
                    wait("vector", "sc", v_sl)
                    wait("vector", "mm", we2_mm_last.get((e_ - 1, tc), 0))
                    v_gm = inc("ve")
                    gmul_done[gq] = v_gm
                    emit("vector", lambda e, b3=b3, tc=tc, hc=hc, sb=gq % 2: e.tensor_mul(
                        gbuf[tc][:, hc, :], silu[sb][:], psFF[b3][:]).then_inc(sems["ve"], 1))
                    bank_last[b3] = ("ve", v_gm)
                    gq += 1
            if hh == 0:
                slot_free[loadix[(e_, 0)]] = cur("mm")
                slot_free[loadix[(e_, 1)]] = cur("mm")
                loadix[(e_, 4)] = emit_load((e_, 4), exp_srcs(16 + e_)[4])
                loadix[(e_, 5)] = emit_load((e_, 5), exp_srcs(16 + e_)[5])
        slot_free[loadix[(e_, 2)]] = cur("mm")
        slot_free[loadix[(e_, 3)]] = cur("mm")
        if e_ + 1 < DSH:
            loadix[(e_ + 1, 0)] = emit_load((e_ + 1, 0), exp_srcs(16 + e_ + 1)[0])
            loadix[(e_ + 1, 1)] = emit_load((e_ + 1, 1), exp_srcs(16 + e_ + 1)[1])
        v2a, s2a = ld_done[(e_, 4)]
        v2b, s2b = ld_done[(e_, 5)]
        wait("tensor", "sw%d" % s2a, v2a)
        wait("tensor", "sw%d" % s2b, v2b)
        w = 0
        gq0 = gq - 16
        for tc in range(2):
            wait("tensor", "ve", gmul_done[gq0 + 8 + 4 * tc + 3])
            for tt4 in range(4):
                tt = tc * 4 + tt4
                wset = 4 * ((e_ + w + 1) % 2)
                for dc4 in range(4):
                    wait_bank("tensor", wset + dc4)
                for hc in range(8):
                    s2 = s2a if hc < 4 else s2b
                    for dc4 in range(4):
                        def w2mm(e, s2=s2, hc=hc, tt4=tt4, tc=tc, dc4=dc4, wset=wset,
                                 last=(hc == 7 and dc4 == 3)):
                            rhs = bass.AP(ring[s2], (hc % 4) * 2048 + dc4 * 512,
                                          [[8192, 128], [1, 512]])
                            ins = e.matmul(psFF[wset + dc4][:],
                                           gbuf[tc][:, hc, tt4 * 128:(tt4 + 1) * 128],
                                           rhs,
                                           start=(hc == 0), stop=(hc == 7))
                            if last:
                                ins.then_inc(sems["mm"], 1)
                        emit("tensor", w2mm)
                v_w2 = inc("mm")
                wait("vector", "mm", v_w2)
                for dc4 in range(4):
                    def fev(e, e_=e_, tt=tt, dc4=dc4, wset=wset, last=(dc4 == 3)):
                        accsl = acc[:, tt, dc4 * 512:(dc4 + 1) * 512]
                        if e_ == 0:
                            ins = e.tensor_scalar(accsl, psFF[wset + dc4][:],
                                                  1.0 / (TOPK + 1), None, op0=ALU.mult)
                        else:
                            ins = e.scalar_tensor_tensor(accsl, psFF[wset + dc4][:],
                                                         1.0 / (TOPK + 1), accsl,
                                                         op0=ALU.mult, op1=ALU.add)
                        if last:
                            ins.then_inc(sems["ve"], 1)
                    emit("vector", fev)
                v_fe = inc("ve")
                ffe_done[(e_, tt4 + 4 * tc)] = v_fe
                for dc4 in range(4):
                    bank_last[wset + dc4] = ("ve", v_fe)
                w += 1
            we2_mm_last[(e_, tc)] = cur("mm")
        slot_free[loadix[(e_, 4)]] = cur("mm")
        slot_free[loadix[(e_, 5)]] = cur("mm")
        if e_ + 1 < DSH:
            loadix[(e_ + 1, 2)] = emit_load((e_ + 1, 2), exp_srcs(16 + e_ + 1)[2])
            loadix[(e_ + 1, 3)] = emit_load((e_ + 1, 3), exp_srcs(16 + e_ + 1)[3])
    v_shared_ve = cur("ve")
    v_shared_mm = cur("mm")
    wait("sync", "ve", v_shared_ve)
    wait("sync", "sac", cur("sac"))
    acst = {}
    for tt in range(NCH):
        acst[tt] = inc("sac", 16)
        emit("sync", lambda e, tt=tt: e.dma_start(
            out=d_accd[tt * 128:(tt + 1) * 128, :], in_=acc[:, tt, :]).then_inc(sems["sac"], 16))
    v_acc_stored = cur("sac")
    ctxF1.close()

    ctxF2 = ExitStack()
    xg_tok = nc.alloc_sbuf_tensor_at("xg_tok", [128, 4, C], BF16, offset=130784)
    xgfm = nc.alloc_sbuf_tensor_at("xgfm", [128, 16, 512], BF16, offset=147168)
    msta = [nc.alloc_sbuf_tensor_at("msta0", [128, 4, C], F32, offset=163552),
            nc.alloc_sbuf_tensor_at("msta1", [128, 4, C], F32, offset=196576)]
    cwg = [ctxF2.enter_context(nc.sbuf_tensor(f"cwg{i}", [128, 4, 64], F32)) for i in range(2)]
    g_sp = [ctxF2.enter_context(nc.sbuf_tensor(f"g_sp{i}", [128, 8, 512], BF16)) for i in range(2)]
    assert nc.sbuf_base <= 130784, ("sparse left overflow", nc.sbuf_base)
    psT3 = [ctxF2.enter_context(nc.psum_tensor(f"psT3{i}", [128, 4, 128], BF16)) for i in range(2)]
    psG1 = [ctxF2.enter_context(nc.psum_tensor(f"psG1{i}", [128, 512], F32)) for i in range(2)]
    psG3 = [ctxF2.enter_context(nc.psum_tensor(f"psG3{i}", [128, 512], F32)) for i in range(2)]
    psW2 = [ctxF2.enter_context(nc.psum_tensor(f"psW2{i}", [128, 512], F32)) for i in range(2)]

    wait("gpsimd", "mm", v_shared_mm)
    wait("gpsimd", "ve", v_shared_ve)
    wait("tensor", "ve", v_shared_ve)
    wait("tensor", "sc", cur("sc"))

    tp_sp = {}
    ev_sp = {}
    gm_sp = {}
    si_sp = {}
    fe_sp = {}
    gthc = {}
    sc_done = {}
    w2_sp = {}
    gth = {}
    def emit_gathers(e_):
        wait("gpsimd", "sbx", v_bread)
        wait("gpsimd", "mm", tp_sp.get(e_ - 1, 0))
        for sub in range(4):
            def gx1(e, e_=e_, sub=sub):
                e.indirect_dma_start(
                    out=xg_tok[:, sub, :], out_offset=None,
                    in_=d_ln2, in_offset=IOA(bidx_sb[:, e_ * 4 + sub:e_ * 4 + sub + 1], 0)
                ).then_inc(sems["sgt"], 16)
            emit("gpsimd", gx1)
            inc("sgt", 16)
        wait("gpsimd", "ve", fe_sp.get((e_ - 2, 15), 0))
        wait("gpsimd", "sac", v_acc_stored)
        wait("gpsimd", "sgc", cur("sgc"))
        for sub in range(4):
            def gx2(e, e_=e_, sub=sub):
                e.indirect_dma_start(
                    out=cwg[e_ % 2][:, sub, :], out_offset=None,
                    in_=d_cwsc, in_offset=IOA(bidx_sb[:, e_ * 4 + sub:e_ * 4 + sub + 1], 0)
                ).then_inc(sems["sgc"], 16)
            emit("gpsimd", gx2)
            inc("sgc", 16)
        gth[e_] = cur("sgt")
        gthc[e_] = cur("sgc")

    gq2 = 0
    for e_ in range(16):
        if e_ == 0:
            for k in range(4):
                loadix[(10, e_, k)] = emit_load((10, e_, k), exp_srcs(0)[k])
            emit_gathers(0)
        wait("tensor", "sgt", gth[e_])
        wait("tensor", "mm", w2_sp.get(e_ - 1, 0))
        for sub in range(4):
            for dq in range(4):
                grp = sub * 4 + dq
                par = grp % 2
                wait("tensor", "ve", ev_sp.get((e_, grp - 2), ev_sp.get((e_ - 1, grp + 14), 0)))
                for j in range(4):
                    dc = dq * 4 + j
                    def tpx(e, par=par, sub=sub, dc=dc, j=j, last=(j == 3)):
                        ins = e.transpose(psT3[par][:, j, :],
                                          xg_tok[:, sub, dc * 128:(dc + 1) * 128], identb[:])
                        if last:
                            ins.then_inc(sems["mm"], 1)
                    emit("tensor", tpx)
                v_tp = inc("mm")
                wait("vector", "mm", v_tp)
                v_ev = inc("ve")
                ev_sp[(e_, grp)] = v_ev
                emit("vector", lambda e, par=par, sub=sub, dq=dq: e.tensor_copy(
                    xgfm[:, dq * 4:(dq + 1) * 4, sub * 128:(sub + 1) * 128],
                    psT3[par][:]).then_inc(sems["ve"], 1))
        tp_sp[e_] = cur("mm")
        v_xgfm = cur("ve")
        for hh in range(2):
            v1, s1 = ld_done[(10, e_, 2 * hh)]
            v3, s3 = ld_done[(10, e_, 2 * hh + 1)]
            wait("tensor", "sw%d" % s1, v1)
            wait("tensor", "sw%d" % s3, v3)
            wait("tensor", "ve", v_xgfm)
            for hc4 in range(4):
                hc = hh * 4 + hc4
                b1 = gq2 % 2
                b3 = gq2 % 2
                wait("tensor", "sc", si_sp.get(gq2 - 2, 0))
                for dc in range(16):
                    def sg1(e, s1=s1, hc4=hc4, dc=dc, b1=b1):
                        ins = e.matmul(psG1[b1][:],
                                       ring[s1][:, dc, hc4 * 128:(hc4 + 1) * 128],
                                       xgfm[:, dc, :],
                                       start=(dc == 0), stop=(dc == 15))
                        if dc == 15:
                            ins.then_inc(sems["mm"], 1)
                    emit("tensor", sg1)
                v_m1 = inc("mm")
                wait("tensor", "ve", gm_sp.get((e_, hc - 2), gm_sp.get((e_ - 1, hc + 6), 0)))
                for dc in range(16):
                    def sg3(e, s3=s3, hc4=hc4, dc=dc, b3=b3):
                        ins = e.matmul(psG3[b3][:],
                                       ring[s3][:, dc, hc4 * 128:(hc4 + 1) * 128],
                                       xgfm[:, dc, :],
                                       start=(dc == 0), stop=(dc == 15))
                        if dc == 15:
                            ins.then_inc(sems["mm"], 1)
                    emit("tensor", sg3)
                v_m3 = inc("mm")
                wait("scalar", "mm", v_m1)
                wait("scalar", "ve", gm_sp.get((e_, hc - 2), gm_sp.get((e_ - 1, hc + 6), 0)))
                v_sl = inc("sc")
                si_sp[gq2] = v_sl
                emit("scalar", lambda e, b1=b1, sb=gq2 % 2: e.activation(
                    silu[sb][:], psG1[b1][:], AF.Silu).then_inc(sems["sc"], 1))
                wait("vector", "mm", v_m3)
                wait("vector", "sc", v_sl)
                wait("vector", "mm", w2_sp.get(e_ - 1, 0))
                v_gm = inc("ve")
                gm_sp[(e_, hc)] = v_gm
                emit("vector", lambda e, b3=b3, e_=e_, hc=hc, sb=gq2 % 2: e.tensor_mul(
                    g_sp[e_ % 2][:, hc, :], silu[sb][:], psG3[b3][:]).then_inc(sems["ve"], 1))
                gq2 += 1
            if hh == 0 and e_ == 0:
                slot_free[loadix[(10, 0, 0)]] = cur("mm")
                slot_free[loadix[(10, 0, 1)]] = cur("mm")
                loadix[(10, 0, 4)] = emit_load((10, 0, 4), exp_srcs(0)[4])
                loadix[(10, 0, 5)] = emit_load((10, 0, 5), exp_srcs(0)[5])
            elif hh == 0:
                slot_free[loadix[(10, e_, 0)]] = cur("mm")
                slot_free[loadix[(10, e_, 1)]] = cur("mm")
                loadix[(10, e_, 4)] = emit_load((10, e_, 4), exp_srcs(e_)[4])
                loadix[(10, e_, 5)] = emit_load((10, e_, 5), exp_srcs(e_)[5])
        slot_free[loadix[(10, e_, 2)]] = cur("mm")
        slot_free[loadix[(10, e_, 3)]] = cur("mm")
        if e_ + 1 < 16:
            loadix[(10, e_ + 1, 0)] = emit_load((10, e_ + 1, 0), exp_srcs(e_ + 1)[0])
            loadix[(10, e_ + 1, 1)] = emit_load((10, e_ + 1, 1), exp_srcs(e_ + 1)[1])
        v2a, s2a = ld_done[(10, e_, 4)]
        v2b, s2b = ld_done[(10, e_, 5)]
        wait("tensor", "sw%d" % s2a, v2a)
        wait("tensor", "sw%d" % s2b, v2b)
        wait("tensor", "ve", gm_sp[(e_, 7)])
        wt = 0
        for sub in range(4):
            for dc4 in range(4):
                bw = wt % 2
                wait("tensor", "ve", fe_sp.get((e_, wt - 2), fe_sp.get((e_ - 1, wt + 14), 0)))
                for hc in range(8):
                    s2 = s2a if hc < 4 else s2b
                    def sw2(e, s2=s2, hc=hc, sub=sub, dc4=dc4, bw=bw, e_=e_,
                            last=(hc == 7)):
                        rhs = bass.AP(ring[s2], (hc % 4) * 2048 + dc4 * 512,
                                      [[8192, 128], [1, 512]])
                        ins = e.matmul(psW2[bw][:],
                                       g_sp[e_ % 2][:, hc, sub * 128:(sub + 1) * 128],
                                       rhs,
                                       start=(hc == 0), stop=(hc == 7))
                        if last:
                            ins.then_inc(sems["mm"], 1)
                    emit("tensor", sw2)
                v_w2 = inc("mm")
                wait("vector", "mm", v_w2)
                wait("vector", "sgc", gthc[e_])
                wait("vector", "ssc", sc_done.get(e_ - 2, 0))
                v_fe = inc("ve")
                fe_sp[(e_, wt)] = v_fe
                emit("vector", lambda e, e_=e_, sub=sub, dc4=dc4, bw=bw: e.tensor_scalar(
                    msta[e_ % 2][:, sub, dc4 * 512:(dc4 + 1) * 512], psW2[bw][:],
                    cwg[e_ % 2][:, sub, e_:e_ + 1], None,
                    op0=ALU.mult).then_inc(sems["ve"], 1))
                wt += 1
        w2_sp[e_] = cur("mm")
        slot_free[loadix[(10, e_, 4)]] = cur("mm")
        slot_free[loadix[(10, e_, 5)]] = cur("mm")
        if e_ + 1 < 16:
            emit_gathers(e_ + 1)
            loadix[(10, e_ + 1, 2)] = emit_load((10, e_ + 1, 2), exp_srcs(e_ + 1)[2])
            loadix[(10, e_ + 1, 3)] = emit_load((10, e_ + 1, 3), exp_srcs(e_ + 1)[3])
        wait("gpsimd", "ve", fe_sp[(e_, 15)])
        wait("gpsimd", "ssc", sc_done.get(e_ - 1, 0))
        wait("gpsimd", "sac", v_acc_stored)
        for sub in range(4):
            def ssc1(e, e_=e_, sub=sub):
                e.indirect_dma_start(
                    out=d_accd, out_offset=IOA(bidx_sb[:, e_ * 4 + sub:e_ * 4 + sub + 1], 0),
                    in_=msta[e_ % 2][:, sub, :], in_offset=None,
                    compute_op=ALU.add).then_inc(sems["ssc"], 16)
            emit("gpsimd", ssc1)
            inc("ssc", 16)
        sc_done[e_] = cur("ssc")
    v_ssc_all = cur("ssc")

    ctxF3 = ExitStack()
    x2r = [ctxF3.enter_context(nc.sbuf_tensor(f"x2r{i}", [128, C], F32)) for i in range(2)]
    accr = [ctxF3.enter_context(nc.sbuf_tensor(f"accr{i}", [128, C], F32)) for i in range(2)]
    out_done = {}
    for tt in range(NCH):
        wait("sync", "scr", v_scr_all)
        wait("sync", "ssc", v_ssc_all)
        olane = "sout%d" % (tt % 2)
        rlane = "sxr%d" % (tt % 2)
        wait("sync", olane, out_done.get(tt - 2, 0))
        v_xr = inc(rlane, 16)
        emit("sync", lambda e, tt=tt, rlane=rlane: e.dma_start(
            out=x2r[tt % 2][:], in_=d_x2[tt * 128:(tt + 1) * 128, :]).then_inc(sems[rlane], 16))
        wait("sync", rlane, v_xr)
        v_ar = inc(rlane, 16)
        emit("sync", lambda e, tt=tt, rlane=rlane: e.dma_start(
            out=accr[tt % 2][:], in_=d_accd[tt * 128:(tt + 1) * 128, :]).then_inc(sems[rlane], 16))
        wait("vector", rlane, v_ar)
        wait("vector", olane, out_done.get(tt - 2, 0))
        v_fin = inc("ve")
        emit("vector", lambda e, tt=tt: e.tensor_add(
            x2r[tt % 2][:], x2r[tt % 2][:], accr[tt % 2][:]).then_inc(sems["ve"], 1))
        wait("sync", "ve", v_fin)
        out_done[tt] = inc(olane, 16)
        emit("sync", lambda e, tt=tt, olane=olane: e.dma_start(
            out=d_out[tt * 128:(tt + 1) * 128, :], in_=x2r[tt % 2][:]).then_inc(sems[olane], 16))
    wait("sync", "sout0", out_done[6])
    wait("sync", "sout1", out_done[7])
    ctxF3.close()
    ctxF2.close()
    ctxF.close()

    with nc.Block() as block:
        @block.sync
        def _(e):
            for fn in prog["sync"]:
                fn(e)

        @block.gpsimd
        def _(e):
            for fn in prog["gpsimd"]:
                fn(e)

        @block.tensor
        def _(e):
            for fn in prog["tensor"]:
                fn(e)

        @block.vector
        def _(e):
            for fn in prog["vector"]:
                fn(e)

        @block.scalar
        def _(e):
            for fn in prog["scalar"]:
                fn(e)

    master.close()
    return nc


def _rope_tables_f64():
    j = np.arange(64, dtype=np.float64)
    inv = 10000.0 ** (-2.0 * j / 128.0)
    t = np.arange(T, dtype=np.float64)
    ang = t[None, :] * inv[:, None]
    cos = np.cos(ang)
    sin = np.sin(ang)
    Ct = np.concatenate([cos, cos], axis=0)
    St = np.concatenate([-sin, sin], axis=0)
    return Ct, St


def _hi_lo(a64):
    hi = a64.astype(np.float32)
    lo = (a64 - hi.astype(np.float64)).astype(np.float32)
    return np.stack([hi, lo], axis=0)


def kernel(L0_attn_x, L0_attn_Wq, L0_attn_Wk, L0_attn_Wv, L0_attn_Wproj,
           L0_ffn_w_router, L0_ffn_we1, L0_ffn_we3, L0_ffn_we2,
           L0_ffn_ws1, L0_ffn_ws3, L0_ffn_ws2):
    x = np.ascontiguousarray(np.asarray(L0_attn_x, dtype=np.float32))
    weights = {
        "wq": np.ascontiguousarray(np.asarray(L0_attn_Wq, dtype=np.float32)),
        "wk": np.ascontiguousarray(np.asarray(L0_attn_Wk, dtype=np.float32)),
        "wv": np.ascontiguousarray(np.asarray(L0_attn_Wv, dtype=np.float32)),
        "wp": np.ascontiguousarray(np.asarray(L0_attn_Wproj, dtype=np.float32)),
        "wr": np.ascontiguousarray(np.asarray(L0_ffn_w_router, dtype=np.float32)),
        "we1": np.ascontiguousarray(np.asarray(L0_ffn_we1, dtype=np.float32)),
        "we3": np.ascontiguousarray(np.asarray(L0_ffn_we3, dtype=np.float32)),
        "we2": np.ascontiguousarray(np.asarray(L0_ffn_we2, dtype=np.float32)),
        "ws1": np.ascontiguousarray(np.asarray(L0_ffn_ws1, dtype=np.float32)),
        "ws3": np.ascontiguousarray(np.asarray(L0_ffn_ws3, dtype=np.float32)),
        "ws2": np.ascontiguousarray(np.asarray(L0_ffn_ws2, dtype=np.float32)),
    }
    ident = np.eye(128, dtype=ml_dtypes.bfloat16)
    Ct64, St64 = _rope_tables_f64()
    tidc = (np.arange(8)[None, :] * 128 + np.arange(128)[:, None]).astype(np.uint32)
    erowc = np.tile(np.arange(16, dtype=np.float32)[None, :], (128, 1))
    ltuc = np.triu(np.ones((128, 128), np.float32), 1).astype(ml_dtypes.bfloat16)

    in_maps = []
    for core in range(8):
        b, r = core // 4, core % 4
        perm = core_perm(r)
        rowp = np.concatenate([np.arange(p * 128, (p + 1) * 128) for p in perm])
        maskA = np.zeros((128, NCH, 512), np.float32)
        maskB = np.zeros((128, NCH, 4, 128), np.float32)
        for oc in range(NCH):
            blk = own_block(r, oc)
            qpos = blk * 128 + np.arange(128)
            kpos = rowp[oc * 512:(oc + 1) * 512]
            bad = kpos[None, :] > qpos[:, None]
            maskA[:, oc, :] = np.where(bad, NEG, 0.0)
            maskB[:, oc, :, :] = np.where(bad, NEG, 0.0).T.reshape(4, 128, 128).transpose(1, 0, 2)
        m = {
            "xb": np.ascontiguousarray(x[b][rowp]),
            "idb": ident,
            "tidc": tidc,
            "erowc": erowc,
            "ltuc": ltuc,
            "ropeC": np.ascontiguousarray(_hi_lo(Ct64[:, rowp])),
            "ropeS": np.ascontiguousarray(_hi_lo(St64[:, rowp])),
            "maskA": maskA.astype(ml_dtypes.bfloat16),
            "maskB": maskB.astype(ml_dtypes.bfloat16),
        }
        m.update(weights)
        in_maps.append(m)

    nc = build_program()
    res = run_bass_kernel_spmd(nc, in_maps, core_ids=list(range(8)))

    out = np.empty((B, T, C), dtype=np.float32)
    for core in range(8):
        b, r = core // 4, core % 4
        oo = np.asarray(res.results[core]["out_own"])
        for oc in range(NCH):
            blk = own_block(r, oc)
            out[b, blk * 128:(blk + 1) * 128, :] = oo[oc * 128:(oc + 1) * 128, :]
    return out


# version 22 (final)
# speedup vs baseline: 1.1067x; 1.0997x over previous
import math
from contextlib import ExitStack

import numpy as np
import ml_dtypes

import concourse.bass as bass
import concourse.mybir as mybir
from concourse.bass_utils import run_bass_kernel_spmd

F32 = mybir.dt.float32
BF16 = mybir.dt.bfloat16
AF = mybir.ActivationFunctionType
ALU = mybir.AluOpType
AX = mybir.AxisListType
U32 = mybir.dt.uint32
I32 = mybir.dt.int32
IOA = bass.IndirectOffsetOnAxis

B, T, C = 2, 4096, 2048
NB = T // 128
NCH = T // 512
TOPK = 4
EH = 1024
NEXP = 20
SCALE = 1.0 / math.sqrt(128.0)
EPS = 1e-5
NEG = -1e30
KiB = 1024

ENGINES = ("sync", "gpsimd", "tensor", "vector", "scalar")
DBG = False


def own_block(r, c):
    return 4 * c + r if c < 4 else 4 * c + 3 - r


def core_perm(r):
    out = []
    for c in range(NCH):
        blk = own_block(r, c)
        out.extend([blk] + [b for b in range(4 * c, 4 * c + 4) if b != blk])
    return out


def build_program():
    nc = bass.Bass("TRN2", target_bir_lowering=False, debug=False,
                   detect_race_conditions=False)

    d_x = nc.dram_tensor("xb", [T, C], F32, kind="ExternalInput").ap()
    d_wq = nc.dram_tensor("wq", [C, C], F32, kind="ExternalInput").ap()
    d_wk = nc.dram_tensor("wk", [C, 512], F32, kind="ExternalInput").ap()
    d_wv = nc.dram_tensor("wv", [C, 512], F32, kind="ExternalInput").ap()
    d_wp = nc.dram_tensor("wp", [C, C], F32, kind="ExternalInput").ap()
    d_wr = nc.dram_tensor("wr", [C, 16], F32, kind="ExternalInput").ap()
    d_we1 = nc.dram_tensor("we1", [16, C, EH], F32, kind="ExternalInput").ap()
    d_we3 = nc.dram_tensor("we3", [16, C, EH], F32, kind="ExternalInput").ap()
    d_we2 = nc.dram_tensor("we2", [16, EH, C], F32, kind="ExternalInput").ap()
    d_ws1 = nc.dram_tensor("ws1", [C, 4096], F32, kind="ExternalInput").ap()
    d_ws3 = nc.dram_tensor("ws3", [C, 4096], F32, kind="ExternalInput").ap()
    d_ws2 = nc.dram_tensor("ws2", [4096, C], F32, kind="ExternalInput").ap()
    d_idb = nc.dram_tensor("idb", [128, 128], BF16, kind="ExternalInput").ap()
    d_ropeC = nc.dram_tensor("ropeC", [2, 128, T], F32, kind="ExternalInput").ap()
    d_ropeS = nc.dram_tensor("ropeS", [2, 128, T], F32, kind="ExternalInput").ap()
    d_maskA = nc.dram_tensor("maskA", [128, NCH, 512], BF16, kind="ExternalInput").ap()
    d_maskB = nc.dram_tensor("maskB", [128, NCH, 4, 128], BF16, kind="ExternalInput").ap()
    d_out = nc.dram_tensor("out_own", [1024, C], F32, kind="ExternalOutput").ap()
    d_ln2 = nc.dram_tensor("ln2scr", [1025, C], BF16, kind="Internal").ap()
    d_cwsc = nc.dram_tensor("cwscr", [1025, 64], F32, kind="Internal").ap()
    d_bidx = nc.dram_tensor("bidxscr", [8192, 1], U32, kind="Internal").ap()
    d_accd = nc.dram_tensor("accscr", [1025, C], F32, kind="Internal").ap()
    d_tid = nc.dram_tensor("tidc", [128, 8], U32, kind="ExternalInput").ap()
    d_erow = nc.dram_tensor("erowc", [128, 16], F32, kind="ExternalInput").ap()
    d_ltu = nc.dram_tensor("ltuc", [128, 128], BF16, kind="ExternalInput").ap()
    ddbg = {}
    if DBG:
        for nm, shp in (("xfm_own", [128, 16, 1024]), ("k_sb", [128, 4, T]),
                        ("v_sb", [128, NB, 4, 129]), ("q_sb", [128, 16, 1024]),
                        ("yfm", [128, NCH, 16, 128]),
                        ("xfm2", [128, 16, 1024])):
            dt = BF16
            ddbg[nm] = nc.dram_tensor("dbg_" + nm, shp, dt, kind="ExternalOutput").ap()
        for nm in ("x2f", "acc"):
            ddbg[nm] = nc.dram_tensor("dbg_" + nm, [128, NCH, C], F32, kind="ExternalOutput").ap()
        ddbg["cw"] = nc.dram_tensor("dbg_cw", [128, NCH, 16], F32, kind="ExternalOutput").ap()
    d_x2 = nc.dram_tensor("x2scr", [1024, C], F32, kind="Internal").ap()

    prog = {e: [] for e in ENGINES}
    S = {}
    last_wait = {e: {} for e in ENGINES}
    master = ExitStack()
    sems = {}
    for name in ("mm", "ve", "sc", "scr", "bar", "fl", "sid", "srope", "smk",
                 "sbx", "sgt", "sgc", "ssc", "sac",
                 "sx0", "sx1", "sxo0", "sxo1", "sxr0", "sxr1", "sout0", "sout1",
                 "sws", "swq0", "swq1", "swp0", "swp1", "sw0", "sw1", "sw2", "sw3", "sw4"):
        sems[name] = master.enter_context(nc.semaphore(name))

    def emit(eng, fn):
        prog[eng].append(fn)

    def inc(sem, amt=1):
        S[sem] = S.get(sem, 0) + amt
        return S[sem]

    def cur(sem):
        return S.get(sem, 0)

    def wait(eng, sem, val):
        if not val or val <= 0:
            return
        assert val <= S.get(sem, 0), ("wait-ahead", eng, sem, val, S.get(sem, 0))
        if last_wait[eng].get(sem, 0) >= val:
            return
        last_wait[eng][sem] = val
        h = sems[sem]
        emit(eng, lambda e, h=h, v=val: e.wait_ge(h, v))

    def barrier():
        h = sems["bar"]
        for eng in ENGINES:
            inc("bar")
            emit(eng, lambda e, h=h: e.sem_inc(h, 1))
        t = cur("bar")
        for eng in ENGINES:
            wait(eng, "bar", t)

    assert nc.sbuf_top >= 229344
    k_sb = nc.alloc_sbuf_tensor_at("k_sb", [128, 4, T], BF16, offset=196576)
    v_sb = nc.alloc_sbuf_tensor_at("v_sb", [128, NB, 4, 129], BF16, offset=163552)
    q_sb = nc.alloc_sbuf_tensor_at("q_sb", [128, 16, 1024], BF16, offset=130784)
    yfm = nc.alloc_sbuf_tensor_at("yfm", [128, NCH, 16, 128], BF16, offset=98016)
    x2f = nc.alloc_sbuf_tensor_at("x2f", [128, NCH, C], F32, offset=130784)
    xfm2 = nc.alloc_sbuf_tensor_at("xfm2", [128, 16, 1024], BF16, offset=196576)
    acc = nc.alloc_sbuf_tensor_at("acc", [128, NCH, C], F32, offset=130784)

    identb = nc.alloc_sbuf_tensor("identb", [128, 128], BF16)
    epsb = nc.alloc_sbuf_tensor("epsb", [128, 1], F32)
    onesb = nc.alloc_sbuf_tensor("onesb", [65, 128], BF16)
    v_id = inc("sid", 16)
    emit("sync", lambda e: e.dma_start(out=identb[:], in_=d_idb).then_inc(sems["sid"], 16))
    emit("vector", lambda e: e.memset(onesb[:], 1.0))
    v_eps = inc("ve")
    emit("vector", lambda e: e.memset(epsb[:], float(EPS)).then_inc(sems["ve"], 1))
    wait("scalar", "ve", v_eps)
    emit("vector", lambda e: e.memset(v_sb[:, :, :, 128:129], 1.0))

    ctxAB = ExitStack()
    ropeC = ctxAB.enter_context(nc.sbuf_tensor("ropeCt", [128, T], BF16))
    ropeS = ctxAB.enter_context(nc.sbuf_tensor("ropeSt", [128, T], BF16))
    roptmp = ctxAB.enter_context(nc.sbuf_tensor("roptmp", [128, 512], F32))
    roptmp2 = ctxAB.enter_context(nc.sbuf_tensor("roptmp2", [128, 512], F32))
    xfm_own = ctxAB.enter_context(nc.sbuf_tensor("xfm_own", [128, 16, 1024], BF16))

    ctx1a = ExitStack()
    ropestg = ctx1a.enter_context(nc.sbuf_tensor("ropestg", [128, 2, T], F32))
    v_rope_add = 0
    for (tbl, dst) in ((d_ropeC, ropeC), (d_ropeS, ropeS)):
        wait("sync", "ve", v_rope_add)
        v_l = inc("srope", 16)
        emit("sync", lambda e, t=tbl: e.dma_start(
            out=ropestg[:], in_=t.rearrange("h p t -> p h t")).then_inc(sems["srope"], 16))
        wait("vector", "srope", v_l)
        v_rope_add = inc("ve")
        emit("vector", lambda e, d=dst: e.tensor_add(
            d[:], ropestg[:, 0, :], ropestg[:, 1, :]).then_inc(sems["ve"], 1))
    ctx1a.close()

    ctx1a = ExitStack()
    xstage = [ctx1a.enter_context(nc.sbuf_tensor(f"xstage{i}", [128, C], F32)) for i in range(2)]
    lnstage = [ctx1a.enter_context(nc.sbuf_tensor(f"lnstage{i}", [128, C], BF16)) for i in range(2)]
    xfm = [ctx1a.enter_context(nc.sbuf_tensor(f"xfm{i}", [128, 16, 512], BF16)) for i in range(2)]
    wk_sb = ctx1a.enter_context(nc.sbuf_tensor("wk_sb", [128, 16, 512], BF16))
    wv_sb = ctx1a.enter_context(nc.sbuf_tensor("wv_sb", [128, 16, 512], BF16))
    st1 = ctx1a.enter_context(nc.sbuf_tensor("st1", [128, 2, 8], F32))
    psT = [ctx1a.enter_context(nc.psum_tensor(f"psT{i}", [128, 4, 128], BF16)) for i in range(4)]
    psK = [ctx1a.enter_context(nc.psum_tensor(f"psK{i}", [128, 512], F32)) for i in range(2)]
    psV = [ctx1a.enter_context(nc.psum_tensor(f"psV{i}", [128, 512], F32)) for i in range(2)]

    w_kv = []
    for (d_w, sb) in ((d_wk, wk_sb), (d_wv, wv_sb)):
        wait("gpsimd", "sws", cur("sws"))
        v = inc("sws", 16)
        w_kv.append(v)
        emit("gpsimd", lambda e, d=d_w, s=sb: e.dma_start(
            out=s[:], in_=d.rearrange("(dc p) f -> p dc f", p=128)).then_inc(sems["sws"], 16))

    ln_done, tp_done, evac_done = {}, {}, {}
    kv_mm_done, rope_done, vevac_done = {}, {}, {}

    for c in range(NCH):
        cb = c % 2
        for t in range(4):
            gt = 4 * c + t
            wait("sync", "ve", v_rope_add if gt == 0 else 0)
            wait("sync", "ve", ln_done.get(gt - 2, 0))
            lane = "sx%d" % (gt % 2)
            v_x = inc(lane, 16)
            emit("sync", lambda e, gt=gt, lane=lane: e.dma_start(
                out=xstage[gt % 2][:],
                in_=d_x[gt * 128:(gt + 1) * 128, :]).then_inc(sems[lane], 16))
            wait("vector", lane, v_x)
            v_rs = inc("ve")
            emit("vector", lambda e, gt=gt: e.reduce_sum(
                st1[:, gt % 2, 0:1], xstage[gt % 2][:], axis=AX.X).then_inc(sems["ve"], 1))
            wait("scalar", lane, v_x)
            wait("scalar", "ve", ln_done.get(gt - 2, 0))
            wait("scalar", "mm", tp_done.get(gt - 2, 0))
            v_sq = inc("sc")
            emit("scalar", lambda e, gt=gt: e.activation(
                lnstage[gt % 2][:], xstage[gt % 2][:], AF.Square,
                accum_out=st1[:, gt % 2, 1:2]).then_inc(sems["sc"], 1))
            wait("vector", "sc", v_sq)
            wait("vector", "ve", v_rs)
            v_mean = inc("ve")
            emit("vector", lambda e, gt=gt: e.tensor_scalar(
                st1[:, gt % 2, 2:3], st1[:, gt % 2, 0:1], 1.0 / C, None,
                op0=ALU.mult).then_inc(sems["ve"], 1))
            wait("vector", "ve", v_mean)
            v_msq = inc("ve")
            emit("vector", lambda e, gt=gt: e.tensor_mul(
                st1[:, gt % 2, 3:4], st1[:, gt % 2, 2:3],
                st1[:, gt % 2, 2:3]).then_inc(sems["ve"], 1))
            wait("vector", "ve", v_msq)
            v_var = inc("ve")
            emit("vector", lambda e, gt=gt: e.scalar_tensor_tensor(
                st1[:, gt % 2, 4:5], st1[:, gt % 2, 1:2], 1.0 / C, st1[:, gt % 2, 3:4],
                op0=ALU.mult, op1=ALU.subtract).then_inc(sems["ve"], 1))
            wait("scalar", "ve", v_var)
            v_sd = inc("sc")
            emit("scalar", lambda e, gt=gt: e.activation(
                st1[:, gt % 2, 5:6], st1[:, gt % 2, 4:5], AF.Sqrt,
                bias=epsb[:, 0:1]).then_inc(sems["sc"], 1))
            wait("vector", "sc", v_sd)
            v_rc2 = inc("ve")
            emit("vector", lambda e, gt=gt: e.reciprocal(
                st1[:, gt % 2, 6:7], st1[:, gt % 2, 5:6]).then_inc(sems["ve"], 1))
            wait("vector", "mm", tp_done.get(gt - 2, 0))
            wait("vector", "ve", v_rc2)
            v_ln = inc("ve")
            ln_done[gt] = v_ln
            emit("vector", lambda e, gt=gt: e.tensor_scalar(
                lnstage[gt % 2][:], xstage[gt % 2][:], st1[:, gt % 2, 2:3], st1[:, gt % 2, 6:7],
                op0=ALU.subtract, op1=ALU.mult).then_inc(sems["ve"], 1))
            wait("tensor", "sid", v_id)
            wait("tensor", "ve", v_ln)
            for g4 in range(4):
                gi = gt * 4 + g4
                par = gi % 4
                wait("tensor", "ve", evac_done.get(gi - 4, 0))
                for j in range(4):
                    dc = g4 * 4 + j
                    def tp(e, par=par, j=j, dc=dc, gt=gt, last=(j == 3)):
                        ins = e.transpose(psT[par][:, j, :],
                                          lnstage[gt % 2][:, dc * 128:(dc + 1) * 128], identb[:])
                        if last:
                            ins.then_inc(sems["mm"], 1)
                    emit("tensor", tp)
                v_tp = inc("mm")
                wait("vector", "mm", v_tp)
                wait("vector", "mm", kv_mm_done.get(c - 2, 0))
                v_ev = inc("ve")
                evac_done[gi] = v_ev
                emit("vector", lambda e, cb=cb, g4=g4, t=t, par=par: e.tensor_copy(
                    xfm[cb][:, g4 * 4:(g4 + 1) * 4, t * 128:(t + 1) * 128],
                    psT[par][:]).then_inc(sems["ve"], 1))
            tp_done[gt] = cur("mm")
        chunk_evac = evac_done[(4 * c + 3) * 4 + 3]
        wait("vector", "ve", evac_done[(4 * c) * 4 + 3])
        emit("vector", lambda e, cb=cb, c=c: e.tensor_copy(
            xfm_own[:, :, c * 128:(c + 1) * 128], xfm[cb][:, :, 0:128]))
        wait("tensor", "sws", w_kv[0])
        wait("tensor", "ve", chunk_evac)
        for fc in range(4):
            prev = (c, fc - 2) if fc >= 2 else (c - 1, fc + 2)
            lastro = (c, fc - 1) if fc >= 1 else (c - 1, 3)
            wait("tensor", "ve", rope_done.get(prev, 0))
            for dc in range(16):
                def kmm(e, cb=cb, fc=fc, dc=dc):
                    ins = e.matmul(psK[fc % 2][:],
                                   wk_sb[:, dc, fc * 128:(fc + 1) * 128],
                                   xfm[cb][:, dc, :],
                                   start=(dc == 0), stop=(dc == 15))
                    if dc == 15:
                        ins.then_inc(sems["mm"], 1)
                emit("tensor", kmm)
            v_kmm = inc("mm")
            wait("scalar", "mm", v_kmm)
            wait("scalar", "ve", rope_done.get(lastro, 0))
            v_rc = inc("sc")
            emit("scalar", lambda e, fc=fc: e.copy(roptmp[0:64, :], psK[fc % 2][64:128, :]))
            emit("scalar", lambda e, fc=fc: e.copy(
                roptmp[64:128, :], psK[fc % 2][0:64, :]).then_inc(sems["sc"], 1))
            wait("vector", "mm", v_kmm)
            wait("vector", "sc", v_rc)
            emit("vector", lambda e, c=c, fc=fc: e.tensor_mul(
                roptmp2[:], psK[fc % 2][:], ropeC[:, c * 512:(c + 1) * 512]))
            v_rm = inc("ve")
            emit("vector", lambda e, c=c: e.tensor_mul(
                roptmp[:], roptmp[:], ropeS[:, c * 512:(c + 1) * 512]).then_inc(sems["ve"], 1))
            wait("vector", "ve", v_rm)
            v_rp = inc("ve")
            rope_done[(c, fc)] = v_rp
            emit("vector", lambda e, c=c, fc=fc: e.tensor_add(
                k_sb[:, fc, c * 512:(c + 1) * 512],
                roptmp2[:], roptmp[:]).then_inc(sems["ve"], 1))
        wait("tensor", "sws", w_kv[1])
        for tt in range(4):
            prev = (c, tt - 2) if tt >= 2 else (c - 1, tt + 2)
            wait("tensor", "ve", vevac_done.get(prev, 0))
            for dc in range(16):
                def vmm(e, cb=cb, tt=tt, dc=dc):
                    ins = e.matmul(psV[tt % 2][:],
                                   xfm[cb][:, dc, tt * 128:(tt + 1) * 128],
                                   wv_sb[:, dc, :],
                                   start=(dc == 0), stop=(dc == 15))
                    if dc == 15:
                        ins.then_inc(sems["mm"], 1)
                emit("tensor", vmm)
            v_vmm = inc("mm")
            wait("vector", "mm", v_vmm)
            v_vev = inc("ve")
            vevac_done[(c, tt)] = v_vev
            emit("vector", lambda e, c=c, tt=tt: e.tensor_copy(
                v_sb[:, c * 4 + tt, :, 0:128],
                psV[tt % 2][:].rearrange("p (h d) -> p h d", h=4)).then_inc(sems["ve"], 1))
        kv_mm_done[c] = cur("mm")
    v_1a_ve = cur("ve")
    v_1a_sc = cur("sc")
    ctx1a.close()

    ctx1b = ExitStack()
    wqring = [ctx1b.enter_context(nc.sbuf_tensor(f"wqring{i}", [128, 16, 512], BF16)) for i in range(2)]
    psQ = [ctx1b.enter_context(nc.psum_tensor(f"psQ{i}", [128, 512], F32)) for i in range(4)]
    q_mm_done, qrope_done = {}, {}

    def emit_wq_load(fb):
        wait("gpsimd", "mm", q_mm_done.get(fb - 2, 0))
        lane = "swq%d" % (fb % 2)
        v = inc(lane, 16)
        emit("gpsimd", lambda e, fb=fb, lane=lane: e.dma_start(
            out=wqring[fb % 2][:],
            in_=d_wq[:, fb * 512:(fb + 1) * 512].rearrange("(dc p) f -> p dc f", p=128)
        ).then_inc(sems[lane], 16))
        return v

    wait("gpsimd", "ve", v_1a_ve)
    wait("gpsimd", "sc", v_1a_sc)
    wait("gpsimd", "mm", kv_mm_done[NCH - 1])
    wait("gpsimd", "sx0", cur("sx0"))
    wait("gpsimd", "sx1", cur("sx1"))
    wq_ld = {0: emit_wq_load(0), 1: emit_wq_load(1)}
    for fb in range(4):
        wait("tensor", "swq%d" % (fb % 2), wq_ld[fb])
        for fcc in range(4):
            fc = fb * 4 + fcc
            for tc in range(2):
                u = fc * 2 + tc
                wait("tensor", "ve", v_1a_ve)
                wait("tensor", "sc", v_1a_sc)
                wait("tensor", "ve", qrope_done.get(u - 4, 0))
                for dc in range(16):
                    def qmm(e, fb=fb, fcc=fcc, tc=tc, dc=dc, u=u):
                        ins = e.matmul(psQ[u % 4][:],
                                       wqring[fb % 2][:, dc, fcc * 128:(fcc + 1) * 128],
                                       xfm_own[:, dc, tc * 512:(tc + 1) * 512],
                                       start=(dc == 0), stop=(dc == 15))
                        if dc == 15:
                            ins.then_inc(sems["mm"], 1)
                    emit("tensor", qmm)
                v_qmm = inc("mm")
                wait("scalar", "mm", v_qmm)
                wait("scalar", "ve", qrope_done.get(u - 1, 0))
                v_rc = inc("sc")
                emit("scalar", lambda e, u=u: e.copy(roptmp[0:64, :], psQ[u % 4][64:128, :]))
                emit("scalar", lambda e, u=u: e.copy(
                    roptmp[64:128, :], psQ[u % 4][0:64, :]).then_inc(sems["sc"], 1))
                wait("vector", "mm", v_qmm)
                wait("vector", "sc", v_rc)

                def ropeap(tb, tc=tc):
                    return bass.AP(tb, tc * 2048, [[T, 128], [512, 4], [1, 128]])
                emit("vector", lambda e, u=u, ap=ropeap(ropeC): e.tensor_mul(
                    roptmp2[:], psQ[u % 4][:], ap))
                v_rm = inc("ve")
                emit("vector", lambda e, ap=ropeap(ropeS): e.tensor_mul(
                    roptmp[:], roptmp[:], ap).then_inc(sems["ve"], 1))
                wait("vector", "ve", v_rm)
                v_rp = inc("ve")
                qrope_done[u] = v_rp
                emit("vector", lambda e, fc=fc, tc=tc: e.tensor_add(
                    q_sb[:, fc, tc * 512:(tc + 1) * 512],
                    roptmp2[:], roptmp[:]).then_inc(sems["ve"], 1))
        q_mm_done[fb] = cur("mm")
        if fb + 2 < 4:
            wq_ld[fb + 2] = emit_wq_load(fb + 2)
    barrier()
    if DBG:
        inc("scr", 16)
        emit("sync", lambda e: e.dma_start(out=ddbg["xfm_own"], in_=xfm_own[:]).then_inc(sems["scr"], 16))
    if DBG:
        inc("scr", 16)
        emit("sync", lambda e: e.dma_start(out=ddbg["k_sb"], in_=k_sb[:]).then_inc(sems["scr"], 16))
    if DBG:
        inc("scr", 16)
        emit("sync", lambda e: e.dma_start(out=ddbg["v_sb"], in_=v_sb[:]).then_inc(sems["scr"], 16))
    if DBG:
        inc("scr", 16)
        emit("sync", lambda e: e.dma_start(out=ddbg["q_sb"], in_=q_sb[:]).then_inc(sems["scr"], 16))
    ctx1b.close()
    ctxAB.close()

    ctxW = ExitStack()
    wpring = [ctxW.enter_context(nc.sbuf_tensor(f"wpring{i}", [128, 16, 512], BF16)) for i in range(2)]
    ctxC = ExitStack()
    maskA = ctxC.enter_context(nc.sbuf_tensor("maskA_s", [128, NCH, 512], BF16))
    maskB = ctxC.enter_context(nc.sbuf_tensor("maskB_s", [128, NCH, 4, 128], BF16))
    pt_sb = [ctxC.enter_context(nc.sbuf_tensor(f"pt{i}", [128, NB, 128], BF16)) for i in range(2)]
    y_sb = [ctxC.enter_context(nc.sbuf_tensor(f"ysb{i}", [128, C], BF16)) for i in range(2)]
    rinvb = ctxC.enter_context(nc.sbuf_tensor("rinvb", [128, 1], F32))

    v_mA = inc("smk", 16)
    emit("sync", lambda e: e.dma_start(out=maskA[:], in_=d_maskA).then_inc(sems["smk"], 16))
    wait("sync", "smk", v_mA)
    v_mB = inc("smk", 16)
    emit("sync", lambda e: e.dma_start(out=maskB[:], in_=d_maskB).then_inc(sems["smk"], 16))


    ctxCB = ExitStack()
    psB = [ctxCB.enter_context(nc.psum_tensor(f"psB{i}", [128, 512], F32)) for i in range(4)]
    psY = [ctxCB.enter_context(nc.psum_tensor(f"psY{i}", [128, 129], F32)) for i in range(2)]
    psYT = [ctxCB.enter_context(nc.psum_tensor(f"psYT{i}", [128, 4, 128], BF16)) for i in range(2)]

    wp_mm_done, x2_done = {}, {}

    def emit_wp_load(fb):
        wait("gpsimd", "mm", wp_mm_done.get(fb - 2, 0))
        lane = "swp%d" % (fb % 2)
        v = inc(lane, 16)
        emit("gpsimd", lambda e, fb=fb, lane=lane: e.dma_start(
            out=wpring[fb % 2][:],
            in_=d_wp[:, fb * 512:(fb + 1) * 512].rearrange("(dc p) f -> p dc f", p=128)
        ).then_inc(sems[lane], 16))
        return v

    wp_ld = {0: emit_wp_load(0), 1: emit_wp_load(1)}
    wait("vector", "smk", v_mB)
    seqB = 0
    exp_done, yev_done, av_done, ytr_done = {}, {}, {}, {}
    ytr_last = {}
    for oc in range(NCH):
        for h in range(16):
            u = oc * 16 + h
            nk = oc + 1
            wait("scalar", "mm", av_done.get(u - 2, 0))
            for kc in range(nk):
                bslot = seqB % 4
                wait("tensor", "sc", exp_done.get(seqB - 4, 0))
                for j in range(4):
                    def bmm(e, bslot=bslot, h=h, oc=oc, kc=kc, j=j):
                        ins = e.matmul(psB[bslot][:, j * 128:(j + 1) * 128],
                                       k_sb[:, h // 4, (kc * 4 + j) * 128:(kc * 4 + j + 1) * 128],
                                       q_sb[:, h, oc * 128:(oc + 1) * 128],
                                       start=(j == 0), stop=(j == 3))
                        if j == 3:
                            ins.then_inc(sems["mm"], 1)
                    emit("tensor", bmm)
                v_bmm = inc("mm")
                if kc == nk - 1:
                    wait("vector", "mm", v_bmm)
                    v_msk = inc("ve")
                    emit("vector", lambda e, bslot=bslot, oc=oc: e.tensor_add(
                        psB[bslot][:], psB[bslot][:],
                        maskB[:, oc, :, :]).then_inc(sems["ve"], 1))
                    wait("scalar", "ve", v_msk)
                else:
                    wait("scalar", "mm", v_bmm)
                v_exp = inc("sc")
                exp_done[seqB] = v_exp
                emit("scalar", lambda e, bslot=bslot, kc=kc, ub=u % 2: e.activation(
                    pt_sb[ub][:, kc * 4:(kc + 1) * 4, :], psB[bslot][:], AF.Exp,
                    scale=float(SCALE)).then_inc(sems["sc"], 1))
                seqB += 1
            wait("tensor", "ve", yev_done.get(u - 2, 0))
            if nk > 1:
                wait("tensor", "sc", exp_done[seqB - 2])
            for j in range(4 * (nk - 1)):
                def ymm1(e, u=u, j=j, h=h):
                    e.matmul(psY[u % 2][:],
                             pt_sb[u % 2][:, j, :],
                             v_sb[:, j, h // 4, :],
                             start=(j == 0), stop=False)
                emit("tensor", ymm1)
            wait("tensor", "sc", exp_done[seqB - 1])
            for j in range(4 * (nk - 1), 4 * nk):
                def ymm(e, u=u, j=j, h=h, last=(j == 4 * nk - 1), first=(j == 0)):
                    ins = e.matmul(psY[u % 2][:],
                                   pt_sb[u % 2][:, j, :],
                                   v_sb[:, j, h // 4, :],
                                   start=first, stop=last)
                    if last:
                        ins.then_inc(sems["mm"], 1)
                emit("tensor", ymm)
            v_ymm = inc("mm")
            av_done[u] = v_ymm
            wait("vector", "mm", v_ymm)
            wait("vector", "mm", ytr_last.get(oc - 2, 0))
            v_ri = inc("ve")
            emit("vector", lambda e, u=u: e.reciprocal(
                rinvb[:], psY[u % 2][:, 128:129]).then_inc(sems["ve"], 1))
            wait("vector", "ve", v_ri)
            v_ye = inc("ve")
            yev_done[u] = v_ye
            emit("vector", lambda e, u=u, oc=oc, h=h: e.tensor_scalar(
                y_sb[oc % 2][:, h * 128:(h + 1) * 128], psY[u % 2][:, 0:128],
                rinvb[:, 0:1], None, op0=ALU.mult).then_inc(sems["ve"], 1))
        wait("tensor", "ve", yev_done[oc * 16 + 15])
        for g4 in range(4):
            gi = oc * 4 + g4
            par = gi % 2
            wait("tensor", "ve", ytr_done.get(gi - 2, 0))
            for j in range(4):
                cc = g4 * 4 + j
                def ytp(e, par=par, oc=oc, cc=cc, j=j, last=(j == 3)):
                    ins = e.transpose(psYT[par][:, j, :],
                                      y_sb[oc % 2][:, cc * 128:(cc + 1) * 128], identb[:])
                    if last:
                        ins.then_inc(sems["mm"], 1)
                emit("tensor", ytp)
            v_yt = inc("mm")
            wait("vector", "mm", v_yt)
            v_yte = inc("ve")
            ytr_done[gi] = v_yte
            emit("vector", lambda e, oc=oc, g4=g4, par=par: e.tensor_copy(
                yfm[:, oc, g4 * 4:(g4 + 1) * 4, :], psYT[par][:]).then_inc(sems["ve"], 1))
        ytr_last[oc] = cur("mm")
    barrier()
    if DBG:
        inc("scr", 16)
        emit("sync", lambda e: e.dma_start(out=ddbg["yfm"], in_=yfm[:]).then_inc(sems["scr"], 16))
    ctxCB.close()
    ctxC.close()

    ctxD = ExitStack()
    xorig = [ctxD.enter_context(nc.sbuf_tensor(f"xorig{i}", [128, 512], F32)) for i in range(2)]
    psW = [ctxD.enter_context(nc.psum_tensor(f"psW{i}", [128, 512], F32)) for i in range(4)]

    for fb in range(4):
        wait("tensor", "swp%d" % (fb % 2), wp_ld[fb])
        for oc in range(NCH):
            k = fb * NCH + oc
            wait("sync", "ve", x2_done.get(k - 2, 0))
            lane = "sxo%d" % (k % 2)
            v_xo = inc(lane, 16)
            emit("sync", lambda e, oc=oc, fb=fb, k=k, lane=lane: e.dma_start(
                out=xorig[k % 2][:],
                in_=d_x[oc * 512:oc * 512 + 128, fb * 512:(fb + 1) * 512]
            ).then_inc(sems[lane], 16))
            wait("tensor", "ve", x2_done.get(k - 4, 0))
            for cc in range(16):
                def pmm(e, fb=fb, oc=oc, cc=cc, k=k):
                    ins = e.matmul(psW[k % 4][:],
                                   yfm[:, oc, cc, :],
                                   wpring[fb % 2][:, cc, :],
                                   start=(cc == 0), stop=(cc == 15))
                    if cc == 15:
                        ins.then_inc(sems["mm"], 1)
                emit("tensor", pmm)
            v_pmm = inc("mm")
            wait("vector", "mm", v_pmm)
            wait("vector", lane, v_xo)
            v_x2 = inc("ve")
            x2_done[k] = v_x2
            emit("vector", lambda e, oc=oc, fb=fb, k=k: e.tensor_add(
                x2f[:, oc, fb * 512:(fb + 1) * 512], psW[k % 4][:],
                xorig[k % 2][:]).then_inc(sems["ve"], 1))
        wp_mm_done[fb] = cur("mm")
        if fb + 2 < 4:
            wp_ld[fb + 2] = emit_wp_load(fb + 2)
    barrier()
    if DBG:
        inc("scr", 16)
        emit("sync", lambda e: e.dma_start(out=ddbg["x2f"], in_=x2f[:]).then_inc(sems["scr"], 16))
    ctxD.close()
    ctxW.close()

    ctxE = ExitStack()
    ln2stage = ctxE.enter_context(nc.sbuf_tensor("ln2stage", [128, C], BF16))
    st2 = ctxE.enter_context(nc.sbuf_tensor("st2", [128, 2, 8], F32))
    psT2 = [ctxE.enter_context(nc.psum_tensor(f"psT2{i}", [128, 4, 128], BF16)) for i in range(4)]

    ln2_done, tp2_done, ev2_done = {}, {}, {}
    for oc in range(NCH):
        v_rs2 = inc("ve")
        emit("vector", lambda e, oc=oc: e.reduce_sum(
            st2[:, oc % 2, 0:1], x2f[:, oc, :], axis=AX.X).then_inc(sems["ve"], 1))
        wait("scalar", "ve", ln2_done.get(oc - 2, 0))
        wait("scalar", "mm", tp2_done.get(oc - 1, 0))
        wait("scalar", "sac", (oc) * 16)
        v_sq = inc("sc")
        emit("scalar", lambda e, oc=oc: e.activation(
            ln2stage[:], x2f[:, oc, :], AF.Square,
            accum_out=st2[:, oc % 2, 1:2]).then_inc(sems["sc"], 1))
        wait("vector", "sc", v_sq)
        wait("vector", "ve", v_rs2)
        v_mean = inc("ve")
        emit("vector", lambda e, oc=oc: e.tensor_scalar(
            st2[:, oc % 2, 2:3], st2[:, oc % 2, 0:1], 1.0 / C, None,
            op0=ALU.mult).then_inc(sems["ve"], 1))
        wait("vector", "ve", v_mean)
        v_msq = inc("ve")
        emit("vector", lambda e, oc=oc: e.tensor_mul(
            st2[:, oc % 2, 3:4], st2[:, oc % 2, 2:3],
            st2[:, oc % 2, 2:3]).then_inc(sems["ve"], 1))
        wait("vector", "ve", v_msq)
        v_var = inc("ve")
        emit("vector", lambda e, oc=oc: e.scalar_tensor_tensor(
            st2[:, oc % 2, 4:5], st2[:, oc % 2, 1:2], 1.0 / C, st2[:, oc % 2, 3:4],
            op0=ALU.mult, op1=ALU.subtract).then_inc(sems["ve"], 1))
        wait("scalar", "ve", v_var)
        v_sd = inc("sc")
        emit("scalar", lambda e, oc=oc: e.activation(
            st2[:, oc % 2, 5:6], st2[:, oc % 2, 4:5], AF.Sqrt,
            bias=epsb[:, 0:1]).then_inc(sems["sc"], 1))
        wait("vector", "sc", v_sd)
        v_rc2 = inc("ve")
        emit("vector", lambda e, oc=oc: e.reciprocal(
            st2[:, oc % 2, 6:7], st2[:, oc % 2, 5:6]).then_inc(sems["ve"], 1))
        wait("vector", "mm", tp2_done.get(oc - 1, 0))
        wait("vector", "ve", v_rc2)
        v_ln = inc("ve")
        ln2_done[oc] = v_ln
        emit("vector", lambda e, oc=oc: e.tensor_scalar(
            ln2stage[:], x2f[:, oc, :], st2[:, oc % 2, 2:3], st2[:, oc % 2, 6:7],
            op0=ALU.subtract, op1=ALU.mult).then_inc(sems["ve"], 1))
        wait("sync", "ve", v_ln)
        inc("scr", 16)
        emit("sync", lambda e, oc=oc: e.dma_start(
            out=d_x2[oc * 128:(oc + 1) * 128, :], in_=x2f[:, oc, :]).then_inc(sems["scr"], 16))
        wait("sync", "mm", tp2_done.get(oc, 0) if False else 0)
        inc("sac", 16)
        emit("sync", lambda e, oc=oc: e.dma_start(
            out=d_ln2[oc * 128:(oc + 1) * 128, :], in_=ln2stage[:]).then_inc(sems["sac"], 16))
        wait("tensor", "ve", v_ln)
        for g4 in range(4):
            gi = oc * 4 + g4
            par = gi % 4
            wait("tensor", "ve", ev2_done.get(gi - 4, 0))
            for j in range(4):
                dc = g4 * 4 + j
                def tp2(e, par=par, dc=dc, j=j, last=(j == 3)):
                    ins = e.transpose(psT2[par][:, j, :],
                                      ln2stage[:, dc * 128:(dc + 1) * 128], identb[:])
                    if last:
                        ins.then_inc(sems["mm"], 1)
                emit("tensor", tp2)
            v_tp = inc("mm")
            wait("vector", "mm", v_tp)
            v_ev = inc("ve")
            ev2_done[gi] = v_ev
            emit("vector", lambda e, oc=oc, g4=g4, par=par: e.tensor_copy(
                xfm2[:, g4 * 4:(g4 + 1) * 4, oc * 128:(oc + 1) * 128],
                psT2[par][:]).then_inc(sems["ve"], 1))
        tp2_done[oc] = cur("mm")
    v_scr_all = cur("scr")
    wait("sync", "scr", v_scr_all)
    barrier()
    if DBG:
        inc("scr", 16)
        emit("sync", lambda e: e.dma_start(out=ddbg["xfm2"], in_=xfm2[:]).then_inc(sems["scr"], 16))
    ctxE.close()

    ctxF = ExitStack()
    ring = [ctxF.enter_context(nc.sbuf_tensor(f"ring{i}", [128, 16, 512], BF16)) for i in range(5)]
    silu = [ctxF.enter_context(nc.sbuf_tensor(f"silu{i}", [128, 512], BF16)) for i in range(2)]
    rtr = ctxF.enter_context(nc.sbuf_tensor("rtr", [128, 32], F32))
    zap = ctxF.enter_context(nc.sbuf_tensor("zap", [128, 16], F32))
    sel = ctxF.enter_context(nc.sbuf_tensor("sel", [128, 16], F32))
    wr_sb = ctxF.enter_context(nc.sbuf_tensor("wr_sb", [128, 16, 16], BF16))
    cwpad = ctxF.enter_context(nc.sbuf_tensor("cwpad", [128, NCH, 64], F32))
    ekf = ctxF.enter_context(nc.sbuf_tensor("ekf", [128, NCH, 4], F32))
    idxi = ctxF.enter_context(nc.sbuf_tensor("idxi", [128, 8], U32))
    oh = ctxF.enter_context(nc.sbuf_tensor("oh", [128, NCH, 16], BF16))
    ohk = ctxF.enter_context(nc.sbuf_tensor("ohk", [128, NCH, 4, 16], BF16))
    rsK = ctxF.enter_context(nc.sbuf_tensor("rsK", [128, 16], F32))
    oht2 = ctxF.enter_context(nc.sbuf_tensor("oht2", [128, 16], BF16))
    rank_tok = ctxF.enter_context(nc.sbuf_tensor("rank_tok", [128, NCH, 16], F32))
    ranksel = ctxF.enter_context(nc.sbuf_tensor("ranksel", [128, NCH, 4], F32))
    slotf = ctxF.enter_context(nc.sbuf_tensor("slotf", [128, 32], F32))
    slotm = ctxF.enter_context(nc.sbuf_tensor("slotm", [128, 32], F32))
    slotg = ctxF.enter_context(nc.sbuf_tensor("slotg", [128, 32], F32))
    zero32 = ctxF.enter_context(nc.sbuf_tensor("zero32", [128, 32], F32))
    slotu = ctxF.enter_context(nc.sbuf_tensor("slotu", [128, 32], U32))
    erow_sb = ctxF.enter_context(nc.sbuf_tensor("erow_sb", [128, 16], F32))
    ltu_sb = ctxF.enter_context(nc.sbuf_tensor("ltu_sb", [128, 128], BF16))
    lton_sb = ctxF.enter_context(nc.sbuf_tensor("lton_sb", [128, 128], BF16))
    tid_sb = ctxF.enter_context(nc.sbuf_tensor("tid_sb", [128, 8], U32))
    identf = ctxF.enter_context(nc.sbuf_tensor("identf2", [128, 128], F32))
    pfil = ctxF.enter_context(nc.sbuf_tensor("pfil", [128, 64], U32))
    bidx_sb = ctxF.enter_context(nc.sbuf_tensor("bidx_sb", [128, 64], U32))

    v_cst = inc("smk", 16)
    emit("sync", lambda e: e.dma_start(out=tid_sb[:], in_=d_tid).then_inc(sems["smk"], 16))
    wait("sync", "smk", v_cst)
    v_cst = inc("smk", 16)
    emit("sync", lambda e: e.dma_start(out=erow_sb[:], in_=d_erow).then_inc(sems["smk"], 16))
    wait("sync", "smk", v_cst)
    v_cst = inc("smk", 16)
    emit("sync", lambda e: e.dma_start(out=ltu_sb[:], in_=d_ltu).then_inc(sems["smk"], 16))
    v_cst_all = v_cst
    emit("vector", lambda e: e.memset(lton_sb[:], 1.0))
    emit("vector", lambda e: e.memset(zero32[:], 0.0))
    emit("vector", lambda e: e.memset(cwpad[:], 0.0))
    emit("vector", lambda e: e.memset(pfil[:], 1024))
    v_ms = inc("ve")
    emit("vector", lambda e: e.tensor_copy(identf[:], identb[:]).then_inc(sems["ve"], 1))
    wait("gpsimd", "ve", v_ms)
    inc("sbx", 16)
    emit("gpsimd", lambda e: e.dma_start(
        out=d_ln2[1024:1025, :].rearrange("o (p f) -> (o p) f", p=128),
        in_=zero32[:, 0:16]).then_inc(sems["sbx"], 16))
    wait("sync", "ve", v_ms)
    wait("sync", "sac", cur("sac"))
    inc("sac", 16)
    emit("sync", lambda e: e.dma_start(
        out=d_cwsc[1024:1025, :].rearrange("o (p f) -> (o p) f", p=2),
        in_=zero32[0:2, 0:32]).then_inc(sems["sac"], 16))
    wait("gpsimd", "ve", v_ms)
    v_pf = inc("sbx", 16)
    emit("gpsimd", lambda e: e.dma_start(
        out=d_bidx.rearrange("(p g) o -> p (g o)", p=128), in_=pfil[:]).then_inc(sems["sbx"], 16))

    ctxF1 = ExitStack()
    gbuf = [ctxF1.enter_context(nc.sbuf_tensor(f"gbuf{i}", [128, 8, 512], BF16)) for i in range(2)]
    psFF = [ctxF1.enter_context(nc.psum_tensor(f"psFF{i}", [128, 512], F32)) for i in range(8)]

    bank_last = {i: ("ve", 0) for i in range(8)}

    def wait_bank(eng, bk):
        sem, val = bank_last[bk]
        wait(eng, sem, val)

    wait("gpsimd", "sws", cur("sws"))
    v_wr = inc("sws", 16)
    emit("gpsimd", lambda e: e.dma_start(
        out=wr_sb[:], in_=d_wr.rearrange("(dc p) f -> p dc f", p=128)).then_inc(sems["sws"], 16))
    wait("tensor", "sws", v_wr)
    for tt in range(NCH):
        for dc in range(16):
            def rmm(e, tt=tt, dc=dc, last=(tt == NCH - 1 and dc == 15)):
                ins = e.matmul(psFF[7][:, tt * 16:(tt + 1) * 16],
                               xfm2[:, dc, tt * 128:(tt + 1) * 128],
                               wr_sb[:, dc, :],
                               start=(tt == 0 and dc == 0), stop=last)
                if last:
                    ins.then_inc(sems["mm"], 1)
            emit("tensor", rmm)
    v_rmm = inc("mm")
    wait("vector", "mm", v_rmm)
    wait("vector", "smk", v_cst_all)
    for tt in range(NCH):
        v1 = inc("ve")
        emit("vector", lambda e, tt=tt: e.tensor_copy(
            rtr[:, 0:16], psFF[7][:, tt * 16:(tt + 1) * 16]).then_inc(sems["ve"], 1))
        wait("vector", "ve", v1)
        v2 = inc("ve")
        emit("vector", lambda e: e.max(out=rtr[:, 16:24], in_=rtr[:, 0:16]).then_inc(sems["ve"], 1))
        wait("vector", "ve", v2)
        v2i = inc("ve")
        emit("vector", lambda e: e.max_index(
            out=idxi[:], in_max=rtr[:, 16:24], in_values=rtr[:, 0:16]).then_inc(sems["ve"], 1))
        wait("vector", "ve", v2i)
        v2c = inc("ve")
        emit("vector", lambda e, tt=tt: e.tensor_copy(
            ekf[:, tt, :], idxi[:, 0:4]).then_inc(sems["ve"], 1))
        wait("vector", "ve", v2c)
        for k in range(4):
            v_t = inc("ve")
            emit("vector", lambda e, tt=tt, k=k: e.tensor_scalar(
                ohk[:, tt, k, :], erow_sb[:], ekf[:, tt, k:k + 1], None,
                op0=ALU.is_equal).then_inc(sems["ve"], 1))
        wait("vector", "ve", cur("ve"))
        v_oh = inc("ve")
        emit("vector", lambda e, tt=tt: e.tensor_add(
            oh[:, tt, :], ohk[:, tt, 0, :], ohk[:, tt, 1, :]).then_inc(sems["ve"], 1))
        wait("vector", "ve", v_oh)
        v_oh2 = inc("ve")
        emit("vector", lambda e, tt=tt: e.tensor_add(
            oht2[:], ohk[:, tt, 2, :], ohk[:, tt, 3, :]).then_inc(sems["ve"], 1))
        wait("vector", "ve", v_oh2)
        v_oh3 = inc("ve")
        emit("vector", lambda e, tt=tt: e.tensor_add(
            oh[:, tt, :], oh[:, tt, :], oht2[:]).then_inc(sems["ve"], 1))
        v3 = inc("ve")
        emit("vector", lambda e: e.memset(rtr[:, 20:24], 3e38).then_inc(sems["ve"], 1))
        wait("vector", "ve", v3)
        emit("vector", lambda e: e.match_replace(
            out=zap[:], in_to_replace=rtr[:, 16:24], in_values=rtr[:, 0:16], imm_value=NEG))
        v4 = inc("ve")
        emit("vector", lambda e: e.match_replace(
            out=zap[:], in_to_replace=rtr[:, 16:24], in_values=rtr[:, 0:16],
            imm_value=NEG).then_inc(sems["ve"], 1))
        wait("vector", "ve", v4)
        v5 = inc("ve")
        emit("vector", lambda e: e.scalar_tensor_tensor(
            zap[:], zap[:], -NEG, rtr[:, 0:16], op0=ALU.add,
            op1=ALU.subtract).then_inc(sems["ve"], 1))
        wait("vector", "ve", v5)
        v_z = inc("ve")
        emit("vector", lambda e: e.tensor_scalar(
            rtr[:, 24:25], rtr[:, 16:17], -1.0, None, op0=ALU.mult).then_inc(sems["ve"], 1))
        wait("scalar", "ve", v_z)
        v_e = inc("sc")
        emit("scalar", lambda e: e.activation(
            sel[:], zap[:], AF.Exp, bias=rtr[:, 24:25], scale=-1.0,
            accum_out=rtr[:, 25:26]).then_inc(sems["sc"], 1))
        wait("vector", "sc", v_e)
        v_ri2 = inc("ve")
        emit("vector", lambda e: e.reciprocal(rtr[:, 26:27], rtr[:, 25:26]).then_inc(sems["ve"], 1))
        wait("vector", "ve", v_ri2)
        v_cw = inc("ve")
        emit("vector", lambda e, tt=tt: e.tensor_scalar(
            cwpad[:, tt, 0:16], sel[:], rtr[:, 26:27], float(TOPK) / (TOPK + 1),
            op0=ALU.mult, op1=ALU.mult).then_inc(sems["ve"], 1))
    v_router_ve = cur("ve")
    wait("sync", "ve", v_router_ve)
    wait("sync", "sac", cur("sac"))
    inc("sac", 16)
    emit("sync", lambda e: e.dma_start(
        out=d_cwsc[0:1024, :].rearrange("(tt p) f -> p tt f", p=128),
        in_=cwpad[:]).then_inc(sems["sac"], 16))
    wait("tensor", "ve", v_router_ve)
    rk_ev = {}
    for tt in range(NCH):
        bk = 6 - (tt % 2)
        wait_bank("tensor", bk)
        for tp in range(tt + 1):
            def kmm2(e, tt=tt, tp=tp, bk=bk, last=(tp == tt)):
                ins = e.matmul(psFF[bk][:, 0:16],
                               ltu_sb[:] if tp == tt else lton_sb[:],
                               oh[:, tp, :],
                               start=(tp == 0), stop=last)
                if last:
                    ins.then_inc(sems["mm"], 1)
            emit("tensor", kmm2)
        v_k = inc("mm")
        wait("vector", "mm", v_k)
        v_ev = inc("ve")
        rk_ev[tt] = v_ev
        bank_last[bk] = ("ve", v_ev)
        emit("vector", lambda e, tt=tt, bk=bk: e.tensor_copy(
            rank_tok[:, tt, :], psFF[bk][:, 0:16]).then_inc(sems["ve"], 1))
    for tt in range(NCH):
        for k in range(4):
            v_mu = inc("ve")
            emit("vector", lambda e, tt=tt, k=k: e.tensor_mul(
                rsK[:], rank_tok[:, tt, :], ohk[:, tt, k, :]).then_inc(sems["ve"], 1))
            wait("vector", "ve", v_mu)
            v_rs = inc("ve")
            emit("vector", lambda e, tt=tt, k=k: e.reduce_sum(
                ranksel[:, tt, k:k + 1], rsK[:], axis=AX.X).then_inc(sems["ve"], 1))
            wait("vector", "ve", v_rs)
    v_ranks_done = cur("ve")
    wait("vector", "ve", v_ranks_done)
    v_sf = inc("ve")
    emit("vector", lambda e: e.scalar_tensor_tensor(
        slotf[:], ekf[:].rearrange("p a b -> p (a b)"), 512.0,
        ranksel[:].rearrange("p a b -> p (a b)"),
        op0=ALU.mult, op1=ALU.add).then_inc(sems["ve"], 1))
    wait("vector", "ve", v_sf)
    rsl = ranksel[:].rearrange("p a b -> p (a b)")
    ekl = ekf[:].rearrange("p a b -> p (a b)")
    v_c1 = inc("ve")
    emit("vector", lambda e: e.tensor_single_scalar(
        slotm[:], rsl, 128.0, op=ALU.is_ge).then_inc(sems["ve"], 1))
    v_c2 = inc("ve")
    emit("vector", lambda e: e.tensor_single_scalar(
        slotg[:], rsl, 256.0, op=ALU.is_ge).then_inc(sems["ve"], 1))
    v_c3 = inc("ve")
    emit("vector", lambda e: e.tensor_single_scalar(
        zero32[:], rsl, 384.0, op=ALU.is_ge).then_inc(sems["ve"], 1))
    wait("vector", "ve", v_c2)
    v_g1 = inc("ve")
    emit("vector", lambda e: e.tensor_add(slotm[:], slotm[:], slotg[:]).then_inc(sems["ve"], 1))
    wait("vector", "ve", v_g1)
    v_g2 = inc("ve")
    emit("vector", lambda e: e.tensor_add(slotm[:], slotm[:], zero32[:]).then_inc(sems["ve"], 1))
    wait("vector", "ve", v_g2)
    v_p = inc("ve")
    emit("vector", lambda e: e.scalar_tensor_tensor(
        slotg[:], slotm[:], -128.0, rsl, op0=ALU.mult, op1=ALU.add).then_inc(sems["ve"], 1))
    v_g = inc("ve")
    emit("vector", lambda e: e.scalar_tensor_tensor(
        slotm[:], ekl, 4.0, slotm[:], op0=ALU.mult, op1=ALU.add).then_inc(sems["ve"], 1))
    wait("vector", "ve", v_g)
    v_ff = inc("ve")
    emit("vector", lambda e: e.scalar_tensor_tensor(
        slotf[:], slotg[:], 64.0, slotm[:], op0=ALU.mult, op1=ALU.add).then_inc(sems["ve"], 1))
    wait("vector", "ve", v_ff)
    v_su = inc("ve")
    emit("vector", lambda e: e.tensor_copy(slotu[:], slotf[:]).then_inc(sems["ve"], 1))
    wait("gpsimd", "ve", v_su)
    wait("gpsimd", "sbx", v_pf)
    wait("gpsimd", "smk", 16)
    for tt in range(NCH):
        for k in range(4):
            def bsc(e, tt=tt, k=k):
                e.indirect_dma_start(
                    out=d_bidx, out_offset=IOA(slotu[:, tt * 4 + k:tt * 4 + k + 1], 0),
                    in_=tid_sb[:, tt:tt + 1], in_offset=None).then_inc(sems["sbx"], 16)
            emit("gpsimd", bsc)
            inc("sbx", 16)
    v_bidx_built = cur("sbx")
    wait("gpsimd", "sbx", v_bidx_built)
    v_bread = inc("sbx", 16)
    emit("gpsimd", lambda e: e.dma_start(
        out=bidx_sb[:], in_=d_bidx.rearrange("(p g) o -> p (g o)", p=128)).then_inc(sems["sbx"], 16))

    slot_free = {}
    ld_done = {}
    nload = [0]

    def exp_srcs(e_):
        if e_ < 16:
            w1 = d_we1[e_]; w3 = d_we3[e_]; w2 = d_we2[e_]
        else:
            s = e_ - 16
            w1 = d_ws1[:, s * EH:(s + 1) * EH]
            w3 = d_ws3[:, s * EH:(s + 1) * EH]
            w2 = d_ws2[s * EH:(s + 1) * EH, :]
        return (
            w1[:, 0:512].rearrange("(dc p) f -> p dc f", p=128),
            w3[:, 0:512].rearrange("(dc p) f -> p dc f", p=128),
            w1[:, 512:1024].rearrange("(dc p) f -> p dc f", p=128),
            w3[:, 512:1024].rearrange("(dc p) f -> p dc f", p=128),
            w2[0:512, :].rearrange("(hc p) f -> p hc f", p=128),
            w2[512:1024, :].rearrange("(hc p) f -> p hc f", p=128),
        )

    def emit_load(key, src):
        i = nload[0]
        nload[0] += 1
        s = i % 5
        if i >= 5:
            assert i - 5 in slot_free, ("ring order", i)
            wait("gpsimd", "mm", slot_free[i - 5])
        lane = "sw%d" % s
        v = inc(lane, 16)
        ld_done[key] = (v, s)
        emit("gpsimd", lambda e, src=src, s=s, lane=lane: e.dma_start(
            out=ring[s][:], in_=src).then_inc(sems[lane], 16))
        return i

    DSH = 4
    loadix = {}
    for k in range(4):
        loadix[(0, k)] = emit_load((0, k), exp_srcs(16)[k])
    gq = 0
    sil_done, gmul_done = {}, {}
    we2_mm_last = {}
    ffe_done = {}
    for e_ in range(DSH):
        ge = 4 * (e_ % 2)
        for hh in range(2):
            v1, s1 = ld_done[(e_, 2 * hh)]
            v3, s3 = ld_done[(e_, 2 * hh + 1)]
            wait("tensor", "sw%d" % s1, v1)
            wait("tensor", "sw%d" % s3, v3)
            for tc in range(2):
                for hc4 in range(4):
                    hc = hh * 4 + hc4
                    b1 = ge + (gq % 2)
                    b3 = ge + 2 + (gq % 2)
                    wait_bank("tensor", b1)
                    for dc in range(16):
                        def g1mm(e, s1=s1, hc4=hc4, dc=dc, tc=tc, b1=b1):
                            ins = e.matmul(psFF[b1][:],
                                           ring[s1][:, dc, hc4 * 128:(hc4 + 1) * 128],
                                           xfm2[:, dc, tc * 512:(tc + 1) * 512],
                                           start=(dc == 0), stop=(dc == 15))
                            if dc == 15:
                                ins.then_inc(sems["mm"], 1)
                        emit("tensor", g1mm)
                    v_m1 = inc("mm")
                    wait_bank("tensor", b3)
                    for dc in range(16):
                        def g3mm(e, s3=s3, hc4=hc4, dc=dc, tc=tc, b3=b3):
                            ins = e.matmul(psFF[b3][:],
                                           ring[s3][:, dc, hc4 * 128:(hc4 + 1) * 128],
                                           xfm2[:, dc, tc * 512:(tc + 1) * 512],
                                           start=(dc == 0), stop=(dc == 15))
                            if dc == 15:
                                ins.then_inc(sems["mm"], 1)
                        emit("tensor", g3mm)
                    v_m3 = inc("mm")
                    wait("scalar", "mm", v_m1)
                    wait("scalar", "ve", gmul_done.get(gq - 2, 0))
                    v_sl = inc("sc")
                    sil_done[gq] = v_sl
                    emit("scalar", lambda e, b1=b1, sb=gq % 2: e.activation(
                        silu[sb][:], psFF[b1][:], AF.Silu).then_inc(sems["sc"], 1))
                    bank_last[b1] = ("sc", v_sl)
                    wait("vector", "mm", v_m3)
                    wait("vector", "sc", v_sl)
                    wait("vector", "mm", we2_mm_last.get((e_ - 1, tc), 0))
                    v_gm = inc("ve")
                    gmul_done[gq] = v_gm
                    emit("vector", lambda e, b3=b3, tc=tc, hc=hc, sb=gq % 2: e.tensor_mul(
                        gbuf[tc][:, hc, :], silu[sb][:], psFF[b3][:]).then_inc(sems["ve"], 1))
                    bank_last[b3] = ("ve", v_gm)
                    gq += 1
            if hh == 0:
                slot_free[loadix[(e_, 0)]] = cur("mm")
                slot_free[loadix[(e_, 1)]] = cur("mm")
                loadix[(e_, 4)] = emit_load((e_, 4), exp_srcs(16 + e_)[4])
                loadix[(e_, 5)] = emit_load((e_, 5), exp_srcs(16 + e_)[5])
        slot_free[loadix[(e_, 2)]] = cur("mm")
        slot_free[loadix[(e_, 3)]] = cur("mm")
        if e_ + 1 < DSH:
            loadix[(e_ + 1, 0)] = emit_load((e_ + 1, 0), exp_srcs(16 + e_ + 1)[0])
            loadix[(e_ + 1, 1)] = emit_load((e_ + 1, 1), exp_srcs(16 + e_ + 1)[1])
        v2a, s2a = ld_done[(e_, 4)]
        v2b, s2b = ld_done[(e_, 5)]
        wait("tensor", "sw%d" % s2a, v2a)
        wait("tensor", "sw%d" % s2b, v2b)
        w = 0
        gq0 = gq - 16
        for tc in range(2):
            wait("tensor", "ve", gmul_done[gq0 + 8 + 4 * tc + 3])
            for tt4 in range(4):
                tt = tc * 4 + tt4
                wset = 4 * ((e_ + w + 1) % 2)
                for dc4 in range(4):
                    wait_bank("tensor", wset + dc4)
                for hc in range(8):
                    s2 = s2a if hc < 4 else s2b
                    for dc4 in range(4):
                        def w2mm(e, s2=s2, hc=hc, tt4=tt4, tc=tc, dc4=dc4, wset=wset,
                                 last=(hc == 7 and dc4 == 3)):
                            rhs = bass.AP(ring[s2], (hc % 4) * 2048 + dc4 * 512,
                                          [[8192, 128], [1, 512]])
                            ins = e.matmul(psFF[wset + dc4][:],
                                           gbuf[tc][:, hc, tt4 * 128:(tt4 + 1) * 128],
                                           rhs,
                                           start=(hc == 0), stop=(hc == 7))
                            if last:
                                ins.then_inc(sems["mm"], 1)
                        emit("tensor", w2mm)
                v_w2 = inc("mm")
                wait("vector", "mm", v_w2)
                for dc4 in range(4):
                    def fev(e, e_=e_, tt=tt, dc4=dc4, wset=wset, last=(dc4 == 3)):
                        accsl = acc[:, tt, dc4 * 512:(dc4 + 1) * 512]
                        if e_ == 0:
                            ins = e.tensor_scalar(accsl, psFF[wset + dc4][:],
                                                  1.0 / (TOPK + 1), None, op0=ALU.mult)
                        else:
                            ins = e.scalar_tensor_tensor(accsl, psFF[wset + dc4][:],
                                                         1.0 / (TOPK + 1), accsl,
                                                         op0=ALU.mult, op1=ALU.add)
                        if last:
                            ins.then_inc(sems["ve"], 1)
                    emit("vector", fev)
                v_fe = inc("ve")
                ffe_done[(e_, tt4 + 4 * tc)] = v_fe
                for dc4 in range(4):
                    bank_last[wset + dc4] = ("ve", v_fe)
                w += 1
            we2_mm_last[(e_, tc)] = cur("mm")
        slot_free[loadix[(e_, 4)]] = cur("mm")
        slot_free[loadix[(e_, 5)]] = cur("mm")
        if e_ + 1 < DSH:
            loadix[(e_ + 1, 2)] = emit_load((e_ + 1, 2), exp_srcs(16 + e_ + 1)[2])
            loadix[(e_ + 1, 3)] = emit_load((e_ + 1, 3), exp_srcs(16 + e_ + 1)[3])
    v_shared_ve = cur("ve")
    v_shared_mm = cur("mm")
    wait("sync", "ve", v_shared_ve)
    wait("sync", "sac", cur("sac"))
    acst = {}
    for tt in range(NCH):
        acst[tt] = inc("sac", 16)
        emit("sync", lambda e, tt=tt: e.dma_start(
            out=d_accd[tt * 128:(tt + 1) * 128, :], in_=acc[:, tt, :]).then_inc(sems["sac"], 16))
    v_acc_stored = cur("sac")
    ctxF1.close()

    ctxF2 = ExitStack()
    xg_tok = nc.alloc_sbuf_tensor_at("xg_tok", [128, 4, C], BF16, offset=130784)
    xgfm = nc.alloc_sbuf_tensor_at("xgfm", [128, 16, 512], BF16, offset=147168)
    msta = [nc.alloc_sbuf_tensor_at("msta0", [128, 4, C], F32, offset=163552),
            nc.alloc_sbuf_tensor_at("msta1", [128, 4, C], F32, offset=196576)]
    cwg = [ctxF2.enter_context(nc.sbuf_tensor(f"cwg{i}", [128, 4, 64], F32)) for i in range(2)]
    g_sp = [ctxF2.enter_context(nc.sbuf_tensor(f"g_sp{i}", [128, 8, 512], BF16)) for i in range(2)]
    assert nc.sbuf_base <= 130784, ("sparse left overflow", nc.sbuf_base)
    psT3 = [ctxF2.enter_context(nc.psum_tensor(f"psT3{i}", [128, 4, 128], BF16)) for i in range(2)]
    psG1 = [ctxF2.enter_context(nc.psum_tensor(f"psG1{i}", [128, 512], F32)) for i in range(2)]
    psG3 = [ctxF2.enter_context(nc.psum_tensor(f"psG3{i}", [128, 512], F32)) for i in range(2)]
    psW2 = [ctxF2.enter_context(nc.psum_tensor(f"psW2{i}", [128, 512], F32)) for i in range(2)]

    wait("gpsimd", "mm", v_shared_mm)
    wait("gpsimd", "ve", v_shared_ve)
    wait("tensor", "ve", v_shared_ve)
    wait("tensor", "sc", cur("sc"))

    tp_sp = {}
    ev_sp = {}
    gm_sp = {}
    si_sp = {}
    fe_sp = {}
    gthc = {}
    sc_done = {}
    w2_sp = {}
    gth = {}
    def emit_gathers(e_):
        wait("gpsimd", "sbx", v_bread)
        wait("gpsimd", "mm", tp_sp.get(e_ - 1, 0))
        for sub in range(4):
            def gx1(e, e_=e_, sub=sub):
                e.indirect_dma_start(
                    out=xg_tok[:, sub, :], out_offset=None,
                    in_=d_ln2, in_offset=IOA(bidx_sb[:, e_ * 4 + sub:e_ * 4 + sub + 1], 0)
                ).then_inc(sems["sgt"], 16)
            emit("gpsimd", gx1)
            inc("sgt", 16)
        wait("gpsimd", "ve", fe_sp.get((e_ - 2, 15), 0))
        wait("gpsimd", "sac", v_acc_stored)
        wait("gpsimd", "sgc", cur("sgc"))
        for sub in range(4):
            def gx2(e, e_=e_, sub=sub):
                e.indirect_dma_start(
                    out=cwg[e_ % 2][:, sub, :], out_offset=None,
                    in_=d_cwsc, in_offset=IOA(bidx_sb[:, e_ * 4 + sub:e_ * 4 + sub + 1], 0)
                ).then_inc(sems["sgc"], 16)
            emit("gpsimd", gx2)
            inc("sgc", 16)
        gth[e_] = cur("sgt")
        gthc[e_] = cur("sgc")

    gq2 = 0
    for e_ in range(16):
        if e_ == 0:
            for k in range(4):
                loadix[(10, e_, k)] = emit_load((10, e_, k), exp_srcs(0)[k])
            emit_gathers(0)
        wait("tensor", "sgt", gth[e_])
        wait("tensor", "mm", w2_sp.get(e_ - 1, 0))
        for sub in range(4):
            for dq in range(4):
                grp = sub * 4 + dq
                par = grp % 2
                wait("tensor", "ve", ev_sp.get((e_, grp - 2), ev_sp.get((e_ - 1, grp + 14), 0)))
                for j in range(4):
                    dc = dq * 4 + j
                    def tpx(e, par=par, sub=sub, dc=dc, j=j, last=(j == 3)):
                        ins = e.transpose(psT3[par][:, j, :],
                                          xg_tok[:, sub, dc * 128:(dc + 1) * 128], identb[:])
                        if last:
                            ins.then_inc(sems["mm"], 1)
                    emit("tensor", tpx)
                v_tp = inc("mm")
                wait("vector", "mm", v_tp)
                v_ev = inc("ve")
                ev_sp[(e_, grp)] = v_ev
                emit("vector", lambda e, par=par, sub=sub, dq=dq: e.tensor_copy(
                    xgfm[:, dq * 4:(dq + 1) * 4, sub * 128:(sub + 1) * 128],
                    psT3[par][:]).then_inc(sems["ve"], 1))
        tp_sp[e_] = cur("mm")
        v_xgfm = cur("ve")
        for hh in range(2):
            v1, s1 = ld_done[(10, e_, 2 * hh)]
            v3, s3 = ld_done[(10, e_, 2 * hh + 1)]
            wait("tensor", "sw%d" % s1, v1)
            wait("tensor", "sw%d" % s3, v3)
            wait("tensor", "ve", v_xgfm)
            for hc4 in range(4):
                hc = hh * 4 + hc4
                b1 = gq2 % 2
                b3 = gq2 % 2
                wait("tensor", "sc", si_sp.get(gq2 - 2, 0))
                for dc in range(16):
                    def sg1(e, s1=s1, hc4=hc4, dc=dc, b1=b1):
                        ins = e.matmul(psG1[b1][:],
                                       ring[s1][:, dc, hc4 * 128:(hc4 + 1) * 128],
                                       xgfm[:, dc, :],
                                       start=(dc == 0), stop=(dc == 15))
                        if dc == 15:
                            ins.then_inc(sems["mm"], 1)
                    emit("tensor", sg1)
                v_m1 = inc("mm")
                wait("tensor", "ve", gm_sp.get((e_, hc - 2), gm_sp.get((e_ - 1, hc + 6), 0)))
                for dc in range(16):
                    def sg3(e, s3=s3, hc4=hc4, dc=dc, b3=b3):
                        ins = e.matmul(psG3[b3][:],
                                       ring[s3][:, dc, hc4 * 128:(hc4 + 1) * 128],
                                       xgfm[:, dc, :],
                                       start=(dc == 0), stop=(dc == 15))
                        if dc == 15:
                            ins.then_inc(sems["mm"], 1)
                    emit("tensor", sg3)
                v_m3 = inc("mm")
                wait("scalar", "mm", v_m1)
                wait("scalar", "ve", gm_sp.get((e_, hc - 2), gm_sp.get((e_ - 1, hc + 6), 0)))
                v_sl = inc("sc")
                si_sp[gq2] = v_sl
                emit("scalar", lambda e, b1=b1, sb=gq2 % 2: e.activation(
                    silu[sb][:], psG1[b1][:], AF.Silu).then_inc(sems["sc"], 1))
                wait("vector", "mm", v_m3)
                wait("vector", "sc", v_sl)
                wait("vector", "mm", w2_sp.get(e_ - 1, 0))
                v_gm = inc("ve")
                gm_sp[(e_, hc)] = v_gm
                emit("vector", lambda e, b3=b3, e_=e_, hc=hc, sb=gq2 % 2: e.tensor_mul(
                    g_sp[e_ % 2][:, hc, :], silu[sb][:], psG3[b3][:]).then_inc(sems["ve"], 1))
                gq2 += 1
            if hh == 0 and e_ == 0:
                slot_free[loadix[(10, 0, 0)]] = cur("mm")
                slot_free[loadix[(10, 0, 1)]] = cur("mm")
                loadix[(10, 0, 4)] = emit_load((10, 0, 4), exp_srcs(0)[4])
                loadix[(10, 0, 5)] = emit_load((10, 0, 5), exp_srcs(0)[5])
            elif hh == 0:
                slot_free[loadix[(10, e_, 0)]] = cur("mm")
                slot_free[loadix[(10, e_, 1)]] = cur("mm")
                loadix[(10, e_, 4)] = emit_load((10, e_, 4), exp_srcs(e_)[4])
                loadix[(10, e_, 5)] = emit_load((10, e_, 5), exp_srcs(e_)[5])
        slot_free[loadix[(10, e_, 2)]] = cur("mm")
        slot_free[loadix[(10, e_, 3)]] = cur("mm")
        if e_ + 1 < 16:
            loadix[(10, e_ + 1, 0)] = emit_load((10, e_ + 1, 0), exp_srcs(e_ + 1)[0])
            loadix[(10, e_ + 1, 1)] = emit_load((10, e_ + 1, 1), exp_srcs(e_ + 1)[1])
        v2a, s2a = ld_done[(10, e_, 4)]
        v2b, s2b = ld_done[(10, e_, 5)]
        wait("tensor", "sw%d" % s2a, v2a)
        wait("tensor", "sw%d" % s2b, v2b)
        wait("tensor", "ve", gm_sp[(e_, 7)])
        wt = 0
        for sub in range(4):
            for dc4 in range(4):
                bw = wt % 2
                wait("tensor", "ve", fe_sp.get((e_, wt - 2), fe_sp.get((e_ - 1, wt + 14), 0)))
                for hc in range(8):
                    s2 = s2a if hc < 4 else s2b
                    def sw2(e, s2=s2, hc=hc, sub=sub, dc4=dc4, bw=bw, e_=e_,
                            last=(hc == 7)):
                        rhs = bass.AP(ring[s2], (hc % 4) * 2048 + dc4 * 512,
                                      [[8192, 128], [1, 512]])
                        ins = e.matmul(psW2[bw][:],
                                       g_sp[e_ % 2][:, hc, sub * 128:(sub + 1) * 128],
                                       rhs,
                                       start=(hc == 0), stop=(hc == 7))
                        if last:
                            ins.then_inc(sems["mm"], 1)
                    emit("tensor", sw2)
                v_w2 = inc("mm")
                wait("vector", "mm", v_w2)
                wait("vector", "sgc", gthc[e_])
                wait("vector", "ssc", sc_done.get(e_ - 2, 0))
                v_fe = inc("ve")
                fe_sp[(e_, wt)] = v_fe
                emit("vector", lambda e, e_=e_, sub=sub, dc4=dc4, bw=bw: e.tensor_scalar(
                    msta[e_ % 2][:, sub, dc4 * 512:(dc4 + 1) * 512], psW2[bw][:],
                    cwg[e_ % 2][:, sub, e_:e_ + 1], None,
                    op0=ALU.mult).then_inc(sems["ve"], 1))
                wt += 1
        w2_sp[e_] = cur("mm")
        slot_free[loadix[(10, e_, 4)]] = cur("mm")
        slot_free[loadix[(10, e_, 5)]] = cur("mm")
        if e_ + 1 < 16:
            emit_gathers(e_ + 1)
            loadix[(10, e_ + 1, 2)] = emit_load((10, e_ + 1, 2), exp_srcs(e_ + 1)[2])
            loadix[(10, e_ + 1, 3)] = emit_load((10, e_ + 1, 3), exp_srcs(e_ + 1)[3])
        wait("gpsimd", "ve", fe_sp[(e_, 15)])
        wait("gpsimd", "ssc", sc_done.get(e_ - 1, 0))
        wait("gpsimd", "sac", v_acc_stored)
        for sub in range(4):
            def ssc1(e, e_=e_, sub=sub):
                e.indirect_dma_start(
                    out=d_accd, out_offset=IOA(bidx_sb[:, e_ * 4 + sub:e_ * 4 + sub + 1], 0),
                    in_=msta[e_ % 2][:, sub, :], in_offset=None,
                    compute_op=ALU.add).then_inc(sems["ssc"], 16)
            emit("gpsimd", ssc1)
            inc("ssc", 16)
        sc_done[e_] = cur("ssc")
    v_ssc_all = cur("ssc")

    ctxF3 = ExitStack()
    x2r = [ctxF3.enter_context(nc.sbuf_tensor(f"x2r{i}", [128, C], F32)) for i in range(2)]
    accr = [ctxF3.enter_context(nc.sbuf_tensor(f"accr{i}", [128, C], F32)) for i in range(2)]
    out_done = {}
    for tt in range(NCH):
        wait("sync", "scr", v_scr_all)
        wait("sync", "ssc", v_ssc_all)
        olane = "sout%d" % (tt % 2)
        rlane = "sxr%d" % (tt % 2)
        wait("sync", olane, out_done.get(tt - 2, 0))
        v_xr = inc(rlane, 16)
        emit("sync", lambda e, tt=tt, rlane=rlane: e.dma_start(
            out=x2r[tt % 2][:], in_=d_x2[tt * 128:(tt + 1) * 128, :]).then_inc(sems[rlane], 16))
        wait("sync", rlane, v_xr)
        v_ar = inc(rlane, 16)
        emit("sync", lambda e, tt=tt, rlane=rlane: e.dma_start(
            out=accr[tt % 2][:], in_=d_accd[tt * 128:(tt + 1) * 128, :]).then_inc(sems[rlane], 16))
        wait("vector", rlane, v_ar)
        wait("vector", olane, out_done.get(tt - 2, 0))
        v_fin = inc("ve")
        emit("vector", lambda e, tt=tt: e.tensor_add(
            x2r[tt % 2][:], x2r[tt % 2][:], accr[tt % 2][:]).then_inc(sems["ve"], 1))
        wait("sync", "ve", v_fin)
        out_done[tt] = inc(olane, 16)
        emit("sync", lambda e, tt=tt, olane=olane: e.dma_start(
            out=d_out[tt * 128:(tt + 1) * 128, :], in_=x2r[tt % 2][:]).then_inc(sems[olane], 16))
    wait("sync", "sout0", out_done[6])
    wait("sync", "sout1", out_done[7])
    ctxF3.close()
    ctxF2.close()
    ctxF.close()

    with nc.Block() as block:
        @block.sync
        def _(e):
            for fn in prog["sync"]:
                fn(e)

        @block.gpsimd
        def _(e):
            for fn in prog["gpsimd"]:
                fn(e)

        @block.tensor
        def _(e):
            for fn in prog["tensor"]:
                fn(e)

        @block.vector
        def _(e):
            for fn in prog["vector"]:
                fn(e)

        @block.scalar
        def _(e):
            for fn in prog["scalar"]:
                fn(e)

    master.close()
    return nc


def _rope_tables_f64():
    j = np.arange(64, dtype=np.float64)
    inv = 10000.0 ** (-2.0 * j / 128.0)
    t = np.arange(T, dtype=np.float64)
    ang = t[None, :] * inv[:, None]
    cos = np.cos(ang)
    sin = np.sin(ang)
    Ct = np.concatenate([cos, cos], axis=0)
    St = np.concatenate([-sin, sin], axis=0)
    return Ct, St


def _hi_lo(a64):
    hi = a64.astype(np.float32)
    lo = (a64 - hi.astype(np.float64)).astype(np.float32)
    return np.stack([hi, lo], axis=0)


def kernel(L0_attn_x, L0_attn_Wq, L0_attn_Wk, L0_attn_Wv, L0_attn_Wproj,
           L0_ffn_w_router, L0_ffn_we1, L0_ffn_we3, L0_ffn_we2,
           L0_ffn_ws1, L0_ffn_ws3, L0_ffn_ws2):
    x = np.ascontiguousarray(np.asarray(L0_attn_x, dtype=np.float32))
    weights = {
        "wq": np.ascontiguousarray(np.asarray(L0_attn_Wq, dtype=np.float32)),
        "wk": np.ascontiguousarray(np.asarray(L0_attn_Wk, dtype=np.float32)),
        "wv": np.ascontiguousarray(np.asarray(L0_attn_Wv, dtype=np.float32)),
        "wp": np.ascontiguousarray(np.asarray(L0_attn_Wproj, dtype=np.float32)),
        "wr": np.ascontiguousarray(np.asarray(L0_ffn_w_router, dtype=np.float32)),
        "we1": np.ascontiguousarray(np.asarray(L0_ffn_we1, dtype=np.float32)),
        "we3": np.ascontiguousarray(np.asarray(L0_ffn_we3, dtype=np.float32)),
        "we2": np.ascontiguousarray(np.asarray(L0_ffn_we2, dtype=np.float32)),
        "ws1": np.ascontiguousarray(np.asarray(L0_ffn_ws1, dtype=np.float32)),
        "ws3": np.ascontiguousarray(np.asarray(L0_ffn_ws3, dtype=np.float32)),
        "ws2": np.ascontiguousarray(np.asarray(L0_ffn_ws2, dtype=np.float32)),
    }
    ident = np.eye(128, dtype=ml_dtypes.bfloat16)
    Ct64, St64 = _rope_tables_f64()
    tidc = (np.arange(8)[None, :] * 128 + np.arange(128)[:, None]).astype(np.uint32)
    erowc = np.tile(np.arange(16, dtype=np.float32)[None, :], (128, 1))
    ltuc = np.triu(np.ones((128, 128), np.float32), 1).astype(ml_dtypes.bfloat16)

    in_maps = []
    for core in range(8):
        b, r = core // 4, core % 4
        perm = core_perm(r)
        rowp = np.concatenate([np.arange(p * 128, (p + 1) * 128) for p in perm])
        maskA = np.zeros((128, NCH, 512), np.float32)
        maskB = np.zeros((128, NCH, 4, 128), np.float32)
        for oc in range(NCH):
            blk = own_block(r, oc)
            qpos = blk * 128 + np.arange(128)
            kpos = rowp[oc * 512:(oc + 1) * 512]
            bad = kpos[None, :] > qpos[:, None]
            maskA[:, oc, :] = np.where(bad, NEG, 0.0)
            maskB[:, oc, :, :] = np.where(bad, NEG, 0.0).T.reshape(4, 128, 128).transpose(1, 0, 2)
        m = {
            "xb": np.ascontiguousarray(x[b][rowp]),
            "idb": ident,
            "tidc": tidc,
            "erowc": erowc,
            "ltuc": ltuc,
            "ropeC": np.ascontiguousarray(_hi_lo(Ct64[:, rowp])),
            "ropeS": np.ascontiguousarray(_hi_lo(St64[:, rowp])),
            "maskA": maskA.astype(ml_dtypes.bfloat16),
            "maskB": maskB.astype(ml_dtypes.bfloat16),
        }
        m.update(weights)
        in_maps.append(m)

    nc = build_program()
    res = run_bass_kernel_spmd(nc, in_maps, core_ids=list(range(8)))

    out = np.empty((B, T, C), dtype=np.float32)
    for core in range(8):
        b, r = core // 4, core % 4
        oo = np.asarray(res.results[core]["out_own"])
        for oc in range(NCH):
            blk = own_block(r, oc)
            out[b, blk * 128:(blk + 1) * 128, :] = oo[oc * 128:(oc + 1) * 128, :]
    return out
